# version 02
# baseline (speedup 1.0000x reference)
import dataclasses
from contextlib import ExitStack

import numpy as np

import concourse.bass as bass
import concourse.mybir as mybir
from concourse.bass_utils import run_bass_kernel_spmd

F32 = mybir.dt.float32
BF16 = mybir.dt.bfloat16
I32 = mybir.dt.int32
AF = mybir.ActivationFunctionType
ALU = mybir.AluOpType
AX = mybir.AxisListType

NCORE = 8
DM = 2048
NCH = 16
TPC = 1024
EPS_LN = 1e-5
EPS_RMS = float(np.finfo(np.float32).eps)
NEG = -1.0e30
SCALE0 = 0.125
SCALE1 = float(1.0 / np.sqrt(128.0))

ENGS = ("tensor", "vector", "scalar", "gpsimd", "sync")
DEBUG0 = False
TRUNC0 = False
TRUNC1 = False
TRUNC2 = False
XNROVR = False
NFLUSH = 0


def bc0(ap, axis, n):
    newap = [list(x) for x in ap.ap]
    assert newap[axis][1] == 1, (newap, axis)
    newap[axis] = [0, n]
    return dataclasses.replace(ap, ap=newap)


class Plan:
    def __init__(self):
        self.ops = {e: [] for e in ENGS}
        self.cnt = {}
        self.guards = {}

    def val(self, s):
        return self.cnt.get(s, 0)

    muted = False

    def emit(self, eng, meth, *args, wait=(), inc=(), **kw):
        if self.muted:
            return
        for s, v in wait:
            assert v <= self.val(s), (eng, meth, s, v, self.val(s))
        for s, n in inc:
            self.cnt[s] = self.val(s) + n
        self.ops[eng].append((meth, args, kw, tuple(wait), tuple(inc)))

    def W(self, name):
        return tuple(self.guards.pop(name, ()))

    def R(self, name, sem):
        g = self.guards.setdefault(name, [])
        g.append((sem, self.val(sem)))

    def after(self, *sems):
        return tuple((s, self.val(s)) for s in sems)


def playback(nc, ctx, plan):
    sems = {name: ctx.enter_context(nc.semaphore(name)) for name in plan.cnt}
    block = ctx.enter_context(nc.Block())
    for eng_name in ("sync", "gpsimd", "scalar", "vector", "tensor"):
        ops = plan.ops[eng_name]
        if not ops:
            continue
        dec = getattr(block, eng_name)

        def make(ops_):
            def body(eng):
                for meth, args, kw, waits, incs in ops_:
                    for s, v in waits:
                        if v > 0:
                            eng.wait_ge(sems[s], v)
                    if callable(meth):
                        meth(eng, sems)
                        continue
                    ins = getattr(eng, meth)(*args, **kw)
                    for s, n in incs:
                        ins.then_inc(sems[s], n)
            return body
        dec(make(ops))


def seed(P, names):
    f = P.after("PE", "VE", "AC", "LD", "LC", "TR", "ST")
    for n in names:
        P.guards.setdefault(n, []).extend(f)


def tiles(total, size):
    out = []
    o = 0
    while o < total:
        out.append((o, min(size, total - o)))
        o += size
    return out



class Env:
    pass


def alloc(env, ctx, name, shape, dt):
    env.seq = getattr(env, "seq", 0) + 1
    return ctx.enter_context(env.nc.sbuf_tensor(f"{name}_{env.seq}", shape, dt))


def psum(env, ctx, name, dt=F32):
    env.seq = getattr(env, "seq", 0) + 1
    shape = [128, 512] if dt == F32 else [128, 1024]
    return ctx.enter_context(env.nc.psum_tensor(f"{name}_{env.seq}", shape, dt))


def emit_ln_setup(env, sub):
    env.s_xb = alloc(env, sub, "ln_xb", [128, 2, 512], BF16)
    env.s_xq = alloc(env, sub, "ln_xq", [128, 2, 512], BF16)
    env.st_m = alloc(env, sub, "ln_m", [1, 512], F32)
    env.st_q = alloc(env, sub, "ln_q", [1, 512], F32)
    env.st_t = alloc(env, sub, "ln_t", [1, 512], F32)
    env.st_t2 = alloc(env, sub, "ln_t2", [1, 512], F32)
    env.st_A = alloc(env, sub, "ln_A", [1, 512], F32)
    env.st_B = alloc(env, sub, "ln_B", [1, 512], F32)
    env.s_t1 = alloc(env, sub, "ln_t1", [128, 512], F32)
    env.p_s = psum(env, sub, "ln_ps")
    env.p_q = psum(env, sub, "ln_pq")
    env.p_A = psum(env, sub, "ln_pA")
    env.p_B = psum(env, sub, "ln_pB")


def emit_ln(env, tag, x3, xn3, T, dep):
    P = env.P
    for t0, tw in tiles(T, 512):
        sl = slice(t0, t0 + tw)
        for c in range(NCH):
            s = c & 1
            P.emit("vector", "tensor_copy", env.s_xb[:, s, 0:tw], x3[:, c, sl],
                   wait=P.W(f"{tag}xb{s}") + (dep if (t0 == 0 and c == 0) else ()),
                   inc=[("VE", 1)])
            P.emit("scalar", "activation", env.s_xq[:, s, 0:tw], env.s_xb[:, s, 0:tw],
                   AF.Square, wait=P.after("VE") + P.W(f"{tag}xq{s}"), inc=[("AC", 1)])
            w = P.after("AC")
            if c == 0:
                w = w + P.W(f"{tag}ps") + P.W(f"{tag}pq")
            P.emit("tensor", "matmul", env.p_s[0:1, 0:tw], env.ones_b[:, :],
                   env.s_xb[:, s, 0:tw], start=(c == 0), stop=(c == 15),
                   skip_group_check=True, wait=w)
            P.emit("tensor", "matmul", env.p_q[0:1, 0:tw], env.ones_b[:, :],
                   env.s_xq[:, s, 0:tw], start=(c == 0), stop=(c == 15),
                   skip_group_check=True, inc=[("PE", 1)])
            P.guards.setdefault(f"{tag}xb{s}", []).append(("PE", P.val("PE")))
            P.guards.setdefault(f"{tag}xq{s}", []).append(("PE", P.val("PE")))
        P.emit("vector", "tensor_scalar", env.st_m[:, 0:tw], env.p_s[0:1, 0:tw],
               1.0 / DM, None, op0=ALU.mult, wait=P.after("PE"))
        P.emit("vector", "tensor_scalar", env.st_q[:, 0:tw], env.p_q[0:1, 0:tw],
               1.0 / DM, EPS_LN, op0=ALU.mult, op1=ALU.add)
        P.emit("vector", "tensor_tensor", env.st_t[:, 0:tw], env.st_m[:, 0:tw],
               env.st_m[:, 0:tw], op=ALU.mult)
        P.emit("vector", "tensor_tensor", env.st_t2[:, 0:tw], env.st_q[:, 0:tw],
               env.st_t[:, 0:tw], op=ALU.subtract, inc=[("VE", 1)])
        P.R(f"{tag}ps", "VE")
        P.R(f"{tag}pq", "VE")
        P.emit("scalar", "activation", env.st_t[:, 0:tw], env.st_t2[:, 0:tw],
               AF.Ln, wait=P.after("VE"))
        P.emit("scalar", "activation", env.st_A[:, 0:tw], env.st_t[:, 0:tw],
               AF.Exp, scale=-0.5, inc=[("AC", 1)],
               wait=P.W(f"{tag}stA"))
        P.emit("vector", "scalar_tensor_tensor", env.st_B[:, 0:tw], env.st_m[:, 0:tw],
               -1.0, env.st_A[:, 0:tw], op0=ALU.mult, op1=ALU.mult,
               wait=P.after("AC") + P.W(f"{tag}stB"), inc=[("VE", 1)])
        P.emit("tensor", "matmul", env.p_A[:, 0:tw], env.ones_f[:, :],
               env.st_A[:, 0:tw], start=True, stop=True,
               wait=P.after("VE") + P.W(f"{tag}pA"))
        P.emit("tensor", "matmul", env.p_B[:, 0:tw], env.ones_f[:, :],
               env.st_B[:, 0:tw], start=True, stop=True,
               wait=P.W(f"{tag}pB"), inc=[("PE", 1)])
        P.R(f"{tag}stA", "PE")
        P.R(f"{tag}stB", "PE")
        for c in range(NCH):
            P.emit("vector", "tensor_tensor", env.s_t1[:, 0:tw], x3[:, c, sl],
                   env.p_A[:, 0:tw], op=ALU.mult,
                   wait=P.after("PE") if c == 0 else ())
            P.emit("vector", "tensor_tensor", xn3[:, c, sl], env.s_t1[:, 0:tw],
                   env.p_B[:, 0:tw], op=ALU.add,
                   inc=[("VE", 1)] if c == 15 else ())
        P.R(f"{tag}pA", "VE")
        P.R(f"{tag}pB", "VE")



def plan_block0(nc, ctx):
    xT = nc.dram_tensor("xT", [DM, TPC], F32, kind="ExternalInput").ap()
    wqkv = nc.dram_tensor("w_qkv", [DM, 3 * DM], F32, kind="ExternalInput").ap()
    wout = nc.dram_tensor("w_out", [DM, DM], F32, kind="ExternalInput").ap()
    wnpp = nc.dram_tensor("w_norm_pp", [128, 16], F32, kind="ExternalInput").ap()
    pmk = nc.dram_tensor("pmkT", [128, 256], F32, kind="ExternalInput").ap()
    pmv = nc.dram_tensor("pmvT", [16, DM], F32, kind="ExternalInput").ap()
    wg = nc.dram_tensor("w_gate", [8192, DM], F32, kind="ExternalInput").ap()
    wu = nc.dram_tensor("w_up", [8192, DM], F32, kind="ExternalInput").ap()
    wd = nc.dram_tensor("w_down", [DM, 8192], F32, kind="ExternalInput").ap()
    mask = nc.dram_tensor("mask0", [128, 144], F32, kind="ExternalInput").ap()
    iden = nc.dram_tensor("ident", [128, 128], F32, kind="ExternalInput").ap()
    x1T = nc.dram_tensor("x1T", [DM, TPC], F32, kind="ExternalOutput").ap()

    env = Env()
    env.nc = nc
    env.ctx = ctx
    env.P = P = Plan()

    env.s_xres = alloc(env, env.ctx, "xres", [128, NCH, TPC], F32)
    env.s_xnT = alloc(env, env.ctx, "xnT", [128, NCH, TPC], BF16)
    env.ones_b = alloc(env, env.ctx, "ones_b", [128, 1], BF16)
    env.ones_f = alloc(env, env.ctx, "ones_f", [1, 128], F32)
    s_wn = alloc(env, env.ctx, "wn", [128, 16], F32)
    s_mask = alloc(env, env.ctx, "mask", [128, 144], F32)
    s_pmk = alloc(env, env.ctx, "pmk", [128, 256], BF16)
    s_pmv = alloc(env, env.ctx, "pmv", [16, DM], BF16)
    s_id = alloc(env, env.ctx, "id", [128, 128], BF16)

    xr = xT.rearrange("(c p) t -> p c t", p=128)
    P.emit("scalar", "dma_start", out=env.s_xres[:, :, :], in_=xr, inc=[("LD", 16)])
    P.emit("scalar", "dma_start", out=s_wn[:, :], in_=wnpp, inc=[("LD", 16)])
    P.emit("scalar", "dma_start", out=s_mask[:, :], in_=mask, inc=[("LD", 16)])
    P.emit("gpsimd", "dma_start", out=s_pmk[:, :], in_=pmk, inc=[("LC", 16)])
    P.emit("gpsimd", "dma_start", out=s_pmv[:, :], in_=pmv, inc=[("LC", 16)])
    P.emit("gpsimd", "dma_start", out=s_id[:, :], in_=iden, inc=[("LC", 16)])
    P.emit("vector", "memset", env.ones_b[:, :], 1.0, inc=[("VE", 1)])
    P.emit("vector", "memset", env.ones_f[:, :], 1.0, inc=[("VE", 1)])

    emit_block0(env, wqkv, wout, wg, wu, wd, s_wn, s_mask, s_pmk, s_pmv, s_id)

    x1r = x1T.rearrange("(c p) t -> p c t", p=128)
    P.emit("sync", "dma_start", out=x1r, in_=env.s_xres[:, :, :],
           wait=P.after("VE", "AC", "PE"), inc=[("ST", 16)])
    P.emit("sync", "nop", wait=P.after("ST"))
    return env


def emit_block0(env, wqkv, wout, wg, wu, wd, s_wn, s_mask, s_pmk, s_pmv, s_id):
    P = env.P
    with ExitStack() as sub:
        emit_ln_setup(env, sub)
        emit_ln(env, "l0", env.s_xres, env.s_xnT, TPC,
                dep=(("LD", 16),) + P.after("VE"))

    with ExitStack() as attn:
        s_xnr = alloc(env, attn, "xnr", [128, NCH, 512], BF16)
        s_qT = alloc(env, attn, "qT0", [128, NCH, 512], BF16)
        s_kT = alloc(env, attn, "kT0", [128, NCH, 512], BF16)
        s_vtok = alloc(env, attn, "vtok0", [128, NCH, 4, 128], BF16)
        seed(P, [f"vtok{c}" for c in range(NCH)])
        p_A = psum(env, attn, "a_pA")
        p_B = psum(env, attn, "a_pB")
        p_C = psum(env, attn, "a_pC")
        p_D = psum(env, attn, "a_pD")
        p_E0 = psum(env, attn, "a_pE0", BF16)
        p_E1 = psum(env, attn, "a_pE1", BF16)
        p_F = psum(env, attn, "a_pF")
        st_r = alloc(env, attn, "st_r", [1, 512], F32)

        for half in range(2):
            hsl = slice(512 * half, 512 * half + 512)
            with ExitStack() as sub:
                s_sq = alloc(env, sub, "rms_sq", [128, 2, 512], BF16)
                seed(P, ["rsq0", "rsq1"])
                for c in range(NCH):
                    s = c & 1
                    P.emit("scalar", "activation", s_sq[:, s, :], env.s_xnT[:, c, hsl],
                           AF.Square, wait=(P.after("VE") if c == 0 else ()) + P.W(f"rsq{s}"),
                           inc=[("AC", 1)])
                    P.emit("tensor", "matmul", p_F[0:1, 0:512], env.ones_b[:, :],
                           s_sq[:, s, :], start=(c == 0), stop=(c == 15),
                           wait=P.after("AC") + (P.W("a_pF") if c == 0 else ()),
                           inc=[("PE", 1)] if c == 15 else ())
                    P.guards.setdefault(f"rsq{s}", []).append(("PE", P.val("PE")))
                P.emit("vector", "tensor_scalar", st_r[:, 0:512], p_F[0:1, 0:512],
                       1.0 / DM, EPS_RMS, op0=ALU.mult, op1=ALU.add,
                       wait=P.after("PE") + P.W("st_r"), inc=[("VE", 1)])
                P.R("a_pF", "VE")
                P.emit("scalar", "activation", st_r[:, 0:512], st_r[:, 0:512],
                       AF.Ln, wait=P.after("VE"))
                P.emit("scalar", "activation", st_r[:, 0:512], st_r[:, 0:512],
                       AF.Exp, scale=-0.5, inc=[("AC", 1)])
                P.emit("tensor", "matmul", p_B[:, 0:512], env.ones_f[:, :],
                       st_r[:, 0:512], start=True, stop=True,
                       wait=P.after("AC") + P.W("a_pB"), inc=[("PE", 1)])
                P.R("st_r", "PE")
                for c in range(NCH):
                    P.emit("vector", "scalar_tensor_tensor", s_xnr[:, c, :],
                           env.s_xnT[:, c, hsl], s_wn[:, c:c + 1], p_B[:, 0:512],
                           op0=ALU.mult, op1=ALU.mult,
                           wait=(P.after("PE") + (("LD", 32),) + P.W("xnr")) if c == 0 else (),
                           inc=[("VE", 1)] if c == 15 else ())
                P.R("a_pB", "VE")

            with ExitStack() as sub:
                s_wq = alloc(env, sub, "wqbuf", [128, 2, NCH, 128], BF16)
                s_vT = alloc(env, sub, "vT0", [128, NCH, 512], BF16)
                seed(P, ["wq0", "wq1"])
                wqr = wqkv.rearrange("(c p) o -> p c o", p=128)
                pq = [p_A, p_C]
                for ob in range(48):
                    s = ob & 1
                    P.emit("gpsimd", "dma_start", out=s_wq[:, s, :, :],
                           in_=wqr[:, :, 128 * ob:128 * (ob + 1)],
                           wait=P.W(f"wq{s}"), inc=[("LC", 16)])
                    lc = P.val("LC")
                    for c in range(NCH):
                        w = ()
                        if c == 0:
                            w = (("LC", lc),) + P.W(f"a_p{'AC'[s]}") + \
                                (P.after("VE", "AC", "PE", "LD", "TR") if ob == 0 else ())
                        P.emit("tensor", "matmul", pq[s][:, 0:512],
                               s_wq[:, s, c, :], s_xnr[:, c, :],
                               start=(c == 0), stop=(c == 15), wait=w,
                               inc=[("PE", 1)] if c == 15 else ())
                    P.guards.setdefault(f"wq{s}", []).append(("PE", P.val("PE")))
                    if ob < 16:
                        P.emit("vector", "tensor_scalar", s_qT[:, ob, :],
                               pq[s][:, 0:512], SCALE0, None, op0=ALU.mult,
                               wait=P.after("PE"), inc=[("VE", 1)])
                    elif ob < 32:
                        P.emit("vector", "tensor_copy", s_kT[:, ob - 16, :],
                               pq[s][:, 0:512], wait=P.after("PE"), inc=[("VE", 1)])
                    else:
                        P.emit("vector", "tensor_copy", s_vT[:, ob - 32, :],
                               pq[s][:, 0:512], wait=P.after("PE"), inc=[("VE", 1)])
                    P.R(f"a_p{'AC'[s]}", "VE")
                P.R("xnr", "PE")
                ve = P.val("VE")
                for c in range(NCH):
                    P.emit("sync", "dma_start_transpose", s_vtok[:, c, :, :],
                           s_vT[:, c, :], wait=(("VE", ve),) + P.W(f"vtok{c}"),
                           inc=[("TR", 16)])

            with ExitStack() as sub:
                s_sm = alloc(env, sub, "sm", [128, 2, 144], F32)
                seed(P, ["sm0", "sm1", "au0", "au1", "an0", "an1", "at0", "at1"])
                s_nm = alloc(env, sub, "nm", [128, 2], F32)
                s_au = alloc(env, sub, "au", [128, 2, 144], BF16)
                s_rs = alloc(env, sub, "rs", [128, 2], F32)
                s_rn = alloc(env, sub, "rn", [128, 2], F32)
                s_an = alloc(env, sub, "an", [128, 2, 144], BF16)
                s_at = alloc(env, sub, "at", [128, 2, 256], BF16)
                s_oT = alloc(env, sub, "oT0", [128, NCH, 512], BF16)
                p_sc = [p_A, p_C]
                p_at = [p_E0, p_E1]
                p_av = [p_B, p_D]
                trv = P.val("TR")
                for it, (h, tl) in enumerate(
                        [(h, t) for h in range(32) for t in range(4)]):
                    s = it & 1
                    a = h & 1
                    ch = h >> 1
                    qsl = (slice(64 * a, 64 * a + 64), ch,
                           slice(128 * tl, 128 * tl + 128))
                    w = P.W(f"a_p{'AC'[s]}")
                    if it == 0:
                        w = w + P.after("VE") + (("LC", 48),) + (("LD", 48),)
                    P.emit("tensor", "matmul", p_sc[s][:, 0:128],
                           s_qT[qsl], s_kT[qsl], start=True, stop=False, wait=w)
                    P.emit("tensor", "matmul", p_sc[s][:, 128:144],
                           s_qT[qsl], s_pmk[64 * a:64 * a + 64, 16 * ch:16 * ch + 16],
                           start=False, stop=True, inc=[("PE", 1)])
                    P.emit("vector", "tensor_tensor", s_sm[:, s, :],
                           p_sc[s][:, 0:144], s_mask[:, :], op=ALU.add,
                           wait=P.after("PE") + P.W(f"sm{s}"))
                    P.emit("vector", "tensor_reduce", s_nm[:, s:s + 1],
                           s_sm[:, s, :], axis=AX.X, op=ALU.max, negate=True,
                           inc=[("VE", 1)])
                    P.guards.setdefault(f"a_p{'AC'[s]}", []).append(("VE", P.val("VE")))
                    P.emit("scalar", "activation", s_au[:, s, :], s_sm[:, s, :],
                           AF.Exp, bias=s_nm[:, s:s + 1],
                           accum_out=s_rs[:, s:s + 1],
                           wait=P.after("VE") + P.W(f"au{s}"), inc=[("AC", 1)])
                    P.guards.setdefault(f"sm{s}", []).append(("AC", P.val("AC")))
                    P.emit("vector", "reciprocal", s_rn[:, s:s + 1], s_rs[:, s:s + 1],
                           wait=P.after("AC"))
                    P.emit("vector", "tensor_scalar", s_an[:, s, :], s_au[:, s, :],
                           s_rn[:, s:s + 1], None, op0=ALU.mult,
                           wait=P.W(f"an{s}"), inc=[("VE", 1)])
                    P.guards.setdefault(f"au{s}", []).append(("VE", P.val("VE")))
                    w = P.after("VE") + P.W(f"a_pE{s}")
                    P.emit("tensor", "transpose", p_at[s][:, 0:128],
                           s_an[:, s, 0:128], s_id[:, :], wait=w)
                    P.emit("tensor", "transpose", p_at[s][0:16, 128:256],
                           s_an[:, s, 128:144], s_id[:, :], inc=[("PE", 1)])
                    P.guards.setdefault(f"an{s}", []).append(("PE", P.val("PE")))
                    P.emit("vector", "tensor_copy", s_at[:, s, :], p_at[s][:, 0:256],
                           wait=P.after("PE") + P.W(f"at{s}"), inc=[("VE", 1)])
                    P.R(f"a_pE{s}", "VE")
                    w = P.after("VE") + P.W(f"a_p{'BD'[s]}")
                    if it == 0:
                        w = w + (("TR", trv),) + (("LC", 32),)
                    tp = (0, 64) if a else None
                    osl = slice(64 * a, 64 * a + 64)
                    P.emit("tensor", "matmul", p_av[s][osl, 0:128],
                           s_vtok[:, ch, tl, 64 * a:64 * a + 64], s_at[:, s, 0:128],
                           start=True, stop=False, tile_position=tp, wait=w)
                    P.emit("tensor", "matmul", p_av[s][osl, 0:128],
                           s_pmv[0:16, 64 * h:64 * h + 64], s_at[0:16, s, 128:256],
                           start=False, stop=True, tile_position=tp, inc=[("PE", 1)])
                    P.guards.setdefault(f"at{s}", []).append(("PE", P.val("PE")))
                    P.guards.setdefault(f"vtok{ch}", []).append(("PE", P.val("PE")))
                    P.emit("vector", "tensor_copy",
                           s_oT[osl, ch, 128 * tl:128 * tl + 128],
                           p_av[s][osl, 0:128],
                           wait=P.after("PE"), inc=[("VE", 1)])
                    P.R(f"a_p{'BD'[s]}", "VE")

                s_wo = alloc(env, sub, "wobuf", [128, 2, NCH, 128], BF16)
                seed(P, ["wo0", "wo1"])
                wor = wout.rearrange("(c p) o -> p c o", p=128)
                pw = [p_A, p_C]
                ove = P.val("VE")
                for dt in range(NCH):
                    s = dt & 1
                    P.emit("gpsimd", "dma_start", out=s_wo[:, s, :, :],
                           in_=wor[:, :, 128 * dt:128 * (dt + 1)],
                           wait=P.W(f"wo{s}"), inc=[("LC", 16)])
                    lc = P.val("LC")
                    for c in range(NCH):
                        w = ()
                        if c == 0:
                            w = (("LC", lc), ("VE", ove)) + P.W(f"a_p{'AC'[s]}")
                        P.emit("tensor", "matmul", pw[s][:, 0:512],
                               s_wo[:, s, c, :], s_oT[:, c, :],
                               start=(c == 0), stop=(c == 15), wait=w,
                               inc=[("PE", 1)] if c == 15 else ())
                    P.guards.setdefault(f"wo{s}", []).append(("PE", P.val("PE")))
                    P.emit("vector", "tensor_tensor", env.s_xres[:, dt, hsl],
                           env.s_xres[:, dt, hsl], pw[s][:, 0:512], op=ALU.add,
                           wait=P.after("PE"), inc=[("VE", 1)])
                    P.R(f"a_p{'AC'[s]}", "VE")

    with ExitStack() as ffn:
        s_gatedT = alloc(env, ffn, "gatedT", [128, 32, TPC], BF16)
        p_A = psum(env, ffn, "f_pA")
        p_B = psum(env, ffn, "f_pB")
        p_C = psum(env, ffn, "f_pC")
        p_D = psum(env, ffn, "f_pD")
        for HH in range(2):
            hoff = 4096 * HH
            with ExitStack() as sub:
                s_graw = alloc(env, sub, "graw", [128, 2, DM], BF16)
                s_uraw = alloc(env, sub, "uraw", [128, 2, DM], BF16)
                s_gT = alloc(env, sub, "gT", [128, 2, NCH, 128], BF16)
                s_uT = alloc(env, sub, "uT", [128, 2, NCH, 128], BF16)
                s_sg = alloc(env, sub, "sg", [128, 2, 512], F32)
                seed(P, ["graw0", "graw1", "uraw0", "uraw1", "gT0", "gT1",
                         "uT0", "uT1", "sg0", "sg1"])
                pg = [p_A, p_C]
                pu = [p_B, p_D]
                for hb in range(32):
                    s = hb & 1
                    rows = slice(hoff + 128 * hb, hoff + 128 * (hb + 1))
                    P.emit("gpsimd", "dma_start", out=s_graw[:, s, :],
                           in_=wg[rows, :], wait=P.W(f"graw{s}"), inc=[("LC", 16)])
                    P.emit("gpsimd", "dma_start", out=s_uraw[:, s, :],
                           in_=wu[rows, :], wait=P.W(f"uraw{s}"), inc=[("LC", 16)])
                    lc = P.val("LC")
                    P.emit("sync", "dma_start_transpose", s_gT[:, s, :, :],
                           s_graw[:, s, :], wait=(("LC", lc),) + P.W(f"gT{s}"),
                           inc=[("TR", 16)])
                    P.guards.setdefault(f"graw{s}", []).append(("TR", P.val("TR")))
                    P.emit("sync", "dma_start_transpose", s_uT[:, s, :, :],
                           s_uraw[:, s, :], wait=P.W(f"uT{s}"), inc=[("TR", 16)])
                    P.guards.setdefault(f"uraw{s}", []).append(("TR", P.val("TR")))
                    tr = P.val("TR")
                    for mi, (sw, pp) in enumerate(((s_gT, pg), (s_uT, pu))):
                        for th in range(2):
                            tsl = slice(512 * th, 512 * th + 512)
                            for c in range(NCH):
                                w = ()
                                if c == 0:
                                    w = (("TR", tr),) + P.W(f"f_p{'ACBD'[2 * mi + th]}")
                                    if hb == 0 and mi == 0 and th == 0:
                                        w = w + P.after("VE") + P.W("gatedT")
                                P.emit("tensor", "matmul",
                                       pp[th][:, 0:512], sw[:, s, c, :],
                                       env.s_xnT[:, c, tsl],
                                       start=(c == 0), stop=(c == 15), wait=w,
                                       inc=[("PE", 1)] if c == 15 else ())
                            if mi == 0:
                                P.guards.setdefault(f"gT{s}", []).append(("PE", P.val("PE")))
                            else:
                                P.guards.setdefault(f"uT{s}", []).append(("PE", P.val("PE")))
                    for th in range(2):
                        tsl = slice(512 * th, 512 * th + 512)
                        P.emit("scalar", "activation", s_sg[:, th, :],
                               pg[th][:, 0:512], AF.Silu,
                               wait=P.after("PE") + P.W(f"sg{th}"), inc=[("AC", 1)])
                        P.R(f"f_p{'AC'[th]}", "AC")
                        P.emit("vector", "tensor_tensor",
                               s_gatedT[:, hb, tsl], s_sg[:, th, :],
                               pu[th][:, 0:512], op=ALU.mult,
                               wait=P.after("AC"), inc=[("VE", 1)])
                        P.R(f"f_p{'BD'[th]}", "VE")
                        P.guards.setdefault(f"sg{th}", []).append(("VE", P.val("VE")))
            with ExitStack() as sub:
                s_draw = alloc(env, sub, "draw", [128, 2, 4096], BF16)
                s_dT = alloc(env, sub, "dT", [128, 2, 32, 128], BF16)
                seed(P, ["draw0", "draw1", "dT0", "dT1"])
                pd = [p_A, p_C]
                gve = P.val("VE")
                for dt in range(NCH):
                    s = dt & 1
                    P.emit("gpsimd", "dma_start", out=s_draw[:, s, :],
                           in_=wd[128 * dt:128 * (dt + 1), hoff:hoff + 4096],
                           wait=P.W(f"draw{s}"), inc=[("LC", 16)])
                    lc = P.val("LC")
                    P.emit("sync", "dma_start_transpose", s_dT[:, s, :, :],
                           s_draw[:, s, :], wait=(("LC", lc),) + P.W(f"dT{s}"),
                           inc=[("TR", 16)])
                    P.guards.setdefault(f"draw{s}", []).append(("TR", P.val("TR")))
                    tr = P.val("TR")
                    for th in range(2):
                        tsl = slice(512 * th, 512 * th + 512)
                        for k in range(32):
                            w = ()
                            if k == 0:
                                w = (("TR", tr), ("VE", gve)) + P.W(f"f_p{'AC'[th]}")
                            P.emit("tensor", "matmul", pd[th][:, 0:512],
                                   s_dT[:, s, k, :], s_gatedT[:, k, tsl],
                                   start=(k == 0), stop=(k == 31), wait=w,
                                   inc=[("PE", 1)] if k == 31 else ())
                        P.emit("vector", "tensor_tensor", env.s_xres[:, dt, tsl],
                               env.s_xres[:, dt, tsl], pd[th][:, 0:512], op=ALU.add,
                               wait=P.after("PE"), inc=[("VE", 1)])
                        P.R(f"f_p{'AC'[th]}", "VE")
                    P.guards.setdefault(f"dT{s}", []).append(("PE", P.val("PE")))
                P.R("gatedT", "PE")



def plan_block1(nc, ctx):
    xAT = nc.dram_tensor("xAT", [DM, TPC], F32, kind="ExternalInput").ap()
    xBT = nc.dram_tensor("xBT", [DM, 640], F32, kind="ExternalInput").ap()
    Wq = nc.dram_tensor("Wq", [DM, DM], F32, kind="ExternalInput").ap()
    Wk = nc.dram_tensor("Wk", [DM, 256], F32, kind="ExternalInput").ap()
    Wv = nc.dram_tensor("Wv", [DM, 256], F32, kind="ExternalInput").ap()
    Wo = nc.dram_tensor("Wo", [DM, DM], F32, kind="ExternalInput").ap()
    bqp = nc.dram_tensor("bq_pp", [128, 16], F32, kind="ExternalInput").ap()
    bkp = nc.dram_tensor("bk_pp", [128, 2], F32, kind="ExternalInput").ap()
    bvp = nc.dram_tensor("bv_pp", [128, 2], F32, kind="ExternalInput").ap()
    bop = nc.dram_tensor("bo_pp", [128, 16], F32, kind="ExternalInput").ap()
    lut = nc.dram_tensor("lut_t", [128, 16], F32, kind="ExternalInput").ap()
    gidx = nc.dram_tensor("gate_idx", [5504, DM], I32, kind="ExternalInput").ap()
    uidx = nc.dram_tensor("up_idx", [5504, DM], I32, kind="ExternalInput").ap()
    didx = nc.dram_tensor("down_idx", [DM, 5504], I32, kind="ExternalInput").ap()
    iden = nc.dram_tensor("ident", [128, 128], F32, kind="ExternalInput").ap()
    oT = nc.dram_tensor("outT", [DM, TPC], F32, kind="ExternalOutput").ap()

    env = Env()
    env.nc = nc
    env.ctx = ctx
    env.P = P = Plan()

    env.s_xres = alloc(env, env.ctx, "xres", [128, NCH, TPC], F32)
    env.s_xnT = alloc(env, env.ctx, "xnTA", [128, NCH, TPC], BF16)
    env.ones_b = alloc(env, env.ctx, "ones_b", [128, 1], BF16)
    env.ones_f = alloc(env, env.ctx, "ones_f", [1, 128], F32)
    s_bq = alloc(env, env.ctx, "bq", [128, 16], F32)
    s_bk = alloc(env, env.ctx, "bk", [128, 2], F32)
    s_bv = alloc(env, env.ctx, "bv", [128, 2], F32)
    s_bo = alloc(env, env.ctx, "bo", [128, 16], F32)
    s_lut = alloc(env, env.ctx, "lut", [128, 16], F32)
    s_id = alloc(env, env.ctx, "id", [128, 128], BF16)

    xar = xAT.rearrange("(c p) t -> p c t", p=128)
    P.emit("scalar", "dma_start", out=env.s_xres[:, :, :], in_=xar, inc=[("LD", 16)])
    for dst, src in ((s_bq, bqp), (s_bk, bkp), (s_bv, bvp), (s_bo, bop), (s_lut, lut)):
        P.emit("scalar", "dma_start", out=dst[:, :], in_=src, inc=[("LD", 16)])
    P.emit("gpsimd", "dma_start", out=s_id[:, :], in_=iden, inc=[("LC", 16)])
    P.emit("vector", "memset", env.ones_b[:, :], 1.0, inc=[("VE", 1)])
    P.emit("vector", "memset", env.ones_f[:, :], 1.0, inc=[("VE", 1)])

    s_xnB = alloc(env, env.ctx, "xnTB", [128, NCH, 640], BF16)

    with ExitStack() as sub:
        emit_ln_setup(env, sub)
        emit_ln(env, "lA", env.s_xres, env.s_xnT, TPC,
                dep=(("LD", 16),) + P.after("VE"))
        s_xB = alloc(env, sub, "xBres", [128, NCH, 640], F32)
        xbr = xBT.rearrange("(c p) t -> p c t", p=128)
        P.emit("scalar", "dma_start", out=s_xB[:, :, :], in_=xbr, inc=[("LD", 16)])
        emit_ln(env, "lB", s_xB, s_xnB, 640, dep=P.after("LD", "VE"))

    emit_block1(env, Wq, Wk, Wv, Wo, gidx, uidx, didx, s_bq, s_bk, s_bv, s_bo,
                s_lut, s_id, s_xnB, P.after("VE"))

    orr = oT.rearrange("(c p) t -> p c t", p=128)
    P.emit("sync", "dma_start", out=orr, in_=env.s_xres[:, :, :],
           wait=P.after("VE", "AC", "PE"), inc=[("ST", 16)])
    P.emit("sync", "nop", wait=P.after("ST"))
    return env


def emit_block1(env, Wq, Wk, Wv, Wo, gidx, uidx, didx, s_bq, s_bk, s_bv, s_bo,
                s_lut, s_id, s_xnB, xnb_dep):
    P = env.P
    with ExitStack() as attn:
        s_qT = alloc(env, attn, "qT1", [128, NCH, 640], BF16)
        s_kT = alloc(env, attn, "kT1", [128, 2, 640], BF16)
        s_vtok = alloc(env, attn, "vtok1", [128, 2, 5, 128], BF16)
        s_oT = alloc(env, attn, "oT1", [128, NCH, TPC], BF16)
        seed(P, ["vtok0", "vtok1"])
        p_A = psum(env, attn, "a_pA")
        p_B = psum(env, attn, "a_pB")
        p_C = psum(env, attn, "a_pC")
        p_D = psum(env, attn, "a_pD")
        p_E0 = psum(env, attn, "a_pE0", BF16)
        p_E1 = psum(env, attn, "a_pE1", BF16)

        with ExitStack() as sub:
            s_wb = alloc(env, sub, "wbuf1", [128, 2, NCH, 128], BF16)
            s_vT = alloc(env, sub, "vT1", [128, 2, 640], BF16)
            seed(P, ["w10", "w11"])
            wqr = Wq.rearrange("(c p) o -> p c o", p=128)
            wkr = Wk.rearrange("(c p) o -> p c o", p=128)
            wvr = Wv.rearrange("(c p) o -> p c o", p=128)
            pq = [p_A, p_C]
            NT = ((0, 512), (512, 128))
            for ob in range(20):
                s = ob & 1
                if ob < 16:
                    src = wqr[:, :, 128 * ob:128 * (ob + 1)]
                elif ob < 18:
                    g = ob - 16
                    src = wkr[:, :, 128 * g:128 * (g + 1)]
                else:
                    g = ob - 18
                    src = wvr[:, :, 128 * g:128 * (g + 1)]
                P.emit("gpsimd", "dma_start", out=s_wb[:, s, :, :], in_=src,
                       wait=P.W(f"w1{s}"), inc=[("LC", 16)])
                lc = P.val("LC")
                for nt, (t0, tw) in enumerate(NT):
                    for c in range(NCH):
                        w = ()
                        if c == 0:
                            w = (("LC", lc),) + P.W(f"a_p{'AC'[s]}")
                            if ob == 0 and nt == 0:
                                w = w + P.after("VE") + xnb_dep
                        P.emit("tensor", "matmul", pq[s][:, 0:tw],
                               s_wb[:, s, c, :], s_xnB[:, c, t0:t0 + tw],
                               start=(c == 0), stop=(c == 15), wait=w,
                               inc=[("PE", 1)] if c == 15 else ())
                    if ob < 16:
                        dst, bias = s_qT[:, ob, t0:t0 + tw], s_bq[:, ob:ob + 1]
                    elif ob < 18:
                        dst, bias = s_kT[:, ob - 16, t0:t0 + tw], s_bk[:, ob - 16:ob - 15]
                    else:
                        dst, bias = s_vT[:, ob - 18, t0:t0 + tw], s_bv[:, ob - 18:ob - 17]
                    P.emit("vector", "tensor_scalar", dst, pq[s][:, 0:tw],
                           bias, None, op0=ALU.add,
                           wait=P.after("PE"), inc=[("VE", 1)])
                    P.R(f"a_p{'AC'[s]}", "VE")
                P.guards.setdefault(f"w1{s}", []).append(("PE", P.val("PE")))
            ve = P.val("VE")
            for g in range(2):
                P.emit("sync", "dma_start_transpose", s_vtok[:, g, :, :],
                       s_vT[:, g, :], wait=(("VE", ve),) + P.W(f"vtok{g}"),
                       inc=[("TR", 16)])

        with ExitStack() as sub:
            s_nm = alloc(env, sub, "nm", [128, 2], F32)
            seed(P, ["nm0", "nm1", "au0", "au1", "an0", "an1", "at0", "at1"])
            s_bias = alloc(env, sub, "sb", [128, 2], F32)
            s_au = alloc(env, sub, "au", [128, 2, 256], BF16)
            s_rs = alloc(env, sub, "rs", [128, 2], F32)
            s_rn = alloc(env, sub, "rn", [128, 2], F32)
            s_an = alloc(env, sub, "an", [128, 2, 256], BF16)
            s_at = alloc(env, sub, "at", [128, 2, 256], BF16)
            p_sc = [p_A, p_C]
            p_at = [p_E0, p_E1]
            p_av = [p_B, p_D]
            trv = P.val("TR")
            it = 0
            for wdw in range(4):
                for h in range(16):
                    for qt in range(2):
                        s = it & 1
                        g = h & 1
                        w = P.W(f"a_p{'AC'[s]}")
                        if it == 0:
                            w = w + P.after("VE") + (("TR", trv),)
                        P.emit("tensor", "matmul", p_sc[s][:, 0:256],
                               s_qT[:, h, 128 * (wdw + qt):128 * (wdw + qt) + 128],
                               s_kT[:, g, 128 * wdw:128 * wdw + 256],
                               start=True, stop=True, wait=w, inc=[("PE", 1)])
                        P.emit("vector", "tensor_reduce", s_nm[:, s:s + 1],
                               p_sc[s][:, 0:256], axis=AX.X, op=ALU.max, negate=True,
                               wait=P.after("PE") + P.W(f"nm{s}"))
                        P.emit("vector", "tensor_scalar", s_bias[:, s:s + 1],
                               s_nm[:, s:s + 1], SCALE1, None, op0=ALU.mult,
                               inc=[("VE", 1)])
                        P.emit("scalar", "activation", s_au[:, s, :],
                               p_sc[s][:, 0:256], AF.Exp, scale=SCALE1,
                               bias=s_bias[:, s:s + 1], accum_out=s_rs[:, s:s + 1],
                               wait=P.after("VE") + P.W(f"au{s}"), inc=[("AC", 1)])
                        P.R(f"a_p{'AC'[s]}", "AC")
                        P.guards.setdefault(f"nm{s}", []).append(("AC", P.val("AC")))
                        P.emit("vector", "reciprocal", s_rn[:, s:s + 1],
                               s_rs[:, s:s + 1], wait=P.after("AC"))
                        P.emit("vector", "tensor_scalar", s_an[:, s, :],
                               s_au[:, s, :], s_rn[:, s:s + 1], None, op0=ALU.mult,
                               wait=P.W(f"an{s}"), inc=[("VE", 1)])
                        P.guards.setdefault(f"au{s}", []).append(("VE", P.val("VE")))
                        w = P.after("VE") + P.W(f"a_pE{s}")
                        P.emit("tensor", "transpose", p_at[s][:, 0:128],
                               s_an[:, s, 0:128], s_id[:, :], wait=w)
                        P.emit("tensor", "transpose", p_at[s][:, 128:256],
                               s_an[:, s, 128:256], s_id[:, :], inc=[("PE", 1)])
                        P.guards.setdefault(f"an{s}", []).append(("PE", P.val("PE")))
                        P.emit("vector", "tensor_copy", s_at[:, s, :],
                               p_at[s][:, 0:256],
                               wait=P.after("PE") + P.W(f"at{s}"), inc=[("VE", 1)])
                        P.R(f"a_pE{s}", "VE")
                        w = P.after("VE") + P.W(f"a_p{'BD'[s]}")
                        for kc in range(2):
                            P.emit("tensor", "matmul", p_av[s][:, 0:128],
                                   s_vtok[:, g, wdw + kc, :],
                                   s_at[:, s, 128 * kc:128 * kc + 128],
                                   start=(kc == 0), stop=(kc == 1),
                                   wait=w if kc == 0 else (),
                                   inc=[("PE", 1)] if kc == 1 else ())
                        P.guards.setdefault(f"at{s}", []).append(("PE", P.val("PE")))
                        P.guards.setdefault(f"vtok{g}", []).append(("PE", P.val("PE")))
                        P.emit("vector", "tensor_copy",
                               s_oT[:, h, 256 * wdw + 128 * qt:256 * wdw + 128 * qt + 128],
                               p_av[s][:, 0:128], wait=P.after("PE"), inc=[("VE", 1)])
                        P.R(f"a_p{'BD'[s]}", "VE")
                        it += 1

            s_wo = alloc(env, sub, "wobuf1", [128, 2, NCH, 128], BF16)
            seed(P, ["wo0", "wo1"])
            wor = Wo.rearrange("(c p) o -> p c o", p=128)
            pw = [p_A, p_C]
            ove = P.val("VE")
            for dt in range(NCH):
                s = dt & 1
                P.emit("gpsimd", "dma_start", out=s_wo[:, s, :, :],
                       in_=wor[:, :, 128 * dt:128 * (dt + 1)],
                       wait=P.W(f"wo{s}"), inc=[("LC", 16)])
                lc = P.val("LC")
                for th in range(2):
                    tsl = slice(512 * th, 512 * th + 512)
                    for c in range(NCH):
                        w = ()
                        if c == 0:
                            w = (("LC", lc), ("VE", ove)) + P.W(f"a_p{'AC'[s]}")
                        P.emit("tensor", "matmul", pw[s][:, 0:512],
                               s_wo[:, s, c, :], s_oT[:, c, tsl],
                               start=(c == 0), stop=(c == 15), wait=w,
                               inc=[("PE", 1)] if c == 15 else ())
                    P.emit("vector", "scalar_tensor_tensor", env.s_xres[:, dt, tsl],
                           pw[s][:, 0:512], s_bo[:, dt:dt + 1], env.s_xres[:, dt, tsl],
                           op0=ALU.add, op1=ALU.add,
                           wait=P.after("PE"), inc=[("VE", 1)])
                    P.R(f"a_p{'AC'[s]}", "VE")
                P.guards.setdefault(f"wo{s}", []).append(("PE", P.val("PE")))

    with ExitStack() as ffn:
        s_gatedT = alloc(env, ffn, "gatedT1", [128, 22, TPC], BF16)
        p_A = psum(env, ffn, "f_pA")
        p_B = psum(env, ffn, "f_pB")
        p_C = psum(env, ffn, "f_pC")
        p_D = psum(env, ffn, "f_pD")
        OH_N = (22, 21)
        OH_O = (0, 2816)
        for OH in range(2):
            nblk = OH_N[OH]
            obase = OH_O[OH]
            with ExitStack() as sub:
                s_raw = alloc(env, sub, "giraw", [128, DM], BF16)
                s_iT = alloc(env, sub, "giT", [128, NCH, 128], BF16)
                s_wbt = alloc(env, sub, "wbuilt", [128, 2, 2, NCH, 128], BF16)
                s_term = alloc(env, sub, "term", [128, DM], BF16)
                s_sg = alloc(env, sub, "sg1", [128, 2, 512], F32)
                seed(P, ["giraw", "giT", "wbuilt00", "wbuilt01", "wbuilt10",
                         "wbuilt11", "term", "sg0", "sg1"])
                pg = [p_A, p_C]
                pu = [p_B, p_D]
                for ob in range(nblk):
                    rows = slice(obase + 128 * ob, obase + 128 * (ob + 1))
                    for mi, idxs in enumerate((gidx, uidx)):
                        s = ob & 1
                        P.emit("gpsimd", "dma_start", out=s_raw[:, :],
                               in_=idxs[rows, :], wait=P.W("giraw"), inc=[("LC", 16)])
                        lc = P.val("LC")
                        P.emit("sync", "dma_start_transpose", s_iT[:, :, :],
                               s_raw[:, :], wait=(("LC", lc),) + P.W("giT"),
                               inc=[("TR", 16)])
                        P.R("giraw", "TR")
                        tr = P.val("TR")
                        wflat = s_wbt[:, s, mi, :, :].rearrange("p c o -> p (c o)")
                        P.emit("vector", "memset", wflat, 0.0,
                               wait=(("TR", tr),) + P.W(f"wbuilt{s}{mi}") + P.W("term"))
                        iflat = s_iT[:, :, :].rearrange("p c o -> p (c o)")
                        for j in range(16):
                            P.emit("vector", "tensor_scalar", s_term[:, :], iflat,
                                   float(j), s_lut[:, j:j + 1],
                                   op0=ALU.is_equal, op1=ALU.mult,
                                   wait=(("LD", P.val("LD")),) if (ob == 0 and mi == 0 and j == 0 and OH == 0) else ())
                            P.emit("vector", "tensor_tensor", wflat, wflat,
                                   s_term[:, :], op=ALU.add,
                                   inc=[("VE", 1)] if j == 15 else ())
                        P.R("giT", "VE")
                        vb = P.val("VE")
                        pp = pg if mi == 0 else pu
                        for th in range(2):
                            tsl = slice(512 * th, 512 * th + 512)
                            for c in range(NCH):
                                w = ()
                                if c == 0:
                                    w = (("VE", vb),) + P.W(f"f_p{'ACBD'[2 * mi + th]}")
                                P.emit("tensor", "matmul", pp[th][:, 0:512],
                                       s_wbt[:, s, mi, c, :], env.s_xnT[:, c, tsl],
                                       start=(c == 0), stop=(c == 15), wait=w,
                                       inc=[("PE", 1)] if c == 15 else ())
                        P.guards.setdefault(f"wbuilt{s}{mi}", []).append(("PE", P.val("PE")))
                        P.guards.setdefault("term", []).append(("VE", vb))
                    for th in range(2):
                        tsl = slice(512 * th, 512 * th + 512)
                        P.emit("scalar", "activation", s_sg[:, th, :],
                               pg[th][:, 0:512], AF.Silu,
                               wait=P.after("PE") + P.W(f"sg{th}"), inc=[("AC", 1)])
                        P.R(f"f_p{'AC'[th]}", "AC")
                        P.emit("vector", "tensor_tensor", s_gatedT[:, ob, tsl],
                               s_sg[:, th, :], pu[th][:, 0:512], op=ALU.mult,
                               wait=P.after("AC"), inc=[("VE", 1)])
                        P.R(f"f_p{'BD'[th]}", "VE")
                        P.guards.setdefault(f"sg{th}", []).append(("VE", P.val("VE")))
            with ExitStack() as sub:
                s_draw = alloc(env, sub, "diraw", [128, nblk * 128], BF16)
                s_diT = alloc(env, sub, "diT", [128, nblk, 128], BF16)
                s_wdb = alloc(env, sub, "wdb", [128, 2, 22, 128], BF16)
                s_term = alloc(env, sub, "term2", [128, nblk * 128], BF16)
                seed(P, ["diraw", "diT", "wdb0", "wdb1", "term2"])
                pd = [p_A, p_C]
                gve = P.val("VE")
                for dt in range(NCH):
                    s = dt & 1
                    P.emit("gpsimd", "dma_start", out=s_draw[:, :],
                           in_=didx[128 * dt:128 * (dt + 1), obase:obase + nblk * 128],
                           wait=P.W("diraw"), inc=[("LC", 16)])
                    lc = P.val("LC")
                    P.emit("sync", "dma_start_transpose", s_diT[:, :, :],
                           s_draw[:, :], wait=(("LC", lc),) + P.W("diT"),
                           inc=[("TR", 16)])
                    P.R("diraw", "TR")
                    tr = P.val("TR")
                    wflat = s_wdb[:, s, 0:nblk, :].rearrange("p c o -> p (c o)")
                    P.emit("vector", "memset", wflat, 0.0,
                           wait=(("TR", tr),) + P.W(f"wdb{s}") + P.W("term2"))
                    iflat = s_diT[:, :, :].rearrange("p c o -> p (c o)")
                    for j in range(16):
                        P.emit("vector", "tensor_scalar", s_term[:, :], iflat,
                               float(j), s_lut[:, j:j + 1],
                               op0=ALU.is_equal, op1=ALU.mult)
                        P.emit("vector", "tensor_tensor", wflat, wflat,
                               s_term[:, :], op=ALU.add,
                               inc=[("VE", 1)] if j == 15 else ())
                    P.R("diT", "VE")
                    vb = P.val("VE")
                    for th in range(2):
                        tsl = slice(512 * th, 512 * th + 512)
                        for k in range(nblk):
                            w = ()
                            if k == 0:
                                w = (("VE", max(vb, gve)),) + P.W(f"f_p{'AC'[th]}")
                            P.emit("tensor", "matmul", pd[th][:, 0:512],
                                   s_wdb[:, s, k, :], s_gatedT[:, k, tsl],
                                   start=(k == 0), stop=(k == nblk - 1), wait=w,
                                   inc=[("PE", 1)] if k == nblk - 1 else ())
                        P.emit("vector", "tensor_tensor", env.s_xres[:, dt, tsl],
                               env.s_xres[:, dt, tsl], pd[th][:, 0:512], op=ALU.add,
                               wait=P.after("PE"), inc=[("VE", 1)])
                        P.R(f"f_p{'AC'[th]}", "VE")
                    P.guards.setdefault(f"wdb{s}", []).append(("PE", P.val("PE")))
                    P.guards.setdefault("term2", []).append(("VE", vb))



def plan_fused(nc, ctx):
    xT = nc.dram_tensor("xT", [DM, TPC], F32, kind="ExternalInput").ap()
    wqkv = nc.dram_tensor("w_qkv", [DM, 3 * DM], F32, kind="ExternalInput").ap()
    wout = nc.dram_tensor("w_out", [DM, DM], F32, kind="ExternalInput").ap()
    wnpp = nc.dram_tensor("w_norm_pp", [128, 16], F32, kind="ExternalInput").ap()
    pmk = nc.dram_tensor("pmkT", [128, 256], F32, kind="ExternalInput").ap()
    pmv = nc.dram_tensor("pmvT", [16, DM], F32, kind="ExternalInput").ap()
    wg = nc.dram_tensor("w_gate", [8192, DM], F32, kind="ExternalInput").ap()
    wu = nc.dram_tensor("w_up", [8192, DM], F32, kind="ExternalInput").ap()
    wd = nc.dram_tensor("w_down", [DM, 8192], F32, kind="ExternalInput").ap()
    mask = nc.dram_tensor("mask0", [128, 144], F32, kind="ExternalInput").ap()
    iden = nc.dram_tensor("ident", [128, 128], F32, kind="ExternalInput").ap()
    Wq = nc.dram_tensor("Wq", [DM, DM], F32, kind="ExternalInput").ap()
    Wk = nc.dram_tensor("Wk", [DM, 256], F32, kind="ExternalInput").ap()
    Wv = nc.dram_tensor("Wv", [DM, 256], F32, kind="ExternalInput").ap()
    Wo = nc.dram_tensor("Wo", [DM, DM], F32, kind="ExternalInput").ap()
    bqp = nc.dram_tensor("bq_pp", [128, 16], F32, kind="ExternalInput").ap()
    bkp = nc.dram_tensor("bk_pp", [128, 2], F32, kind="ExternalInput").ap()
    bvp = nc.dram_tensor("bv_pp", [128, 2], F32, kind="ExternalInput").ap()
    bop = nc.dram_tensor("bo_pp", [128, 16], F32, kind="ExternalInput").ap()
    lut = nc.dram_tensor("lut_t", [128, 16], F32, kind="ExternalInput").ap()
    gidx = nc.dram_tensor("gate_idx", [5504, DM], I32, kind="ExternalInput").ap()
    uidx = nc.dram_tensor("up_idx", [5504, DM], I32, kind="ExternalInput").ap()
    didx = nc.dram_tensor("down_idx", [DM, 5504], I32, kind="ExternalInput").ap()
    boff = nc.dram_tensor("boff", [1, 10], I32, kind="ExternalInput").ap()
    oT = nc.dram_tensor("outT", [DM, TPC], F32, kind="ExternalOutput").ap()
    cc_in = nc.dram_tensor("cc_in", [DM, TPC], BF16)
    cc_out = nc.dram_tensor("cc_out", [4 * DM, TPC], BF16)

    env = Env()
    env.nc = nc
    env.ctx = ctx
    env.P = P = Plan()

    env.s_xres = alloc(env, env.ctx, "xres", [128, NCH, TPC], F32)
    env.s_xnT = alloc(env, env.ctx, "xnT", [128, NCH, TPC], BF16)
    env.ones_b = alloc(env, env.ctx, "ones_b", [128, 1], BF16)
    env.ones_f = alloc(env, env.ctx, "ones_f", [1, 128], F32)
    s_wn = alloc(env, env.ctx, "wn", [128, 16], F32)
    s_mask = alloc(env, env.ctx, "mask", [128, 144], F32)
    s_pmk = alloc(env, env.ctx, "pmk", [128, 256], BF16)
    s_pmv = alloc(env, env.ctx, "pmv", [16, DM], BF16)
    s_id = alloc(env, env.ctx, "id", [128, 128], BF16)
    s_bq = alloc(env, env.ctx, "bq", [128, 16], F32)
    s_bk = alloc(env, env.ctx, "bk", [128, 2], F32)
    s_bv = alloc(env, env.ctx, "bv", [128, 2], F32)
    s_bo = alloc(env, env.ctx, "bo", [128, 16], F32)
    s_lut = alloc(env, env.ctx, "lut", [128, 16], F32)
    s_off = alloc(env, env.ctx, "boffs", [1, 10], I32)

    xr = xT.rearrange("(c p) t -> p c t", p=128)
    P.emit("scalar", "dma_start", out=env.s_xres[:, :, :], in_=xr, inc=[("LD", 16)])
    P.emit("scalar", "dma_start", out=s_wn[:, :], in_=wnpp, inc=[("LD", 16)])
    P.emit("scalar", "dma_start", out=s_mask[:, :], in_=mask, inc=[("LD", 16)])
    for dst, srcap in ((s_bq, bqp), (s_bk, bkp), (s_bv, bvp), (s_bo, bop),
                       (s_lut, lut), (s_off, boff)):
        P.emit("scalar", "dma_start", out=dst[:, :], in_=srcap, inc=[("LD", 16)])
    P.emit("gpsimd", "dma_start", out=s_pmk[:, :], in_=pmk, inc=[("LC", 16)])
    P.emit("gpsimd", "dma_start", out=s_pmv[:, :], in_=pmv, inc=[("LC", 16)])
    P.emit("gpsimd", "dma_start", out=s_id[:, :], in_=iden, inc=[("LC", 16)])
    P.emit("vector", "memset", env.ones_b[:, :], 1.0, inc=[("VE", 1)])
    P.emit("vector", "memset", env.ones_f[:, :], 1.0, inc=[("VE", 1)])

    emit_block0(env, wqkv, wout, wg, wu, wd, s_wn, s_mask, s_pmk, s_pmv, s_id)

    with ExitStack() as late:
        s_xnB = alloc(env, late, "xnTB", [128, NCH, 640], BF16)

        with ExitStack() as sub:
            emit_ln_setup(env, sub)
            emit_ln(env, "l1", env.s_xres, env.s_xnT, TPC, dep=P.after("VE"))

        P.emit("scalar", "dma_start",
               out=cc_in.rearrange("(c p) t -> p c t", p=128),
               in_=env.s_xnT[:, :, :], wait=P.after("VE"), inc=[("CC", 16)])
        P.emit("gpsimd", "collective_compute", "AllGather", ALU.bypass,
               replica_groups=[[0, 1, 2, 3], [4, 5, 6, 7]],
               ins=[cc_in.ap().opt()], outs=[cc_out.ap().opt()],
               wait=P.after("CC"), inc=[("CC", 1)])
        ccv = P.val("CC")

        def _bgather(eng, sems):
            for j in range(5):
                rv_row = eng.value_load(s_off[0:1, 2 * j:2 * j + 1])
                rv_col = eng.value_load(s_off[0:1, 2 * j + 1:2 * j + 2])
                src_ap = cc_out[bass.ds(rv_row, DM), bass.ds(rv_col, 128)]
                eng.dma_start(
                    out=s_xnB[:, :, 128 * j:128 * (j + 1)],
                    in_=src_ap.rearrange("(c p) t -> p c t", p=128)
                ).then_inc(sems["LC"], 16)

        P.emit("gpsimd", _bgather,
               wait=(("CC", ccv),) + P.after("LD", "PE"), inc=[("LC", 80)])

        emit_block1(env, Wq, Wk, Wv, Wo, gidx, uidx, didx, s_bq, s_bk, s_bv,
                    s_bo, s_lut, s_id, s_xnB, P.after("LC"))

    orr = oT.rearrange("(c p) t -> p c t", p=128)
    P.emit("sync", "dma_start", out=orr, in_=env.s_xres[:, :, :],
           wait=P.after("VE", "AC", "PE"), inc=[("ST", 16)])
    P.emit("sync", "nop", wait=P.after("ST"))
    return env



def _build_mask0():
    m = np.full((128, 144), NEG, np.float32)
    for q in range(128):
        sseg, j = q // 32, q % 32
        for c in range(128):
            if c // 32 == sseg and (c % 32) <= j - 16:
                m[q, c] = 0.0
        for p in range(16):
            if p <= j:
                m[q, 128 + p] = 0.0
    return m


def _pp(v, n):
    return np.ascontiguousarray(np.asarray(v, np.float32).reshape(n, 128).T)


def kernel(**inp):
    x = np.asarray(inp["L0_attn_seq"], np.float32)
    pm = np.asarray(inp["L0_attn_pm"], np.float32)
    pmkT = np.ascontiguousarray(
        pm[0].reshape(16, 2, 16, 64).transpose(1, 3, 0, 2).reshape(128, 256))
    pmvT = np.ascontiguousarray(pm[1].transpose(1, 0, 2).reshape(16, 2048))
    wn_pp = _pp(inp["L0_attn_w_norm"], 16)
    mask0 = _build_mask0()
    ident = np.eye(128, dtype=np.float32)
    lut_t = np.ascontiguousarray(
        np.tile(np.asarray(inp["L1_ffn_lut"], np.float32)[None, :], (128, 1)))

    with ExitStack() as ctx:
        nc = bass.Bass("TRN2", target_bir_lowering=False)
        plan = plan_fused(nc, ctx)
        playback(nc, ctx, plan.P)
        in_maps = []
        for core in range(NCORE):
            b, m = divmod(core, 4)
            offs = []
            for j in range(5):
                t = 512 * m + 128 * j
                offs += [2048 * (t // 1024), t % 1024]
            in_maps.append({
                "xT": np.ascontiguousarray(x[b, 1024 * m:1024 * (m + 1), :].T),
                "w_qkv": inp["L0_attn_w_qkv"],
                "w_out": inp["L0_attn_w_out"],
                "w_norm_pp": wn_pp,
                "pmkT": pmkT,
                "pmvT": pmvT,
                "w_gate": inp["L0_ffn_w_gate"],
                "w_up": inp["L0_ffn_w_up"],
                "w_down": inp["L0_ffn_w_down"],
                "mask0": mask0,
                "ident": ident,
                "Wq": inp["L1_attn_Wq"],
                "Wk": inp["L1_attn_Wk"],
                "Wv": inp["L1_attn_Wv"],
                "Wo": inp["L1_attn_Wo"],
                "bq_pp": _pp(inp["L1_attn_bq"], 16),
                "bk_pp": _pp(inp["L1_attn_bk"], 2),
                "bv_pp": _pp(inp["L1_attn_bv"], 2),
                "bo_pp": _pp(inp["L1_attn_bo"], 16),
                "lut_t": lut_t,
                "gate_idx": inp["L1_ffn_gate_idx"],
                "up_idx": inp["L1_ffn_up_idx"],
                "down_idx": inp["L1_ffn_down_idx"],
                "boff": np.asarray([offs], np.int32),
            })
        res = run_bass_kernel_spmd(nc, in_maps, core_ids=list(range(NCORE)))

    outf = [np.concatenate([res.results[4 * b + m]["outT"] for m in range(4)], axis=1)
            for b in range(2)]
    out = np.stack([outf[0].T, outf[1].T], axis=0)
    return np.ascontiguousarray(out).astype(np.float32)


# version 04
# speedup vs baseline: 1.3424x; 1.3424x over previous
import dataclasses
from contextlib import ExitStack

import numpy as np

import concourse.bass as bass
import concourse.mybir as mybir
from concourse.bass_utils import run_bass_kernel_spmd

F32 = mybir.dt.float32
BF16 = mybir.dt.bfloat16
I32 = mybir.dt.int32
AF = mybir.ActivationFunctionType
ALU = mybir.AluOpType
AX = mybir.AxisListType

NCORE = 8
DM = 2048
NCH = 16
TPC = 1024
EPS_LN = 1e-5
EPS_RMS = float(np.finfo(np.float32).eps)
NEG = -1.0e30
SCALE0 = 0.125
SCALE1 = float(1.0 / np.sqrt(128.0))

ENGS = ("tensor", "vector", "scalar", "gpsimd", "sync")
DEBUG0 = False
TRUNC0 = False
TRUNC1 = False
TRUNC2 = False
XNROVR = False
NFLUSH = 0


def bc0(ap, axis, n):
    newap = [list(x) for x in ap.ap]
    assert newap[axis][1] == 1, (newap, axis)
    newap[axis] = [0, n]
    return dataclasses.replace(ap, ap=newap)


class Plan:
    def __init__(self):
        self.ops = {e: [] for e in ENGS}
        self.cnt = {}
        self.guards = {}

    def val(self, s):
        return self.cnt.get(s, 0)

    muted = False

    def emit(self, eng, meth, *args, wait=(), inc=(), **kw):
        if self.muted:
            return
        for s, v in wait:
            assert v <= self.val(s), (eng, meth, s, v, self.val(s))
        for s, n in inc:
            self.cnt[s] = self.val(s) + n
        self.ops[eng].append((meth, args, kw, tuple(wait), tuple(inc)))

    def W(self, name):
        return tuple(self.guards.pop(name, ()))

    def R(self, name, sem):
        g = self.guards.setdefault(name, [])
        g.append((sem, self.val(sem)))

    def after(self, *sems):
        return tuple((s, self.val(s)) for s in sems)


def playback(nc, ctx, plan):
    sems = {name: ctx.enter_context(nc.semaphore(name)) for name in plan.cnt}
    block = ctx.enter_context(nc.Block())
    for eng_name in ("sync", "gpsimd", "scalar", "vector", "tensor"):
        ops = plan.ops[eng_name]
        if not ops:
            continue
        dec = getattr(block, eng_name)

        def make(ops_):
            def body(eng):
                for meth, args, kw, waits, incs in ops_:
                    for s, v in waits:
                        if v > 0:
                            eng.wait_ge(sems[s], v)
                    if callable(meth):
                        meth(eng, sems)
                        continue
                    ins = getattr(eng, meth)(*args, **kw)
                    for s, n in incs:
                        ins.then_inc(sems[s], n)
            return body
        dec(make(ops))


def seed(P, names):
    f = P.after("PE", "VE", "AC", "LD", "LC", "TR", "ST")
    for n in names:
        P.guards.setdefault(n, []).extend(f)


def tiles(total, size):
    out = []
    o = 0
    while o < total:
        out.append((o, min(size, total - o)))
        o += size
    return out



class Env:
    pass


def alloc(env, ctx, name, shape, dt):
    env.seq = getattr(env, "seq", 0) + 1
    return ctx.enter_context(env.nc.sbuf_tensor(f"{name}_{env.seq}", shape, dt))


def psum(env, ctx, name, dt=F32):
    env.seq = getattr(env, "seq", 0) + 1
    shape = [128, 512] if dt == F32 else [128, 1024]
    return ctx.enter_context(env.nc.psum_tensor(f"{name}_{env.seq}", shape, dt))


def emit_ln_setup(env, sub):
    env.s_xb = alloc(env, sub, "ln_xb", [128, 2, 512], BF16)
    env.s_xq = alloc(env, sub, "ln_xq", [128, 2, 512], BF16)
    env.st_m = alloc(env, sub, "ln_m", [1, 512], F32)
    env.st_q = alloc(env, sub, "ln_q", [1, 512], F32)
    env.st_t = alloc(env, sub, "ln_t", [1, 512], F32)
    env.st_t2 = alloc(env, sub, "ln_t2", [1, 512], F32)
    env.st_A = alloc(env, sub, "ln_A", [1, 512], F32)
    env.st_B = alloc(env, sub, "ln_B", [1, 512], F32)
    env.s_t1 = alloc(env, sub, "ln_t1", [128, 512], F32)
    env.p_s = psum(env, sub, "ln_ps")
    env.p_q = psum(env, sub, "ln_pq")
    env.p_A = psum(env, sub, "ln_pA")
    env.p_B = psum(env, sub, "ln_pB")


def emit_ln(env, tag, x3, xn3, T, dep):
    P = env.P
    for t0, tw in tiles(T, 512):
        sl = slice(t0, t0 + tw)
        for c in range(NCH):
            s = c & 1
            P.emit("vector", "tensor_copy", env.s_xb[:, s, 0:tw], x3[:, c, sl],
                   wait=P.W(f"{tag}xb{s}") + (dep if (t0 == 0 and c == 0) else ()),
                   inc=[("VE", 1)])
            P.emit("scalar", "activation", env.s_xq[:, s, 0:tw], env.s_xb[:, s, 0:tw],
                   AF.Square, wait=P.after("VE") + P.W(f"{tag}xq{s}"), inc=[("AC", 1)])
            w = P.after("AC")
            if c == 0:
                w = w + P.W(f"{tag}ps") + P.W(f"{tag}pq")
            P.emit("tensor", "matmul", env.p_s[0:1, 0:tw], env.ones_b[:, :],
                   env.s_xb[:, s, 0:tw], start=(c == 0), stop=(c == 15),
                   skip_group_check=True, wait=w)
            P.emit("tensor", "matmul", env.p_q[0:1, 0:tw], env.ones_b[:, :],
                   env.s_xq[:, s, 0:tw], start=(c == 0), stop=(c == 15),
                   skip_group_check=True, inc=[("PE", 1)])
            P.guards.setdefault(f"{tag}xb{s}", []).append(("PE", P.val("PE")))
            P.guards.setdefault(f"{tag}xq{s}", []).append(("PE", P.val("PE")))
        P.emit("vector", "tensor_scalar", env.st_m[:, 0:tw], env.p_s[0:1, 0:tw],
               1.0 / DM, None, op0=ALU.mult, wait=P.after("PE"))
        P.emit("vector", "tensor_scalar", env.st_q[:, 0:tw], env.p_q[0:1, 0:tw],
               1.0 / DM, EPS_LN, op0=ALU.mult, op1=ALU.add)
        P.emit("vector", "tensor_tensor", env.st_t[:, 0:tw], env.st_m[:, 0:tw],
               env.st_m[:, 0:tw], op=ALU.mult)
        P.emit("vector", "tensor_tensor", env.st_t2[:, 0:tw], env.st_q[:, 0:tw],
               env.st_t[:, 0:tw], op=ALU.subtract, inc=[("VE", 1)])
        P.R(f"{tag}ps", "VE")
        P.R(f"{tag}pq", "VE")
        P.emit("scalar", "activation", env.st_t[:, 0:tw], env.st_t2[:, 0:tw],
               AF.Ln, wait=P.after("VE"))
        P.emit("scalar", "activation", env.st_A[:, 0:tw], env.st_t[:, 0:tw],
               AF.Exp, scale=-0.5, inc=[("AC", 1)],
               wait=P.W(f"{tag}stA"))
        P.emit("vector", "scalar_tensor_tensor", env.st_B[:, 0:tw], env.st_m[:, 0:tw],
               -1.0, env.st_A[:, 0:tw], op0=ALU.mult, op1=ALU.mult,
               wait=P.after("AC") + P.W(f"{tag}stB"), inc=[("VE", 1)])
        P.emit("tensor", "matmul", env.p_A[:, 0:tw], env.ones_f[:, :],
               env.st_A[:, 0:tw], start=True, stop=True,
               wait=P.after("VE") + P.W(f"{tag}pA"))
        P.emit("tensor", "matmul", env.p_B[:, 0:tw], env.ones_f[:, :],
               env.st_B[:, 0:tw], start=True, stop=True,
               wait=P.W(f"{tag}pB"), inc=[("PE", 1)])
        P.R(f"{tag}stA", "PE")
        P.R(f"{tag}stB", "PE")
        for c in range(NCH):
            P.emit("vector", "tensor_tensor", env.s_t1[:, 0:tw], x3[:, c, sl],
                   env.p_A[:, 0:tw], op=ALU.mult,
                   wait=P.after("PE") if c == 0 else ())
            P.emit("vector", "tensor_tensor", xn3[:, c, sl], env.s_t1[:, 0:tw],
                   env.p_B[:, 0:tw], op=ALU.add,
                   inc=[("VE", 1)] if c == 15 else ())
        P.R(f"{tag}pA", "VE")
        P.R(f"{tag}pB", "VE")




GB_BASE = [0, 6, 12, 18, 24, 30, 36, 37]
GB_POS = {ob: ob for ob in range(42)}
GB_POS[42] = 47


class GhostFeeder:

    def __init__(self, env, g_my, u_my, d_my, cc_gu_in, cc_d_in, s_lut):
        self.env = env
        self.cc_gu_in, self.cc_d_in = cc_gu_in, cc_d_in
        self.s_lut = s_lut
        self.pieces = []
        for mi, srcs in enumerate((g_my, u_my)):
            for i in range(6):
                self.pieces.append(dict(
                    src=srcs[128 * i:128 * (i + 1), :], wid=DM, nblk=NCH,
                    dst=cc_gu_in[128 * (6 * mi + i):128 * (6 * mi + i + 1), :]))
        for j in range(2):
            for cc, (c0, w) in enumerate(((0, 2048), (2048, 2048), (4096, 1408))):
                self.pieces.append(dict(
                    src=d_my[128 * j:128 * (j + 1), c0:c0 + w], wid=w,
                    nblk=w // 128,
                    dst=cc_d_in[128 * j:128 * (j + 1), c0:c0 + w]))
        self.pi = 0
        self.js = 0
        self.staged = -1
        self.done = False

    def setup(self, ctx):
        env = self.env
        self.s_raw = alloc(env, ctx, "gb_raw", [128, 1, DM], BF16)
        self.s_iT = alloc(env, ctx, "gb_iT", [128, 2, NCH, 128], BF16)
        self.s_bld = alloc(env, ctx, "gb_bld", [128, NCH, 128], BF16)
        self.s_term = alloc(env, ctx, "gb_term", [128, DM], BF16)
        seed(env.P, ["gbraw0", "gbraw1", "gbiT0", "gbiT1", "gbbld", "gbterm"])
        env.P.guards.setdefault("gbterm", []).append(("LD", env.P.val("LD")))

    def _stage(self, k):
        if k >= len(self.pieces):
            return
        P = self.env.P
        pc = self.pieces[k]
        s = k & 1
        P.emit("gpsimd", "dma_start", out=self.s_raw[:, 0, 0:pc["wid"]],
               in_=pc["src"], wait=P.W("gbraw0"), inc=[("LC", 16)])
        lc = P.val("LC")
        P.emit("sync", "dma_start_transpose", self.s_iT[:, s, 0:pc["nblk"], :],
               self.s_raw[:, 0, 0:pc["wid"]],
               wait=(("LC", lc),) + P.W(f"gbiT{s}"), inc=[("TR", 16)])
        P.R("gbraw0", "TR")
        self.staged = k

    def feed(self, n):
        if self.done:
            return
        P = self.env.P
        while n > 0 and not self.done:
            if self.staged < self.pi:
                self._stage(self.pi)
                if self.staged < self.pi:
                    self.done = True
                    return
            pc = self.pieces[self.pi]
            s = self.pi & 1
            wid, nblk = pc["wid"], pc["nblk"]
            iflat = self.s_iT[:, s, 0:nblk, :].rearrange("p c o -> p (c o)")
            bflat = self.s_bld[:, 0:nblk, :].rearrange("p c o -> p (c o)")
            if self.js == 0:
                tr = P.val("TR")
                P.emit("vector", "memset", bflat, 0.0,
                       wait=(("TR", tr),) + P.W("gbbld"))
                self._stage(self.pi + 1)
            while n > 0 and self.js < 16:
                j = self.js
                P.emit("vector", "tensor_scalar", self.s_term[:, 0:wid], iflat,
                       float(j), self.s_lut[:, j:j + 1],
                       op0=ALU.is_equal, op1=ALU.mult,
                       wait=P.W("gbterm") if j == 0 else ())
                P.emit("vector", "tensor_tensor", bflat, bflat,
                       self.s_term[:, 0:wid], op=ALU.add,
                       inc=[("VE", 1)] if j == 15 else ())
                self.js += 1
                n -= 1
            if self.js == 16:
                P.R(f"gbiT{s}", "VE")
                P.R("gbterm", "VE")
                ve = P.val("VE")
                P.emit("scalar", "dma_start", out=pc["dst"], in_=bflat,
                       wait=(("VE", ve),), inc=[("CW", 16)])
                P.R("gbbld", "CW")
                self.pi += 1
                self.js = 0
                if self.pi >= len(self.pieces):
                    self.done = True

    def flush(self):
        while not self.done:
            self.feed(16)



def plan_block0(nc, ctx):
    xT = nc.dram_tensor("xT", [DM, TPC], F32, kind="ExternalInput").ap()
    wqkv = nc.dram_tensor("w_qkv", [DM, 3 * DM], F32, kind="ExternalInput").ap()
    wout = nc.dram_tensor("w_out", [DM, DM], F32, kind="ExternalInput").ap()
    wnpp = nc.dram_tensor("w_norm_pp", [128, 16], F32, kind="ExternalInput").ap()
    pmk = nc.dram_tensor("pmkT", [128, 256], F32, kind="ExternalInput").ap()
    pmv = nc.dram_tensor("pmvT", [16, DM], F32, kind="ExternalInput").ap()
    wg = nc.dram_tensor("w_gate", [8192, DM], F32, kind="ExternalInput").ap()
    wu = nc.dram_tensor("w_up", [8192, DM], F32, kind="ExternalInput").ap()
    wd = nc.dram_tensor("w_down", [DM, 8192], F32, kind="ExternalInput").ap()
    mask = nc.dram_tensor("mask0", [128, 144], F32, kind="ExternalInput").ap()
    iden = nc.dram_tensor("ident", [128, 128], F32, kind="ExternalInput").ap()
    x1T = nc.dram_tensor("x1T", [DM, TPC], F32, kind="ExternalOutput").ap()

    env = Env()
    env.nc = nc
    env.ctx = ctx
    env.P = P = Plan()

    env.s_xres = alloc(env, env.ctx, "xres", [128, NCH, TPC], F32)
    env.s_xnT = alloc(env, env.ctx, "xnT", [128, NCH, TPC], BF16)
    env.ones_b = alloc(env, env.ctx, "ones_b", [128, 1], BF16)
    env.ones_f = alloc(env, env.ctx, "ones_f", [1, 128], F32)
    s_wn = alloc(env, env.ctx, "wn", [128, 16], F32)
    s_mask = alloc(env, env.ctx, "mask", [128, 144], F32)
    s_pmk = alloc(env, env.ctx, "pmk", [128, 256], BF16)
    s_pmv = alloc(env, env.ctx, "pmv", [16, DM], BF16)
    s_id = alloc(env, env.ctx, "id", [128, 128], BF16)

    xr = xT.rearrange("(c p) t -> p c t", p=128)
    P.emit("scalar", "dma_start", out=env.s_xres[:, :, :], in_=xr, inc=[("LD", 16)])
    P.emit("scalar", "dma_start", out=s_wn[:, :], in_=wnpp, inc=[("LD", 16)])
    P.emit("scalar", "dma_start", out=s_mask[:, :], in_=mask, inc=[("LD", 16)])
    P.emit("gpsimd", "dma_start", out=s_pmk[:, :], in_=pmk, inc=[("LC", 16)])
    P.emit("gpsimd", "dma_start", out=s_pmv[:, :], in_=pmv, inc=[("LC", 16)])
    P.emit("gpsimd", "dma_start", out=s_id[:, :], in_=iden, inc=[("LC", 16)])
    P.emit("vector", "memset", env.ones_b[:, :], 1.0, inc=[("VE", 1)])
    P.emit("vector", "memset", env.ones_f[:, :], 1.0, inc=[("VE", 1)])

    emit_block0(env, wqkv, wout, wg, wu, wd, s_wn, s_mask, s_pmk, s_pmv, s_id)

    x1r = x1T.rearrange("(c p) t -> p c t", p=128)
    P.emit("sync", "dma_start", out=x1r, in_=env.s_xres[:, :, :],
           wait=P.after("VE", "AC", "PE"), inc=[("ST", 16)])
    P.emit("sync", "nop", wait=P.after("ST"))
    return env


def emit_block0(env, wqkv, wout, wg, wu, wd, s_wn, s_mask, s_pmk, s_pmv, s_id,
                feeder=None):
    P = env.P
    with ExitStack() as sub:
        emit_ln_setup(env, sub)
        emit_ln(env, "l0", env.s_xres, env.s_xnT, TPC,
                dep=(("LD", 16),) + P.after("VE"))

    with ExitStack() as attn:
        s_qT = alloc(env, attn, "qT0", [128, NCH, 512], BF16)
        s_kT = alloc(env, attn, "kT0", [128, NCH, 512], BF16)
        s_vtok = alloc(env, attn, "vtok0", [128, NCH, 4, 128], BF16)
        seed(P, [f"vtok{c}" for c in range(NCH)])
        if feeder is not None:
            feeder.setup(attn)
        p_A = psum(env, attn, "a_pA")
        p_B = psum(env, attn, "a_pB")
        p_C = psum(env, attn, "a_pC")
        p_D = psum(env, attn, "a_pD")
        p_E0 = psum(env, attn, "a_pE0", BF16)
        p_E1 = psum(env, attn, "a_pE1", BF16)
        p_F = psum(env, attn, "a_pF")
        p_G = psum(env, attn, "a_pG")
        st_r = alloc(env, attn, "st_r", [1, 512], F32)

        for half in range(2):
            hsl = slice(512 * half, 512 * half + 512)
            halfstk = env.ctx.enter_context(ExitStack()) if False else ExitStack()
            halfstk.__enter__()
            s_xnr = alloc(env, halfstk, "xnr", [128, NCH, 512], BF16)
            with ExitStack() as sub:
                s_sq = alloc(env, sub, "rms_sq", [128, 2, 512], BF16)
                seed(P, ["rsq0", "rsq1"])
                for c in range(NCH):
                    s = c & 1
                    P.emit("scalar", "activation", s_sq[:, s, :], env.s_xnT[:, c, hsl],
                           AF.Square, wait=(P.after("VE") if c == 0 else ()) + P.W(f"rsq{s}"),
                           inc=[("AC", 1)])
                    P.emit("tensor", "matmul", p_F[0:1, 0:512], env.ones_b[:, :],
                           s_sq[:, s, :], start=(c == 0), stop=(c == 15),
                           wait=P.after("AC") + (P.W("a_pF") if c == 0 else ()),
                           inc=[("PE", 1)] if c == 15 else ())
                    P.guards.setdefault(f"rsq{s}", []).append(("PE", P.val("PE")))
                P.emit("vector", "tensor_scalar", st_r[:, 0:512], p_F[0:1, 0:512],
                       1.0 / DM, EPS_RMS, op0=ALU.mult, op1=ALU.add,
                       wait=P.after("PE") + P.W("st_r"), inc=[("VE", 1)])
                P.R("a_pF", "VE")
                P.emit("scalar", "activation", st_r[:, 0:512], st_r[:, 0:512],
                       AF.Ln, wait=P.after("VE"))
                P.emit("scalar", "activation", st_r[:, 0:512], st_r[:, 0:512],
                       AF.Exp, scale=-0.5, inc=[("AC", 1)])
                P.emit("tensor", "matmul", p_B[:, 0:512], env.ones_f[:, :],
                       st_r[:, 0:512], start=True, stop=True,
                       wait=P.after("AC") + P.W("a_pB"), inc=[("PE", 1)])
                P.R("st_r", "PE")
                for c in range(NCH):
                    P.emit("vector", "scalar_tensor_tensor", s_xnr[:, c, :],
                           env.s_xnT[:, c, hsl], s_wn[:, c:c + 1], p_B[:, 0:512],
                           op0=ALU.mult, op1=ALU.mult,
                           wait=(P.after("PE") + (("LD", 32),) + P.W("xnr")) if c == 0 else (),
                           inc=[("VE", 1)] if c == 15 else ())
                P.R("a_pB", "VE")

            with ExitStack() as sub:
                s_wq = alloc(env, sub, "wqbuf", [128, 2, NCH, 128], BF16)
                s_vT = alloc(env, sub, "vT0", [128, 2, 512], BF16)
                seed(P, ["wq0", "wq1", "vTs0", "vTs1"])
                wqr = wqkv.rearrange("(c p) o -> p c o", p=128)
                pq = [p_A, p_C]
                for ob in range(48):
                    s = ob & 1
                    P.emit("gpsimd", "dma_start", out=s_wq[:, s, :, :],
                           in_=wqr[:, :, 128 * ob:128 * (ob + 1)],
                           wait=P.W(f"wq{s}"), inc=[("LC", 16)])
                    lc = P.val("LC")
                    for c in range(NCH):
                        w = ()
                        if c == 0:
                            w = (("LC", lc),) + P.W(f"a_p{'ACFG'[s]}") + \
                                (P.after("VE", "AC", "PE", "LD", "TR") if ob == 0 else ())
                        P.emit("tensor", "matmul", pq[s][:, 0:512],
                               s_wq[:, s, c, :], s_xnr[:, c, :],
                               start=(c == 0), stop=(c == 15), wait=w,
                               inc=[("PE", 1)] if c == 15 else ())
                    P.guards.setdefault(f"wq{s}", []).append(("PE", P.val("PE")))
                    if ob < 16:
                        P.emit("vector", "tensor_scalar", s_qT[:, ob, :],
                               pq[s][:, 0:512], SCALE0, None, op0=ALU.mult,
                               wait=P.after("PE"), inc=[("VE", 1)])
                    elif ob < 32:
                        P.emit("vector", "tensor_copy", s_kT[:, ob - 16, :],
                               pq[s][:, 0:512], wait=P.after("PE"), inc=[("VE", 1)])
                    else:
                        vs = (ob - 32) & 1
                        P.emit("vector", "tensor_copy", s_vT[:, vs, :],
                               pq[s][:, 0:512],
                               wait=P.after("PE") + P.W(f"vTs{vs}"), inc=[("VE", 1)])
                        c_v = ob - 32
                        P.emit("sync", "dma_start_transpose",
                               s_vtok[:, c_v, :, :], s_vT[:, vs, :],
                               wait=P.after("VE") + P.W(f"vtok{c_v}"),
                               inc=[("TR", 16)])
                        P.guards.setdefault(f"vTs{vs}", []).append(("TR", P.val("TR")))
                    P.R(f"a_p{'ACFG'[s]}", "VE")
                    if feeder is not None:
                        feeder.feed(3)
                P.R("xnr", "PE")
            halfstk.__exit__(None, None, None)

            with ExitStack() as sub:
                s_sm = alloc(env, sub, "sm", [128, 4, 144], F32)
                seed(P, [f"sm{i}" for i in range(4)] + [f"au{i}" for i in range(4)]
                     + [f"an{i}" for i in range(4)] + ["at0", "at1"])
                s_nm = alloc(env, sub, "nm", [128, 4], F32)
                s_au = alloc(env, sub, "au", [128, 4, 144], BF16)
                s_rs = alloc(env, sub, "rs", [128, 4], F32)
                s_rn = alloc(env, sub, "rn", [128, 4], F32)
                s_an = alloc(env, sub, "an", [128, 4, 144], BF16)
                s_at = alloc(env, sub, "at", [128, 2, 256], BF16)
                s_oT = alloc(env, sub, "oT0", [128, NCH, 512], BF16)
                p_sc = [p_A, p_C, p_F, p_G]
                p_at = [p_E0, p_E1]
                p_av = [p_B, p_D]
                trv = P.val("TR")
                for it, (h, tl) in enumerate(
                        [(h, t) for h in range(32) for t in range(4)]):
                    s = it & 3
                    s2 = it & 1
                    a = h & 1
                    ch = h >> 1
                    qsl = (slice(64 * a, 64 * a + 64), ch,
                           slice(128 * tl, 128 * tl + 128))
                    w = P.W(f"a_p{'ACFG'[s]}")
                    if it == 0:
                        w = w + P.after("VE") + (("LC", 48),) + (("LD", 48),)
                    P.emit("tensor", "matmul", p_sc[s][:, 0:128],
                           s_qT[qsl], s_kT[qsl], start=True, stop=False, wait=w)
                    P.emit("tensor", "matmul", p_sc[s][:, 128:144],
                           s_qT[qsl], s_pmk[64 * a:64 * a + 64, 16 * ch:16 * ch + 16],
                           start=False, stop=True, inc=[("PE", 1)])
                    P.emit("vector", "tensor_tensor", s_sm[:, s, :],
                           p_sc[s][:, 0:144], s_mask[:, :], op=ALU.add,
                           wait=P.after("PE") + P.W(f"sm{s}"))
                    P.emit("vector", "tensor_reduce", s_nm[:, s:s + 1],
                           s_sm[:, s, :], axis=AX.X, op=ALU.max, negate=True,
                           inc=[("VE", 1)])
                    P.guards.setdefault(f"a_p{'ACFG'[s]}", []).append(("VE", P.val("VE")))
                    P.emit("scalar", "activation", s_au[:, s, :], s_sm[:, s, :],
                           AF.Exp, bias=s_nm[:, s:s + 1],
                           accum_out=s_rs[:, s:s + 1],
                           wait=P.after("VE") + P.W(f"au{s}"), inc=[("AC", 1)])
                    P.guards.setdefault(f"sm{s}", []).append(("AC", P.val("AC")))
                    P.emit("vector", "reciprocal", s_rn[:, s:s + 1], s_rs[:, s:s + 1],
                           wait=P.after("AC"))
                    P.emit("vector", "tensor_scalar", s_an[:, s, :], s_au[:, s, :],
                           s_rn[:, s:s + 1], None, op0=ALU.mult,
                           wait=P.W(f"an{s}"), inc=[("VE", 1)])
                    P.guards.setdefault(f"au{s}", []).append(("VE", P.val("VE")))
                    w = P.after("VE") + P.W(f"a_pE{s2}")
                    P.emit("tensor", "transpose", p_at[s2][:, 0:128],
                           s_an[:, s, 0:128], s_id[:, :], wait=w)
                    P.emit("tensor", "transpose", p_at[s2][0:16, 128:256],
                           s_an[:, s, 128:144], s_id[:, :], inc=[("PE", 1)])
                    P.guards.setdefault(f"an{s}", []).append(("PE", P.val("PE")))
                    P.emit("vector", "tensor_copy", s_at[:, s2, :], p_at[s2][:, 0:256],
                           wait=P.after("PE") + P.W(f"at{s2}"), inc=[("VE", 1)])
                    P.R(f"a_pE{s2}", "VE")
                    w = P.after("VE") + P.W(f"a_p{'BD'[s2]}")
                    if it == 0:
                        w = w + (("TR", trv),) + (("LC", 32),)
                    tp = (0, 64) if a else None
                    osl = slice(64 * a, 64 * a + 64)
                    P.emit("tensor", "matmul", p_av[s2][osl, 0:128],
                           s_vtok[:, ch, tl, 64 * a:64 * a + 64], s_at[:, s2, 0:128],
                           start=True, stop=False, tile_position=tp, wait=w)
                    P.emit("tensor", "matmul", p_av[s2][osl, 0:128],
                           s_pmv[0:16, 64 * h:64 * h + 64], s_at[0:16, s2, 128:256],
                           start=False, stop=True, tile_position=tp, inc=[("PE", 1)])
                    P.guards.setdefault(f"at{s2}", []).append(("PE", P.val("PE")))
                    P.guards.setdefault(f"vtok{ch}", []).append(("PE", P.val("PE")))
                    P.emit("vector", "tensor_copy",
                           s_oT[osl, ch, 128 * tl:128 * tl + 128],
                           p_av[s2][osl, 0:128],
                           wait=P.after("PE"), inc=[("VE", 1)])
                    P.R(f"a_p{'BD'[s2]}", "VE")

                s_wo = alloc(env, sub, "wobuf", [128, 2, NCH, 128], BF16)
                seed(P, ["wo0", "wo1"])
                wor = wout.rearrange("(c p) o -> p c o", p=128)
                pw = [p_A, p_C]
                ove = P.val("VE")
                for dt in range(NCH):
                    s = dt & 1
                    P.emit("gpsimd", "dma_start", out=s_wo[:, s, :, :],
                           in_=wor[:, :, 128 * dt:128 * (dt + 1)],
                           wait=P.W(f"wo{s}"), inc=[("LC", 16)])
                    lc = P.val("LC")
                    for c in range(NCH):
                        w = ()
                        if c == 0:
                            w = (("LC", lc), ("VE", ove)) + P.W(f"a_p{'ACFG'[s]}")
                        P.emit("tensor", "matmul", pw[s][:, 0:512],
                               s_wo[:, s, c, :], s_oT[:, c, :],
                               start=(c == 0), stop=(c == 15), wait=w,
                               inc=[("PE", 1)] if c == 15 else ())
                    P.guards.setdefault(f"wo{s}", []).append(("PE", P.val("PE")))
                    P.emit("vector", "tensor_tensor", env.s_xres[:, dt, hsl],
                           env.s_xres[:, dt, hsl], pw[s][:, 0:512], op=ALU.add,
                           wait=P.after("PE"), inc=[("VE", 1)])
                    P.R(f"a_p{'ACFG'[s]}", "VE")
                    if feeder is not None:
                        feeder.feed(4)

        if feeder is not None:
            feeder.flush()

    with ExitStack() as ffn:
        s_gatedT = alloc(env, ffn, "gatedT", [128, 32, TPC], BF16)
        p_A = psum(env, ffn, "f_pA")
        p_B = psum(env, ffn, "f_pB")
        p_C = psum(env, ffn, "f_pC")
        p_D = psum(env, ffn, "f_pD")
        for HH in range(2):
            hoff = 4096 * HH
            with ExitStack() as sub:
                s_graw = alloc(env, sub, "graw", [128, 2, DM], BF16)
                s_uraw = alloc(env, sub, "uraw", [128, 2, DM], BF16)
                s_gT = alloc(env, sub, "gT", [128, 2, NCH, 128], BF16)
                s_uT = alloc(env, sub, "uT", [128, 2, NCH, 128], BF16)
                s_sg = alloc(env, sub, "sg", [128, 2, 512], F32)
                seed(P, ["graw0", "graw1", "uraw0", "uraw1", "gT0", "gT1",
                         "uT0", "uT1", "sg0", "sg1"])
                pg = [p_A, p_C]
                pu = [p_B, p_D]
                for hb in range(32):
                    s = hb & 1
                    rows = slice(hoff + 128 * hb, hoff + 128 * (hb + 1))
                    P.emit("gpsimd", "dma_start", out=s_graw[:, s, :],
                           in_=wg[rows, :], wait=P.W(f"graw{s}"), inc=[("LC", 16)])
                    P.emit("gpsimd", "dma_start", out=s_uraw[:, s, :],
                           in_=wu[rows, :], wait=P.W(f"uraw{s}"), inc=[("LC", 16)])
                    lc = P.val("LC")
                    P.emit("sync", "dma_start_transpose", s_gT[:, s, :, :],
                           s_graw[:, s, :], wait=(("LC", lc),) + P.W(f"gT{s}"),
                           inc=[("TR", 16)])
                    P.guards.setdefault(f"graw{s}", []).append(("TR", P.val("TR")))
                    P.emit("sync", "dma_start_transpose", s_uT[:, s, :, :],
                           s_uraw[:, s, :], wait=P.W(f"uT{s}"), inc=[("TR", 16)])
                    P.guards.setdefault(f"uraw{s}", []).append(("TR", P.val("TR")))
                    tr = P.val("TR")
                    for mi, (sw, pp) in enumerate(((s_gT, pg), (s_uT, pu))):
                        for th in range(2):
                            tsl = slice(512 * th, 512 * th + 512)
                            for c in range(NCH):
                                w = ()
                                if c == 0:
                                    w = (("TR", tr),) + P.W(f"f_p{'ACBD'[2 * mi + th]}")
                                    if hb == 0 and mi == 0 and th == 0:
                                        w = w + P.after("VE") + P.W("gatedT")
                                P.emit("tensor", "matmul",
                                       pp[th][:, 0:512], sw[:, s, c, :],
                                       env.s_xnT[:, c, tsl],
                                       start=(c == 0), stop=(c == 15), wait=w,
                                       inc=[("PE", 1)] if c == 15 else ())
                            if mi == 0:
                                P.guards.setdefault(f"gT{s}", []).append(("PE", P.val("PE")))
                            else:
                                P.guards.setdefault(f"uT{s}", []).append(("PE", P.val("PE")))
                    for th in range(2):
                        tsl = slice(512 * th, 512 * th + 512)
                        P.emit("scalar", "activation", s_sg[:, th, :],
                               pg[th][:, 0:512], AF.Silu,
                               wait=P.after("PE") + P.W(f"sg{th}"), inc=[("AC", 1)])
                        P.R(f"f_p{'AC'[th]}", "AC")
                        P.emit("vector", "tensor_tensor",
                               s_gatedT[:, hb, tsl], s_sg[:, th, :],
                               pu[th][:, 0:512], op=ALU.mult,
                               wait=P.after("AC"), inc=[("VE", 1)])
                        P.R(f"f_p{'BD'[th]}", "VE")
                        P.guards.setdefault(f"sg{th}", []).append(("VE", P.val("VE")))
            with ExitStack() as sub:
                s_draw = alloc(env, sub, "draw", [128, 2, 4096], BF16)
                s_dT = alloc(env, sub, "dT", [128, 2, 32, 128], BF16)
                seed(P, ["draw0", "draw1", "dT0", "dT1"])
                pd = [p_A, p_C]
                gve = P.val("VE")
                for dt in range(NCH):
                    s = dt & 1
                    P.emit("gpsimd", "dma_start", out=s_draw[:, s, :],
                           in_=wd[128 * dt:128 * (dt + 1), hoff:hoff + 4096],
                           wait=P.W(f"draw{s}"), inc=[("LC", 16)])
                    lc = P.val("LC")
                    P.emit("sync", "dma_start_transpose", s_dT[:, s, :, :],
                           s_draw[:, s, :], wait=(("LC", lc),) + P.W(f"dT{s}"),
                           inc=[("TR", 16)])
                    P.guards.setdefault(f"draw{s}", []).append(("TR", P.val("TR")))
                    tr = P.val("TR")
                    for th in range(2):
                        tsl = slice(512 * th, 512 * th + 512)
                        for k in range(32):
                            w = ()
                            if k == 0:
                                w = (("TR", tr), ("VE", gve)) + P.W(f"f_p{'AC'[th]}")
                            P.emit("tensor", "matmul", pd[th][:, 0:512],
                                   s_dT[:, s, k, :], s_gatedT[:, k, tsl],
                                   start=(k == 0), stop=(k == 31), wait=w,
                                   inc=[("PE", 1)] if k == 31 else ())
                        P.emit("vector", "tensor_tensor", env.s_xres[:, dt, tsl],
                               env.s_xres[:, dt, tsl], pd[th][:, 0:512], op=ALU.add,
                               wait=P.after("PE"), inc=[("VE", 1)])
                        P.R(f"f_p{'AC'[th]}", "VE")
                    P.guards.setdefault(f"dT{s}", []).append(("PE", P.val("PE")))
                P.R("gatedT", "PE")



def plan_block1(nc, ctx):
    xAT = nc.dram_tensor("xAT", [DM, TPC], F32, kind="ExternalInput").ap()
    xBT = nc.dram_tensor("xBT", [DM, 640], F32, kind="ExternalInput").ap()
    Wq = nc.dram_tensor("Wq", [DM, DM], F32, kind="ExternalInput").ap()
    Wk = nc.dram_tensor("Wk", [DM, 256], F32, kind="ExternalInput").ap()
    Wv = nc.dram_tensor("Wv", [DM, 256], F32, kind="ExternalInput").ap()
    Wo = nc.dram_tensor("Wo", [DM, DM], F32, kind="ExternalInput").ap()
    bqp = nc.dram_tensor("bq_pp", [128, 16], F32, kind="ExternalInput").ap()
    bkp = nc.dram_tensor("bk_pp", [128, 2], F32, kind="ExternalInput").ap()
    bvp = nc.dram_tensor("bv_pp", [128, 2], F32, kind="ExternalInput").ap()
    bop = nc.dram_tensor("bo_pp", [128, 16], F32, kind="ExternalInput").ap()
    lut = nc.dram_tensor("lut_t", [128, 16], F32, kind="ExternalInput").ap()
    gidx = nc.dram_tensor("gate_idx", [5504, DM], I32, kind="ExternalInput").ap()
    uidx = nc.dram_tensor("up_idx", [5504, DM], I32, kind="ExternalInput").ap()
    didx = nc.dram_tensor("down_idx", [DM, 5504], I32, kind="ExternalInput").ap()
    iden = nc.dram_tensor("ident", [128, 128], F32, kind="ExternalInput").ap()
    oT = nc.dram_tensor("outT", [DM, TPC], F32, kind="ExternalOutput").ap()

    env = Env()
    env.nc = nc
    env.ctx = ctx
    env.P = P = Plan()

    env.s_xres = alloc(env, env.ctx, "xres", [128, NCH, TPC], F32)
    env.s_xnT = alloc(env, env.ctx, "xnTA", [128, NCH, TPC], BF16)
    env.ones_b = alloc(env, env.ctx, "ones_b", [128, 1], BF16)
    env.ones_f = alloc(env, env.ctx, "ones_f", [1, 128], F32)
    s_bq = alloc(env, env.ctx, "bq", [128, 16], F32)
    s_bk = alloc(env, env.ctx, "bk", [128, 2], F32)
    s_bv = alloc(env, env.ctx, "bv", [128, 2], F32)
    s_bo = alloc(env, env.ctx, "bo", [128, 16], F32)
    s_lut = alloc(env, env.ctx, "lut", [128, 16], F32)
    s_id = alloc(env, env.ctx, "id", [128, 128], BF16)

    xar = xAT.rearrange("(c p) t -> p c t", p=128)
    P.emit("scalar", "dma_start", out=env.s_xres[:, :, :], in_=xar, inc=[("LD", 16)])
    for dst, src in ((s_bq, bqp), (s_bk, bkp), (s_bv, bvp), (s_bo, bop), (s_lut, lut)):
        P.emit("scalar", "dma_start", out=dst[:, :], in_=src, inc=[("LD", 16)])
    P.emit("gpsimd", "dma_start", out=s_id[:, :], in_=iden, inc=[("LC", 16)])
    P.emit("vector", "memset", env.ones_b[:, :], 1.0, inc=[("VE", 1)])
    P.emit("vector", "memset", env.ones_f[:, :], 1.0, inc=[("VE", 1)])

    s_xnB = alloc(env, env.ctx, "xnTB", [128, NCH, 640], BF16)

    with ExitStack() as sub:
        emit_ln_setup(env, sub)
        emit_ln(env, "lA", env.s_xres, env.s_xnT, TPC,
                dep=(("LD", 16),) + P.after("VE"))
        s_xB = alloc(env, sub, "xBres", [128, NCH, 640], F32)
        xbr = xBT.rearrange("(c p) t -> p c t", p=128)
        P.emit("scalar", "dma_start", out=s_xB[:, :, :], in_=xbr, inc=[("LD", 16)])
        emit_ln(env, "lB", s_xB, s_xnB, 640, dep=P.after("LD", "VE"))

    emit_block1(env, Wq, Wk, Wv, Wo, gidx, uidx, didx, s_bq, s_bk, s_bv, s_bo,
                s_lut, s_id, s_xnB, P.after("VE"))

    orr = oT.rearrange("(c p) t -> p c t", p=128)
    P.emit("sync", "dma_start", out=orr, in_=env.s_xres[:, :, :],
           wait=P.after("VE", "AC", "PE"), inc=[("ST", 16)])
    P.emit("sync", "nop", wait=P.after("ST"))
    return env


def emit_block1(env, Wq, Wk, Wv, Wo, gidx, uidx, didx, s_bq, s_bk, s_bv, s_bo,
                s_lut, s_id, s_xnB, xnb_dep, ghost=("local",),
                wload=("gpsimd", "LC", ())):
    P = env.P
    weng, wsem, wdep = wload
    with ExitStack() as attn:
        s_qT = alloc(env, attn, "qT1", [128, NCH, 640], BF16)
        s_kT = alloc(env, attn, "kT1", [128, 2, 640], BF16)
        s_vtok = alloc(env, attn, "vtok1", [128, 2, 5, 128], BF16)
        s_oT = alloc(env, attn, "oT1", [128, NCH, TPC], BF16)
        seed(P, ["vtok0", "vtok1"])
        p_A = psum(env, attn, "a_pA")
        p_B = psum(env, attn, "a_pB")
        p_C = psum(env, attn, "a_pC")
        p_D = psum(env, attn, "a_pD")
        p_E0 = psum(env, attn, "a_pE0", BF16)
        p_E1 = psum(env, attn, "a_pE1", BF16)
        p_F = psum(env, attn, "a_pF")
        p_G = psum(env, attn, "a_pG")

        with ExitStack() as sub:
            s_wb = alloc(env, sub, "wbuf1", [128, 2, NCH, 128], BF16)
            s_vT = alloc(env, sub, "vT1", [128, 2, 640], BF16)
            seed(P, ["w10", "w11"])
            wqr = Wq.rearrange("(c p) o -> p c o", p=128)
            wkr = Wk.rearrange("(c p) o -> p c o", p=128)
            wvr = Wv.rearrange("(c p) o -> p c o", p=128)
            pq = [p_A, p_C]
            NT = ((0, 512), (512, 128))
            for ob in range(20):
                s = ob & 1
                if ob < 16:
                    src = wqr[:, :, 128 * ob:128 * (ob + 1)]
                elif ob < 18:
                    g = ob - 16
                    src = wkr[:, :, 128 * g:128 * (g + 1)]
                else:
                    g = ob - 18
                    src = wvr[:, :, 128 * g:128 * (g + 1)]
                P.emit(weng, "dma_start", out=s_wb[:, s, :, :], in_=src,
                       wait=P.W(f"w1{s}") + (wdep if ob < 2 else ()),
                       inc=[(wsem, 16)])
                lc = P.val(wsem)
                for nt, (t0, tw) in enumerate(NT):
                    for c in range(NCH):
                        w = ()
                        if c == 0:
                            w = ((wsem, lc),) + P.W(f"a_p{'ACFG'[s]}")
                            if ob == 0 and nt == 0:
                                w = w + P.after("VE") + xnb_dep
                        P.emit("tensor", "matmul", pq[s][:, 0:tw],
                               s_wb[:, s, c, :], s_xnB[:, c, t0:t0 + tw],
                               start=(c == 0), stop=(c == 15), wait=w,
                               inc=[("PE", 1)] if c == 15 else ())
                    if ob < 16:
                        dst, bias = s_qT[:, ob, t0:t0 + tw], s_bq[:, ob:ob + 1]
                    elif ob < 18:
                        dst, bias = s_kT[:, ob - 16, t0:t0 + tw], s_bk[:, ob - 16:ob - 15]
                    else:
                        dst, bias = s_vT[:, ob - 18, t0:t0 + tw], s_bv[:, ob - 18:ob - 17]
                    P.emit("vector", "tensor_scalar", dst, pq[s][:, 0:tw],
                           bias, None, op0=ALU.add,
                           wait=P.after("PE"), inc=[("VE", 1)])
                    P.R(f"a_p{'ACFG'[s]}", "VE")
                P.guards.setdefault(f"w1{s}", []).append(("PE", P.val("PE")))
            ve = P.val("VE")
            for g in range(2):
                P.emit("sync", "dma_start_transpose", s_vtok[:, g, :, :],
                       s_vT[:, g, :], wait=(("VE", ve),) + P.W(f"vtok{g}"),
                       inc=[("TR", 16)])

        with ExitStack() as sub:
            s_nm = alloc(env, sub, "nm", [128, 4], F32)
            seed(P, [f"nm{i}" for i in range(4)] + [f"au{i}" for i in range(4)]
                 + [f"an{i}" for i in range(4)] + ["at0", "at1"])
            s_bias = alloc(env, sub, "sb", [128, 4], F32)
            s_au = alloc(env, sub, "au", [128, 4, 256], BF16)
            s_rs = alloc(env, sub, "rs", [128, 4], F32)
            s_rn = alloc(env, sub, "rn", [128, 4], F32)
            s_an = alloc(env, sub, "an", [128, 4, 256], BF16)
            s_at = alloc(env, sub, "at", [128, 2, 256], BF16)
            p_sc = [p_A, p_C, p_F, p_G]
            p_at = [p_E0, p_E1]
            p_av = [p_B, p_D]
            trv = P.val("TR")
            it = 0
            for wdw in range(4):
                for h in range(16):
                    for qt in range(2):
                        s = it & 3
                        s2 = it & 1
                        g = h & 1
                        w = P.W(f"a_p{'ACFG'[s]}")
                        if it == 0:
                            w = w + P.after("VE") + (("TR", trv),)
                        P.emit("tensor", "matmul", p_sc[s][:, 0:256],
                               s_qT[:, h, 128 * (wdw + qt):128 * (wdw + qt) + 128],
                               s_kT[:, g, 128 * wdw:128 * wdw + 256],
                               start=True, stop=True, wait=w, inc=[("PE", 1)])
                        P.emit("vector", "tensor_reduce", s_nm[:, s:s + 1],
                               p_sc[s][:, 0:256], axis=AX.X, op=ALU.max, negate=True,
                               wait=P.after("PE") + P.W(f"nm{s}"))
                        P.emit("vector", "tensor_scalar", s_bias[:, s:s + 1],
                               s_nm[:, s:s + 1], SCALE1, None, op0=ALU.mult,
                               inc=[("VE", 1)])
                        P.emit("scalar", "activation", s_au[:, s, :],
                               p_sc[s][:, 0:256], AF.Exp, scale=SCALE1,
                               bias=s_bias[:, s:s + 1], accum_out=s_rs[:, s:s + 1],
                               wait=P.after("VE") + P.W(f"au{s}"), inc=[("AC", 1)])
                        P.R(f"a_p{'ACFG'[s]}", "AC")
                        P.guards.setdefault(f"nm{s}", []).append(("AC", P.val("AC")))
                        P.emit("vector", "reciprocal", s_rn[:, s:s + 1],
                               s_rs[:, s:s + 1], wait=P.after("AC"))
                        P.emit("vector", "tensor_scalar", s_an[:, s, :],
                               s_au[:, s, :], s_rn[:, s:s + 1], None, op0=ALU.mult,
                               wait=P.W(f"an{s}"), inc=[("VE", 1)])
                        P.guards.setdefault(f"au{s}", []).append(("VE", P.val("VE")))
                        w = P.after("VE") + P.W(f"a_pE{s2}")
                        P.emit("tensor", "transpose", p_at[s2][:, 0:128],
                               s_an[:, s, 0:128], s_id[:, :], wait=w)
                        P.emit("tensor", "transpose", p_at[s2][:, 128:256],
                               s_an[:, s, 128:256], s_id[:, :], inc=[("PE", 1)])
                        P.guards.setdefault(f"an{s}", []).append(("PE", P.val("PE")))
                        P.emit("vector", "tensor_copy", s_at[:, s2, :],
                               p_at[s2][:, 0:256],
                               wait=P.after("PE") + P.W(f"at{s2}"), inc=[("VE", 1)])
                        P.R(f"a_pE{s2}", "VE")
                        w = P.after("VE") + P.W(f"a_p{'BD'[s2]}")
                        for kc in range(2):
                            P.emit("tensor", "matmul", p_av[s2][:, 0:128],
                                   s_vtok[:, g, wdw + kc, :],
                                   s_at[:, s2, 128 * kc:128 * kc + 128],
                                   start=(kc == 0), stop=(kc == 1),
                                   wait=w if kc == 0 else (),
                                   inc=[("PE", 1)] if kc == 1 else ())
                        P.guards.setdefault(f"at{s2}", []).append(("PE", P.val("PE")))
                        P.guards.setdefault(f"vtok{g}", []).append(("PE", P.val("PE")))
                        P.emit("vector", "tensor_copy",
                               s_oT[:, h, 256 * wdw + 128 * qt:256 * wdw + 128 * qt + 128],
                               p_av[s2][:, 0:128], wait=P.after("PE"), inc=[("VE", 1)])
                        P.R(f"a_p{'BD'[s2]}", "VE")
                        it += 1

            s_wo = alloc(env, sub, "wobuf1", [128, 2, NCH, 128], BF16)
            seed(P, ["wo0", "wo1"])
            wor = Wo.rearrange("(c p) o -> p c o", p=128)
            pw = [p_A, p_C]
            ove = P.val("VE")
            for dt in range(NCH):
                s = dt & 1
                P.emit(weng, "dma_start", out=s_wo[:, s, :, :],
                       in_=wor[:, :, 128 * dt:128 * (dt + 1)],
                       wait=P.W(f"wo{s}"), inc=[(wsem, 16)])
                lc = P.val(wsem)
                for th in range(2):
                    tsl = slice(512 * th, 512 * th + 512)
                    for c in range(NCH):
                        w = ()
                        if c == 0:
                            w = ((wsem, lc), ("VE", ove)) + P.W(f"a_p{'ACFG'[s]}")
                        P.emit("tensor", "matmul", pw[s][:, 0:512],
                               s_wo[:, s, c, :], s_oT[:, c, tsl],
                               start=(c == 0), stop=(c == 15), wait=w,
                               inc=[("PE", 1)] if c == 15 else ())
                    P.emit("vector", "scalar_tensor_tensor", env.s_xres[:, dt, tsl],
                           pw[s][:, 0:512], s_bo[:, dt:dt + 1], env.s_xres[:, dt, tsl],
                           op0=ALU.add, op1=ALU.add,
                           wait=P.after("PE"), inc=[("VE", 1)])
                    P.R(f"a_p{'ACFG'[s]}", "VE")
                P.guards.setdefault(f"wo{s}", []).append(("PE", P.val("PE")))

    with ExitStack() as ffn:
        s_gatedT = alloc(env, ffn, "gatedT1", [128, 22, TPC], BF16)
        p_A = psum(env, ffn, "f_pA")
        p_B = psum(env, ffn, "f_pB")
        p_C = psum(env, ffn, "f_pC")
        p_D = psum(env, ffn, "f_pD")
        OH_N = (22, 21)
        OH_O = (0, 2816)
        for OH in range(2):
            nblk = OH_N[OH]
            obase = OH_O[OH]
            with ExitStack() as sub:
                if ghost[0] != "cc":
                    s_raw = alloc(env, sub, "giraw", [128, DM], BF16)
                    s_iT = alloc(env, sub, "giT", [128, NCH, 128], BF16)
                    s_term = alloc(env, sub, "term", [128, DM], BF16)
                s_wbt = alloc(env, sub, "wbuilt", [128, 2, 2, NCH, 128], BF16)
                s_sg = alloc(env, sub, "sg1", [128, 2, 512], F32)
                seed(P, ["giraw", "giT", "wbuilt00", "wbuilt01", "wbuilt10",
                         "wbuilt11", "term", "sg0", "sg1"])
                pg = [p_A, p_C]
                pu = [p_B, p_D]
                for ob in range(nblk):
                    rows = slice(obase + 128 * ob, obase + 128 * (ob + 1))
                    for mi, idxs in enumerate((gidx, uidx)):
                        s = ob & 1
                        if ghost[0] == "cc":
                            pos = GB_POS[obase // 128 + ob]
                            r6, i6 = divmod(pos, 6)
                            row0 = r6 * 1536 + (6 * mi + i6) * 128
                            wflat = s_wbt[:, s, mi, :, :].rearrange("p c o -> p (c o)")
                            P.emit("scalar", "dma_start", out=wflat,
                                   in_=ghost[1][row0:row0 + 128, :],
                                   wait=P.W(f"wbuilt{s}{mi}") +
                                        (ghost[3] if (ob == 0 and mi == 0 and OH == 0) else ()),
                                   inc=[("LD", 16)])
                            vb = ("LD", P.val("LD"))
                        else:
                            P.emit("gpsimd", "dma_start", out=s_raw[:, :],
                                   in_=idxs[rows, :], wait=P.W("giraw"), inc=[("LC", 16)])
                            lc = P.val("LC")
                            P.emit("sync", "dma_start_transpose", s_iT[:, :, :],
                                   s_raw[:, :], wait=(("LC", lc),) + P.W("giT"),
                                   inc=[("TR", 16)])
                            P.R("giraw", "TR")
                            tr = P.val("TR")
                            wflat = s_wbt[:, s, mi, :, :].rearrange("p c o -> p (c o)")
                            P.emit("vector", "memset", wflat, 0.0,
                                   wait=(("TR", tr),) + P.W(f"wbuilt{s}{mi}") + P.W("term"))
                            iflat = s_iT[:, :, :].rearrange("p c o -> p (c o)")
                            for j in range(16):
                                P.emit("vector", "tensor_scalar", s_term[:, :], iflat,
                                       float(j), s_lut[:, j:j + 1],
                                       op0=ALU.is_equal, op1=ALU.mult,
                                       wait=(("LD", P.val("LD")),) if (ob == 0 and mi == 0 and j == 0 and OH == 0) else ())
                                P.emit("vector", "tensor_tensor", wflat, wflat,
                                       s_term[:, :], op=ALU.add,
                                       inc=[("VE", 1)] if j == 15 else ())
                            P.R("giT", "VE")
                            vb = ("VE", P.val("VE"))
                        pp = pg if mi == 0 else pu
                        for th in range(2):
                            tsl = slice(512 * th, 512 * th + 512)
                            for c in range(NCH):
                                w = ()
                                if c == 0:
                                    w = (vb,) + P.W(f"f_p{'ACBD'[2 * mi + th]}")
                                P.emit("tensor", "matmul", pp[th][:, 0:512],
                                       s_wbt[:, s, mi, c, :], env.s_xnT[:, c, tsl],
                                       start=(c == 0), stop=(c == 15), wait=w,
                                       inc=[("PE", 1)] if c == 15 else ())
                        P.guards.setdefault(f"wbuilt{s}{mi}", []).append(("PE", P.val("PE")))
                        if ghost[0] != "cc":
                            P.guards.setdefault("term", []).append(vb)
                    for th in range(2):
                        tsl = slice(512 * th, 512 * th + 512)
                        P.emit("scalar", "activation", s_sg[:, th, :],
                               pg[th][:, 0:512], AF.Silu,
                               wait=P.after("PE") + P.W(f"sg{th}"), inc=[("AC", 1)])
                        P.R(f"f_p{'AC'[th]}", "AC")
                        P.emit("vector", "tensor_tensor", s_gatedT[:, ob, tsl],
                               s_sg[:, th, :], pu[th][:, 0:512], op=ALU.mult,
                               wait=P.after("AC"), inc=[("VE", 1)])
                        P.R(f"f_p{'BD'[th]}", "VE")
                        P.guards.setdefault(f"sg{th}", []).append(("VE", P.val("VE")))
            with ExitStack() as sub:
                if ghost[0] != "cc":
                    s_draw = alloc(env, sub, "diraw", [128, nblk * 128], BF16)
                    s_diT = alloc(env, sub, "diT", [128, nblk, 128], BF16)
                    s_term = alloc(env, sub, "term2", [128, nblk * 128], BF16)
                s_wdb = alloc(env, sub, "wdb", [128, 2, 22, 128], BF16)
                seed(P, ["diraw", "diT", "wdb0", "wdb1", "term2"])
                pd = [p_A, p_C]
                gve = P.val("VE")
                for dt in range(NCH):
                    s = dt & 1
                    if ghost[0] == "cc":
                        wflat = s_wdb[:, s, 0:nblk, :].rearrange("p c o -> p (c o)")
                        P.emit("scalar", "dma_start", out=wflat,
                               in_=ghost[2][128 * dt:128 * (dt + 1),
                                            obase:obase + nblk * 128],
                               wait=P.W(f"wdb{s}"), inc=[("LD", 16)])
                        vbw = ("LD", P.val("LD"))
                    else:
                        P.emit("gpsimd", "dma_start", out=s_draw[:, :],
                               in_=didx[128 * dt:128 * (dt + 1), obase:obase + nblk * 128],
                               wait=P.W("diraw"), inc=[("LC", 16)])
                        lc = P.val("LC")
                        P.emit("sync", "dma_start_transpose", s_diT[:, :, :],
                               s_draw[:, :], wait=(("LC", lc),) + P.W("diT"),
                               inc=[("TR", 16)])
                        P.R("diraw", "TR")
                        tr = P.val("TR")
                        wflat = s_wdb[:, s, 0:nblk, :].rearrange("p c o -> p (c o)")
                        P.emit("vector", "memset", wflat, 0.0,
                               wait=(("TR", tr),) + P.W(f"wdb{s}") + P.W("term2"))
                        iflat = s_diT[:, :, :].rearrange("p c o -> p (c o)")
                        for j in range(16):
                            P.emit("vector", "tensor_scalar", s_term[:, :], iflat,
                                   float(j), s_lut[:, j:j + 1],
                                   op0=ALU.is_equal, op1=ALU.mult)
                            P.emit("vector", "tensor_tensor", wflat, wflat,
                                   s_term[:, :], op=ALU.add,
                                   inc=[("VE", 1)] if j == 15 else ())
                        P.R("diT", "VE")
                        vbw = ("VE", P.val("VE"))
                    for th in range(2):
                        tsl = slice(512 * th, 512 * th + 512)
                        for k in range(nblk):
                            w = ()
                            if k == 0:
                                w = (vbw, ("VE", gve)) + P.W(f"f_p{'AC'[th]}")
                            P.emit("tensor", "matmul", pd[th][:, 0:512],
                                   s_wdb[:, s, k, :], s_gatedT[:, k, tsl],
                                   start=(k == 0), stop=(k == nblk - 1), wait=w,
                                   inc=[("PE", 1)] if k == nblk - 1 else ())
                        P.emit("vector", "tensor_tensor", env.s_xres[:, dt, tsl],
                               env.s_xres[:, dt, tsl], pd[th][:, 0:512], op=ALU.add,
                               wait=P.after("PE"), inc=[("VE", 1)])
                        P.R(f"f_p{'AC'[th]}", "VE")
                    P.guards.setdefault(f"wdb{s}", []).append(("PE", P.val("PE")))
                    if ghost[0] != "cc":
                        P.guards.setdefault("term2", []).append(vbw)



def plan_fused(nc, ctx):
    xT = nc.dram_tensor("xT", [DM, TPC], F32, kind="ExternalInput").ap()
    wqkv = nc.dram_tensor("w_qkv", [DM, 3 * DM], F32, kind="ExternalInput").ap()
    wout = nc.dram_tensor("w_out", [DM, DM], F32, kind="ExternalInput").ap()
    wnpp = nc.dram_tensor("w_norm_pp", [128, 16], F32, kind="ExternalInput").ap()
    pmk = nc.dram_tensor("pmkT", [128, 256], F32, kind="ExternalInput").ap()
    pmv = nc.dram_tensor("pmvT", [16, DM], F32, kind="ExternalInput").ap()
    wg = nc.dram_tensor("w_gate", [8192, DM], F32, kind="ExternalInput").ap()
    wu = nc.dram_tensor("w_up", [8192, DM], F32, kind="ExternalInput").ap()
    wd = nc.dram_tensor("w_down", [DM, 8192], F32, kind="ExternalInput").ap()
    mask = nc.dram_tensor("mask0", [128, 144], F32, kind="ExternalInput").ap()
    iden = nc.dram_tensor("ident", [128, 128], F32, kind="ExternalInput").ap()
    Wq = nc.dram_tensor("Wq", [DM, DM], F32, kind="ExternalInput").ap()
    Wk = nc.dram_tensor("Wk", [DM, 256], F32, kind="ExternalInput").ap()
    Wv = nc.dram_tensor("Wv", [DM, 256], F32, kind="ExternalInput").ap()
    Wo = nc.dram_tensor("Wo", [DM, DM], F32, kind="ExternalInput").ap()
    bqp = nc.dram_tensor("bq_pp", [128, 16], F32, kind="ExternalInput").ap()
    bkp = nc.dram_tensor("bk_pp", [128, 2], F32, kind="ExternalInput").ap()
    bvp = nc.dram_tensor("bv_pp", [128, 2], F32, kind="ExternalInput").ap()
    bop = nc.dram_tensor("bo_pp", [128, 16], F32, kind="ExternalInput").ap()
    lut = nc.dram_tensor("lut_t", [128, 16], F32, kind="ExternalInput").ap()
    g_my = nc.dram_tensor("g_my", [768, DM], I32, kind="ExternalInput").ap()
    u_my = nc.dram_tensor("u_my", [768, DM], I32, kind="ExternalInput").ap()
    d_my = nc.dram_tensor("d_my", [256, 5504], I32, kind="ExternalInput").ap()
    boff = nc.dram_tensor("boff", [1, 10], I32, kind="ExternalInput").ap()
    oT = nc.dram_tensor("outT", [DM, TPC], F32, kind="ExternalOutput").ap()
    cc_in = nc.dram_tensor("cc_in", [DM, TPC], BF16)
    cc_out = nc.dram_tensor("cc_out", [4 * DM, TPC], BF16)
    cc_gu_in = nc.dram_tensor("cc_gu_in", [1536, DM], BF16)
    cc_gu_out = nc.dram_tensor("cc_gu_out", [8 * 1536, DM], BF16)
    cc_d_in = nc.dram_tensor("cc_d_in", [256, 5504], BF16)
    cc_d_out = nc.dram_tensor("cc_d_out", [2048, 5504], BF16)
    Wq_b = nc.dram_tensor("Wq_b", [DM, DM], BF16)
    Wk_b = nc.dram_tensor("Wk_b", [DM, 256], BF16)
    Wv_b = nc.dram_tensor("Wv_b", [DM, 256], BF16)
    Wo_b = nc.dram_tensor("Wo_b", [DM, DM], BF16)

    env = Env()
    env.nc = nc
    env.ctx = ctx
    env.P = P = Plan()

    env.s_xres = alloc(env, env.ctx, "xres", [128, NCH, TPC], F32)
    env.s_xnT = alloc(env, env.ctx, "xnT", [128, NCH, TPC], BF16)
    env.ones_b = alloc(env, env.ctx, "ones_b", [128, 1], BF16)
    env.ones_f = alloc(env, env.ctx, "ones_f", [1, 128], F32)
    s_wn = alloc(env, env.ctx, "wn", [128, 16], F32)
    s_mask = alloc(env, env.ctx, "mask", [128, 144], F32)
    s_pmk = alloc(env, env.ctx, "pmk", [128, 256], BF16)
    s_pmv = alloc(env, env.ctx, "pmv", [16, DM], BF16)
    s_id = alloc(env, env.ctx, "id", [128, 128], BF16)
    s_bq = alloc(env, env.ctx, "bq", [128, 16], F32)
    s_bk = alloc(env, env.ctx, "bk", [128, 2], F32)
    s_bv = alloc(env, env.ctx, "bv", [128, 2], F32)
    s_bo = alloc(env, env.ctx, "bo", [128, 16], F32)
    s_lut = alloc(env, env.ctx, "lut", [128, 16], F32)
    s_off = alloc(env, env.ctx, "boffs", [1, 10], I32)

    xr = xT.rearrange("(c p) t -> p c t", p=128)
    P.emit("scalar", "dma_start", out=env.s_xres[:, :, :], in_=xr, inc=[("LD", 16)])
    P.emit("scalar", "dma_start", out=s_wn[:, :], in_=wnpp, inc=[("LD", 16)])
    P.emit("scalar", "dma_start", out=s_mask[:, :], in_=mask, inc=[("LD", 16)])
    for dst, srcap in ((s_bq, bqp), (s_bk, bkp), (s_bv, bvp), (s_bo, bop),
                       (s_lut, lut), (s_off, boff)):
        P.emit("scalar", "dma_start", out=dst[:, :], in_=srcap, inc=[("LD", 16)])
    P.emit("gpsimd", "dma_start", out=s_pmk[:, :], in_=pmk, inc=[("LC", 16)])
    P.emit("gpsimd", "dma_start", out=s_pmv[:, :], in_=pmv, inc=[("LC", 16)])
    P.emit("gpsimd", "dma_start", out=s_id[:, :], in_=iden, inc=[("LC", 16)])
    P.emit("vector", "memset", env.ones_b[:, :], 1.0, inc=[("VE", 1)])
    P.emit("vector", "memset", env.ones_f[:, :], 1.0, inc=[("VE", 1)])

    for dst, srcap in ((Wq_b, Wq), (Wk_b, Wk), (Wv_b, Wv), (Wo_b, Wo)):
        P.emit("gpsimd", "dma_start", out=dst.ap(), in_=srcap, inc=[("PC", 16)])
    pcdep = P.after("PC")

    feeder = GhostFeeder(env, g_my, u_my, d_my, cc_gu_in.ap(), cc_d_in.ap(), s_lut)
    emit_block0(env, wqkv, wout, wg, wu, wd, s_wn, s_mask, s_pmk, s_pmv, s_id,
                feeder=feeder)
    assert feeder.done

    cw = P.val("CW")
    P.emit("gpsimd", "collective_compute", "AllGather", ALU.bypass,
           replica_groups=[list(range(8))],
           ins=[cc_gu_in.ap().opt()], outs=[cc_gu_out.ap().opt()],
           wait=(("CW", cw),), inc=[("CD", 1)])
    P.emit("gpsimd", "collective_compute", "AllGather", ALU.bypass,
           replica_groups=[list(range(8))],
           ins=[cc_d_in.ap().opt()], outs=[cc_d_out.ap().opt()],
           inc=[("CD", 1)])
    ccdep = P.after("CD")

    with ExitStack() as late:
        s_xnB = alloc(env, late, "xnTB", [128, NCH, 640], BF16)

        with ExitStack() as sub:
            emit_ln_setup(env, sub)
            emit_ln(env, "l1", env.s_xres, env.s_xnT, TPC, dep=P.after("VE"))

        P.emit("scalar", "dma_start",
               out=cc_in.rearrange("(c p) t -> p c t", p=128),
               in_=env.s_xnT[:, :, :], wait=P.after("VE"), inc=[("CC", 16)])
        P.emit("gpsimd", "collective_compute", "AllGather", ALU.bypass,
               replica_groups=[[0, 1, 2, 3], [4, 5, 6, 7]],
               ins=[cc_in.ap().opt()], outs=[cc_out.ap().opt()],
               wait=P.after("CC"), inc=[("CC", 1)])
        ccv = P.val("CC")

        def _bgather(eng, sems):
            for j in range(5):
                rv_row = eng.value_load(s_off[0:1, 2 * j:2 * j + 1])
                rv_col = eng.value_load(s_off[0:1, 2 * j + 1:2 * j + 2])
                src_ap = cc_out[bass.ds(rv_row, DM), bass.ds(rv_col, 128)]
                eng.dma_start(
                    out=s_xnB[:, :, 128 * j:128 * (j + 1)],
                    in_=src_ap.rearrange("(c p) t -> p c t", p=128)
                ).then_inc(sems["LC"], 16)

        P.emit("gpsimd", _bgather,
               wait=(("CC", ccv),) + P.after("LD", "PE"), inc=[("LC", 80)])

        emit_block1(env, Wq_b.ap(), Wk_b.ap(), Wv_b.ap(), Wo_b.ap(), None, None,
                    None, s_bq, s_bk, s_bv, s_bo, s_lut, s_id, s_xnB,
                    P.after("LC"),
                    ghost=("cc", cc_gu_out.ap(), cc_d_out.ap(), ccdep),
                    wload=("scalar", "LD", pcdep))

    orr = oT.rearrange("(c p) t -> p c t", p=128)
    P.emit("sync", "dma_start", out=orr, in_=env.s_xres[:, :, :],
           wait=P.after("VE", "AC", "PE"), inc=[("ST", 16)])
    P.emit("sync", "nop", wait=P.after("ST"))
    return env



def _build_mask0():
    m = np.full((128, 144), NEG, np.float32)
    for q in range(128):
        sseg, j = q // 32, q % 32
        for c in range(128):
            if c // 32 == sseg and (c % 32) <= j - 16:
                m[q, c] = 0.0
        for p in range(16):
            if p <= j:
                m[q, 128 + p] = 0.0
    return m


def _pp(v, n):
    return np.ascontiguousarray(np.asarray(v, np.float32).reshape(n, 128).T)


def kernel(**inp):
    x = np.asarray(inp["L0_attn_seq"], np.float32)
    pm = np.asarray(inp["L0_attn_pm"], np.float32)
    pmkT = np.ascontiguousarray(
        pm[0].reshape(16, 2, 16, 64).transpose(1, 3, 0, 2).reshape(128, 256))
    pmvT = np.ascontiguousarray(pm[1].transpose(1, 0, 2).reshape(16, 2048))
    wn_pp = _pp(inp["L0_attn_w_norm"], 16)
    mask0 = _build_mask0()
    ident = np.eye(128, dtype=np.float32)
    lut_t = np.ascontiguousarray(
        np.tile(np.asarray(inp["L1_ffn_lut"], np.float32)[None, :], (128, 1)))
    gidx_np = np.asarray(inp["L1_ffn_gate_idx"], np.int32)
    uidx_np = np.asarray(inp["L1_ffn_up_idx"], np.int32)
    didx_np = np.asarray(inp["L1_ffn_down_idx"], np.int32)

    with ExitStack() as ctx:
        nc = bass.Bass("TRN2", target_bir_lowering=False)
        plan = plan_fused(nc, ctx)
        playback(nc, ctx, plan.P)
        in_maps = []
        for core in range(NCORE):
            b, m = divmod(core, 4)
            offs = []
            for j in range(5):
                t = 512 * m + 128 * j
                offs += [2048 * (t // 1024), t % 1024]
            in_maps.append({
                "xT": np.ascontiguousarray(x[b, 1024 * m:1024 * (m + 1), :].T),
                "w_qkv": inp["L0_attn_w_qkv"],
                "w_out": inp["L0_attn_w_out"],
                "w_norm_pp": wn_pp,
                "pmkT": pmkT,
                "pmvT": pmvT,
                "w_gate": inp["L0_ffn_w_gate"],
                "w_up": inp["L0_ffn_w_up"],
                "w_down": inp["L0_ffn_w_down"],
                "mask0": mask0,
                "ident": ident,
                "Wq": inp["L1_attn_Wq"],
                "Wk": inp["L1_attn_Wk"],
                "Wv": inp["L1_attn_Wv"],
                "Wo": inp["L1_attn_Wo"],
                "bq_pp": _pp(inp["L1_attn_bq"], 16),
                "bk_pp": _pp(inp["L1_attn_bk"], 2),
                "bv_pp": _pp(inp["L1_attn_bv"], 2),
                "bo_pp": _pp(inp["L1_attn_bo"], 16),
                "lut_t": lut_t,
                "g_my": np.ascontiguousarray(
                    gidx_np[128 * GB_BASE[core]:128 * (GB_BASE[core] + 6), :]),
                "u_my": np.ascontiguousarray(
                    uidx_np[128 * GB_BASE[core]:128 * (GB_BASE[core] + 6), :]),
                "d_my": np.ascontiguousarray(didx_np[256 * core:256 * (core + 1), :]),
                "boff": np.asarray([offs], np.int32),
            })
        res = run_bass_kernel_spmd(nc, in_maps, core_ids=list(range(NCORE)))

    outf = [np.concatenate([res.results[4 * b + m]["outT"] for m in range(4)], axis=1)
            for b in range(2)]
    out = np.stack([outf[0].T, outf[1].T], axis=0)
    return np.ascontiguousarray(out).astype(np.float32)


# version 06
# speedup vs baseline: 1.5382x; 1.1459x over previous
import dataclasses
from contextlib import ExitStack

import numpy as np

import concourse.bass as bass
import concourse.mybir as mybir
from concourse.bass_utils import run_bass_kernel_spmd

F32 = mybir.dt.float32
BF16 = mybir.dt.bfloat16
I32 = mybir.dt.int32
AF = mybir.ActivationFunctionType
ALU = mybir.AluOpType
AX = mybir.AxisListType

NCORE = 8
DM = 2048
NCH = 16
TPC = 1024
EPS_LN = 1e-5
EPS_RMS = float(np.finfo(np.float32).eps)
NEG = -1.0e30
SCALE0 = 0.125
SCALE1 = float(1.0 / np.sqrt(128.0))

ENGS = ("tensor", "vector", "scalar", "gpsimd", "sync")
DEBUG0 = False
TRUNC0 = False
TRUNC1 = False
TRUNC2 = False
XNROVR = False
NFLUSH = 0


def bc0(ap, axis, n):
    newap = [list(x) for x in ap.ap]
    assert newap[axis][1] == 1, (newap, axis)
    newap[axis] = [0, n]
    return dataclasses.replace(ap, ap=newap)


class Plan:
    def __init__(self):
        self.ops = {e: [] for e in ENGS}
        self.cnt = {}
        self.guards = {}

    def val(self, s):
        return self.cnt.get(s, 0)

    muted = False

    def emit(self, eng, meth, *args, wait=(), inc=(), **kw):
        if self.muted:
            return
        for s, v in wait:
            assert v <= self.val(s), (eng, meth, s, v, self.val(s))
        for s, n in inc:
            self.cnt[s] = self.val(s) + n
        self.ops[eng].append((meth, args, kw, tuple(wait), tuple(inc)))

    def W(self, name):
        return tuple(self.guards.pop(name, ()))

    def R(self, name, sem):
        g = self.guards.setdefault(name, [])
        g.append((sem, self.val(sem)))

    def after(self, *sems):
        return tuple((s, self.val(s)) for s in sems)


def playback(nc, ctx, plan):
    sems = {name: ctx.enter_context(nc.semaphore(name)) for name in plan.cnt}
    block = ctx.enter_context(nc.Block())
    for eng_name in ("sync", "gpsimd", "scalar", "vector", "tensor"):
        ops = plan.ops[eng_name]
        if not ops:
            continue
        dec = getattr(block, eng_name)

        def make(ops_):
            def body(eng):
                for meth, args, kw, waits, incs in ops_:
                    for s, v in waits:
                        if v > 0:
                            eng.wait_ge(sems[s], v)
                    if callable(meth):
                        meth(eng, sems)
                        continue
                    ins = getattr(eng, meth)(*args, **kw)
                    for s, n in incs:
                        ins.then_inc(sems[s], n)
            return body
        dec(make(ops))


def seed(P, names):
    f = P.after("PE", "VE", "AC", "LD", "LC", "TR", "ST")
    for n in names:
        P.guards.setdefault(n, []).extend(f)


def tiles(total, size):
    out = []
    o = 0
    while o < total:
        out.append((o, min(size, total - o)))
        o += size
    return out



class Env:
    dbg = None
    half = 0


def alloc(env, ctx, name, shape, dt):
    env.seq = getattr(env, "seq", 0) + 1
    return ctx.enter_context(env.nc.sbuf_tensor(f"{name}_{env.seq}", shape, dt))


def psum(env, ctx, name, dt=F32):
    env.seq = getattr(env, "seq", 0) + 1
    shape = [128, 512] if dt == F32 else [128, 1024]
    return ctx.enter_context(env.nc.psum_tensor(f"{name}_{env.seq}", shape, dt))


def emit_ln_setup(env, sub):
    env.s_xb = alloc(env, sub, "ln_xb", [128, 2, 512], BF16)
    env.s_xq = alloc(env, sub, "ln_xq", [128, 2, 512], BF16)
    env.st_m = alloc(env, sub, "ln_m", [1, 512], F32)
    env.st_q = alloc(env, sub, "ln_q", [1, 512], F32)
    env.st_t = alloc(env, sub, "ln_t", [1, 512], F32)
    env.st_t2 = alloc(env, sub, "ln_t2", [1, 512], F32)
    env.st_A = alloc(env, sub, "ln_A", [1, 512], F32)
    env.st_B = alloc(env, sub, "ln_B", [1, 512], F32)
    env.s_t1 = alloc(env, sub, "ln_t1", [128, 512], F32)
    env.p_s = psum(env, sub, "ln_ps")
    env.p_q = psum(env, sub, "ln_pq")
    env.p_A = psum(env, sub, "ln_pA")
    env.p_B = psum(env, sub, "ln_pB")


def emit_ln(env, tag, x3, xn3, T, dep):
    P = env.P
    for t0, tw in tiles(T, 512):
        sl = slice(t0, t0 + tw)
        for c in range(NCH):
            s = c & 1
            P.emit("vector", "tensor_copy", env.s_xb[:, s, 0:tw], x3[:, c, sl],
                   wait=P.W(f"{tag}xb{s}") + (dep if (t0 == 0 and c == 0) else ()),
                   inc=[("VE", 1)])
            P.emit("scalar", "activation", env.s_xq[:, s, 0:tw], env.s_xb[:, s, 0:tw],
                   AF.Square, wait=P.after("VE") + P.W(f"{tag}xq{s}"), inc=[("AC", 1)])
            w = P.after("AC")
            if c == 0:
                w = w + P.W(f"{tag}ps") + P.W(f"{tag}pq")
            P.emit("tensor", "matmul", env.p_s[0:1, 0:tw], env.ones_b[:, :],
                   env.s_xb[:, s, 0:tw], start=(c == 0), stop=(c == 15),
                   skip_group_check=True, wait=w)
            P.emit("tensor", "matmul", env.p_q[0:1, 0:tw], env.ones_b[:, :],
                   env.s_xq[:, s, 0:tw], start=(c == 0), stop=(c == 15),
                   skip_group_check=True, inc=[("PE", 1)])
            P.guards.setdefault(f"{tag}xb{s}", []).append(("PE", P.val("PE")))
            P.guards.setdefault(f"{tag}xq{s}", []).append(("PE", P.val("PE")))
        P.emit("vector", "tensor_scalar", env.st_m[:, 0:tw], env.p_s[0:1, 0:tw],
               1.0 / DM, None, op0=ALU.mult, wait=P.after("PE"))
        P.emit("vector", "tensor_scalar", env.st_q[:, 0:tw], env.p_q[0:1, 0:tw],
               1.0 / DM, EPS_LN, op0=ALU.mult, op1=ALU.add)
        P.emit("vector", "tensor_tensor", env.st_t[:, 0:tw], env.st_m[:, 0:tw],
               env.st_m[:, 0:tw], op=ALU.mult)
        P.emit("vector", "tensor_tensor", env.st_t2[:, 0:tw], env.st_q[:, 0:tw],
               env.st_t[:, 0:tw], op=ALU.subtract, inc=[("VE", 1)])
        P.R(f"{tag}ps", "VE")
        P.R(f"{tag}pq", "VE")
        P.emit("scalar", "activation", env.st_t[:, 0:tw], env.st_t2[:, 0:tw],
               AF.Ln, wait=P.after("VE"))
        P.emit("scalar", "activation", env.st_A[:, 0:tw], env.st_t[:, 0:tw],
               AF.Exp, scale=-0.5, inc=[("AC", 1)],
               wait=P.W(f"{tag}stA"))
        P.emit("vector", "scalar_tensor_tensor", env.st_B[:, 0:tw], env.st_m[:, 0:tw],
               -1.0, env.st_A[:, 0:tw], op0=ALU.mult, op1=ALU.mult,
               wait=P.after("AC") + P.W(f"{tag}stB"), inc=[("VE", 1)])
        P.emit("tensor", "matmul", env.p_A[:, 0:tw], env.ones_f[:, :],
               env.st_A[:, 0:tw], start=True, stop=True,
               wait=P.after("VE") + P.W(f"{tag}pA"))
        P.emit("tensor", "matmul", env.p_B[:, 0:tw], env.ones_f[:, :],
               env.st_B[:, 0:tw], start=True, stop=True,
               wait=P.W(f"{tag}pB"), inc=[("PE", 1)])
        P.R(f"{tag}stA", "PE")
        P.R(f"{tag}stB", "PE")
        for c in range(NCH):
            P.emit("vector", "tensor_tensor", env.s_t1[:, 0:tw], x3[:, c, sl],
                   env.p_A[:, 0:tw], op=ALU.mult,
                   wait=P.after("PE") if c == 0 else ())
            P.emit("vector", "tensor_tensor", xn3[:, c, sl], env.s_t1[:, 0:tw],
                   env.p_B[:, 0:tw], op=ALU.add,
                   inc=[("VE", 1)] if c == 15 else ())
        P.R(f"{tag}pA", "VE")
        P.R(f"{tag}pB", "VE")




GB_BASE = [0, 6, 12, 18, 24, 30, 36, 37]
GB_POS = {ob: ob for ob in range(42)}
GB_POS[42] = 47


class GhostFeeder:

    def __init__(self, env, g_my, u_my, d_my, cc_gu_in, cc_d_in, s_lut):
        self.env = env
        self.cc_gu_in, self.cc_d_in = cc_gu_in, cc_d_in
        self.s_lut = s_lut
        self.pieces = []
        for mi, srcs in enumerate((g_my, u_my)):
            for i in range(6):
                self.pieces.append(dict(
                    src=srcs[128 * i:128 * (i + 1), :], wid=DM, nblk=NCH,
                    dst=cc_gu_in[128 * (6 * mi + i):128 * (6 * mi + i + 1), :]))
        for j in range(2):
            for cc, (c0, w) in enumerate(((0, 2048), (2048, 2048), (4096, 1408))):
                self.pieces.append(dict(
                    src=d_my[128 * j:128 * (j + 1), c0:c0 + w], wid=w,
                    nblk=w // 128,
                    dst=cc_d_in[128 * j:128 * (j + 1), c0:c0 + w]))
        self.pi = 0
        self.js = 0
        self.staged = -1
        self.done = False

    def setup(self, ctx):
        env = self.env
        self.s_raw = alloc(env, ctx, "gb_raw", [128, 1, DM], BF16)
        self.s_iT = alloc(env, ctx, "gb_iT", [128, 2, NCH, 128], BF16)
        self.s_bld = alloc(env, ctx, "gb_bld", [128, NCH, 128], BF16)
        self.s_term = alloc(env, ctx, "gb_term", [128, DM], BF16)
        seed(env.P, ["gbraw0", "gbraw1", "gbiT0", "gbiT1", "gbbld", "gbterm"])
        env.P.guards.setdefault("gbterm", []).append(("LD", env.P.val("LD")))

    def _stage(self, k):
        if k >= len(self.pieces):
            return
        P = self.env.P
        pc = self.pieces[k]
        s = k & 1
        P.emit("gpsimd", "dma_start", out=self.s_raw[:, 0, 0:pc["wid"]],
               in_=pc["src"], wait=P.W("gbraw0"), inc=[("LC", 16)])
        lc = P.val("LC")
        P.emit("sync", "dma_start_transpose", self.s_iT[:, s, 0:pc["nblk"], :],
               self.s_raw[:, 0, 0:pc["wid"]],
               wait=(("LC", lc),) + P.W(f"gbiT{s}"), inc=[("TR", 16)])
        P.R("gbraw0", "TR")
        self.staged = k

    def feed(self, n):
        if self.done:
            return
        P = self.env.P
        while n > 0 and not self.done:
            if self.staged < self.pi:
                self._stage(self.pi)
                if self.staged < self.pi:
                    self.done = True
                    return
            pc = self.pieces[self.pi]
            s = self.pi & 1
            wid, nblk = pc["wid"], pc["nblk"]
            iflat = self.s_iT[:, s, 0:nblk, :].rearrange("p c o -> p (c o)")
            bflat = self.s_bld[:, 0:nblk, :].rearrange("p c o -> p (c o)")
            if self.js == 0:
                tr = P.val("TR")
                P.emit("vector", "memset", bflat, 0.0,
                       wait=(("TR", tr),) + P.W("gbbld"))
                self._stage(self.pi + 1)
            while n > 0 and self.js < 16:
                j = self.js
                P.emit("vector", "tensor_scalar", self.s_term[:, 0:wid], iflat,
                       float(j), self.s_lut[:, j:j + 1],
                       op0=ALU.is_equal, op1=ALU.mult,
                       wait=P.W("gbterm") if j == 0 else ())
                P.emit("vector", "tensor_tensor", bflat, bflat,
                       self.s_term[:, 0:wid], op=ALU.add,
                       inc=[("VE", 1)] if j == 15 else ())
                self.js += 1
                n -= 1
            if self.js == 16:
                P.R(f"gbiT{s}", "VE")
                P.R("gbterm", "VE")
                ve = P.val("VE")
                P.emit("scalar", "dma_start", out=pc["dst"], in_=bflat,
                       wait=(("VE", ve),), inc=[("CW", 16)])
                P.R("gbbld", "CW")
                self.pi += 1
                self.js = 0
                if self.pi >= len(self.pieces):
                    self.done = True

    def flush(self):
        while not self.done:
            self.feed(16)



def plan_block0(nc, ctx):
    xT = nc.dram_tensor("xT", [DM, TPC], F32, kind="ExternalInput").ap()
    wqkv = nc.dram_tensor("w_qkv", [DM, 3 * DM], F32, kind="ExternalInput").ap()
    wout = nc.dram_tensor("w_out", [DM, DM], F32, kind="ExternalInput").ap()
    wnpp = nc.dram_tensor("w_norm_pp", [128, 16], F32, kind="ExternalInput").ap()
    pmk = nc.dram_tensor("pmkT", [128, 256], F32, kind="ExternalInput").ap()
    pmv = nc.dram_tensor("pmvT", [16, DM], F32, kind="ExternalInput").ap()
    wg = nc.dram_tensor("w_gate", [8192, DM], F32, kind="ExternalInput").ap()
    wu = nc.dram_tensor("w_up", [8192, DM], F32, kind="ExternalInput").ap()
    wd = nc.dram_tensor("w_down", [DM, 8192], F32, kind="ExternalInput").ap()
    mask = nc.dram_tensor("mask0", [128, 144], F32, kind="ExternalInput").ap()
    iden = nc.dram_tensor("ident", [128, 128], F32, kind="ExternalInput").ap()
    x1T = nc.dram_tensor("x1T", [DM, TPC], F32, kind="ExternalOutput").ap()

    env = Env()
    env.nc = nc
    env.ctx = ctx
    env.P = P = Plan()
    import os as _os
    if _os.environ.get("DBG0", "0") == "1":
        env.dbg = (
            nc.dram_tensor("dbg_an", [2, 32, 128, 256], BF16,
                           kind="ExternalOutput").ap(),
            nc.dram_tensor("dbg_at", [2, 32, 128, 256], BF16,
                           kind="ExternalOutput").ap(),
            nc.dram_tensor("dbg_rs", [2, 32, 2, 128], F32,
                           kind="ExternalOutput").ap(),
        )

    env.s_xres = alloc(env, env.ctx, "xres", [128, NCH, TPC], F32)
    env.s_xnT = alloc(env, env.ctx, "xnT", [128, NCH, TPC], BF16)
    env.ones_b = alloc(env, env.ctx, "ones_b", [128, 1], BF16)
    env.ones_f = alloc(env, env.ctx, "ones_f", [1, 128], F32)
    s_wn = alloc(env, env.ctx, "wn", [128, 16], F32)
    s_mask = alloc(env, env.ctx, "mask", [128, 144], F32)
    s_pmk = alloc(env, env.ctx, "pmk", [128, 256], BF16)
    s_pmv = alloc(env, env.ctx, "pmv", [16, DM], BF16)
    s_id = alloc(env, env.ctx, "id", [128, 128], BF16)

    xr = xT.rearrange("(c p) t -> p c t", p=128)
    P.emit("scalar", "dma_start", out=env.s_xres[:, :, :], in_=xr, inc=[("LD", 16)])
    P.emit("scalar", "dma_start", out=s_wn[:, :], in_=wnpp, inc=[("LD", 16)])
    P.emit("scalar", "dma_start", out=s_mask[:, :], in_=mask, inc=[("LD", 16)])
    P.emit("gpsimd", "dma_start", out=s_pmk[:, :], in_=pmk, inc=[("LC", 16)])
    P.emit("gpsimd", "dma_start", out=s_pmv[:, :], in_=pmv, inc=[("LC", 16)])
    P.emit("gpsimd", "dma_start", out=s_id[:, :], in_=iden, inc=[("LC", 16)])
    P.emit("vector", "memset", env.ones_b[:, :], 1.0, inc=[("VE", 1)])
    P.emit("vector", "memset", env.ones_f[:, :], 1.0, inc=[("VE", 1)])

    emit_block0(env, wqkv, wout, wg, wu, wd, s_wn, s_mask, s_pmk, s_pmv, s_id)

    x1r = x1T.rearrange("(c p) t -> p c t", p=128)
    P.emit("sync", "dma_start", out=x1r, in_=env.s_xres[:, :, :],
           wait=P.after("VE", "AC", "PE"), inc=[("ST", 16)])
    P.emit("sync", "nop", wait=P.after("ST"))
    return env


def emit_block0(env, wqkv, wout, wg, wu, wd, s_wn, s_mask, s_pmk, s_pmv, s_id,
                feeder=None, post_attn=None):
    P = env.P
    with ExitStack() as sub:
        emit_ln_setup(env, sub)
        emit_ln(env, "l0", env.s_xres, env.s_xnT, TPC,
                dep=(("LD", 16),) + P.after("VE"))

    with ExitStack() as attn:
        s_qT = alloc(env, attn, "qT0", [128, NCH, 512], BF16)
        s_kT = alloc(env, attn, "kT0", [128, NCH, 512], BF16)
        s_vtok = alloc(env, attn, "vtok0", [128, NCH, 4, 128], BF16)
        seed(P, [f"vtok{c}" for c in range(NCH)])
        if feeder is not None:
            feeder.setup(attn)
        p_A = psum(env, attn, "a_pA")
        p_B = psum(env, attn, "a_pB")
        p_C = psum(env, attn, "a_pC")
        p_D = psum(env, attn, "a_pD")
        p_E0 = psum(env, attn, "a_pE0", BF16)
        p_E1 = psum(env, attn, "a_pE1", BF16)
        p_F = psum(env, attn, "a_pF")
        p_G = psum(env, attn, "a_pG")
        st_r = alloc(env, attn, "st_r", [1, 512], F32)

        for half in range(2):
            env.half = half
            hsl = slice(512 * half, 512 * half + 512)
            halfstk = env.ctx.enter_context(ExitStack()) if False else ExitStack()
            halfstk.__enter__()
            s_xnr = alloc(env, halfstk, "xnr", [128, NCH, 512], BF16)
            with ExitStack() as sub:
                s_sq = alloc(env, sub, "rms_sq", [128, 2, 512], BF16)
                seed(P, ["rsq0", "rsq1"])
                for c in range(NCH):
                    s = c & 1
                    P.emit("scalar", "activation", s_sq[:, s, :], env.s_xnT[:, c, hsl],
                           AF.Square, wait=(P.after("VE") if c == 0 else ()) + P.W(f"rsq{s}"),
                           inc=[("AC", 1)])
                    P.emit("tensor", "matmul", p_F[0:1, 0:512], env.ones_b[:, :],
                           s_sq[:, s, :], start=(c == 0), stop=(c == 15),
                           wait=P.after("AC") + (P.W("a_pF") if c == 0 else ()),
                           inc=[("PE", 1)] if c == 15 else ())
                    P.guards.setdefault(f"rsq{s}", []).append(("PE", P.val("PE")))
                P.emit("vector", "tensor_scalar", st_r[:, 0:512], p_F[0:1, 0:512],
                       1.0 / DM, EPS_RMS, op0=ALU.mult, op1=ALU.add,
                       wait=P.after("PE") + P.W("st_r"), inc=[("VE", 1)])
                P.R("a_pF", "VE")
                P.emit("scalar", "activation", st_r[:, 0:512], st_r[:, 0:512],
                       AF.Ln, wait=P.after("VE"))
                P.emit("scalar", "activation", st_r[:, 0:512], st_r[:, 0:512],
                       AF.Exp, scale=-0.5, inc=[("AC", 1)])
                P.emit("tensor", "matmul", p_B[:, 0:512], env.ones_f[:, :],
                       st_r[:, 0:512], start=True, stop=True,
                       wait=P.after("AC") + P.W("a_pB"), inc=[("PE", 1)])
                P.R("st_r", "PE")
                for c in range(NCH):
                    P.emit("vector", "scalar_tensor_tensor", s_xnr[:, c, :],
                           env.s_xnT[:, c, hsl], s_wn[:, c:c + 1], p_B[:, 0:512],
                           op0=ALU.mult, op1=ALU.mult,
                           wait=(P.after("PE") + (("LD", 32),) + P.W("xnr")) if c == 0 else (),
                           inc=[("VE", 1)] if c == 15 else ())
                P.R("a_pB", "VE")

            with ExitStack() as sub:
                s_wq = alloc(env, sub, "wqbuf", [128, 2, NCH, 128], BF16)
                s_vT = alloc(env, sub, "vT0", [128, 2, 512], BF16)
                seed(P, ["wq0", "wq1", "vTs0", "vTs1"])
                wqr = wqkv.rearrange("(c p) o -> p c o", p=128)
                pq = [p_A, p_C]
                for ob in range(48):
                    s = ob & 1
                    P.emit("gpsimd", "dma_start", out=s_wq[:, s, :, :],
                           in_=wqr[:, :, 128 * ob:128 * (ob + 1)],
                           wait=P.W(f"wq{s}"), inc=[("LC", 16)])
                    lc = P.val("LC")
                    for c in range(NCH):
                        w = ()
                        if c == 0:
                            w = (("LC", lc),) + P.W(f"a_p{'ACFG'[s]}") + \
                                (P.after("VE", "AC", "PE", "LD", "TR") if ob == 0 else ())
                        P.emit("tensor", "matmul", pq[s][:, 0:512],
                               s_wq[:, s, c, :], s_xnr[:, c, :],
                               start=(c == 0), stop=(c == 15), wait=w,
                               inc=[("PE", 1)] if c == 15 else ())
                    P.guards.setdefault(f"wq{s}", []).append(("PE", P.val("PE")))
                    if ob < 16:
                        P.emit("vector", "tensor_scalar", s_qT[:, ob, :],
                               pq[s][:, 0:512], SCALE0, None, op0=ALU.mult,
                               wait=P.after("PE"), inc=[("VE", 1)])
                    elif ob < 32:
                        P.emit("vector", "tensor_copy", s_kT[:, ob - 16, :],
                               pq[s][:, 0:512], wait=P.after("PE"), inc=[("VE", 1)])
                    else:
                        vs = (ob - 32) & 1
                        P.emit("vector", "tensor_copy", s_vT[:, vs, :],
                               pq[s][:, 0:512],
                               wait=P.after("PE") + P.W(f"vTs{vs}"), inc=[("VE", 1)])
                        c_v = ob - 32
                        P.emit("sync", "dma_start_transpose",
                               s_vtok[:, c_v, :, :], s_vT[:, vs, :],
                               wait=P.after("VE") + P.W(f"vtok{c_v}"),
                               inc=[("TR", 16)])
                        P.guards.setdefault(f"vTs{vs}", []).append(("TR", P.val("TR")))
                    P.R(f"a_p{'ACFG'[s]}", "VE")
                    if feeder is not None:
                        feeder.feed(3)
                P.R("xnr", "PE")
            halfstk.__exit__(None, None, None)

            with ExitStack() as sub:
                s_sm = alloc(env, sub, "sm", [128, 4, 144], F32)
                seed(P, [f"sm{i}" for i in range(4)] + [f"au{i}" for i in range(4)]
                     + [f"an{i}" for i in range(4)] + ["at0", "at1"])
                s_nm = alloc(env, sub, "nm", [128, 4], F32)
                s_au = alloc(env, sub, "au", [128, 4, 144], BF16)
                s_rs = alloc(env, sub, "rs", [128, 4], F32)
                s_rn = alloc(env, sub, "rn", [128, 4], F32)
                s_an = alloc(env, sub, "an", [128, 4, 256], BF16)
                s_at = alloc(env, sub, "at", [128, 2, 2, 128], BF16)
                s_oT = alloc(env, sub, "oT0", [128, NCH, 512], BF16)
                p_sc = [p_A, p_C, p_F, p_G]
                p_at = [p_E0, p_E1]
                p_av = [p_B, p_D]
                trv = P.val("TR")
                iters = [(h, t) for h in range(32) for t in range(4)]
                NIT = len(iters)
                st = {}

                def hp(it):
                    h, tl = iters[it]
                    return h, tl, h & 1, h >> 1

                def stage_scores(it):
                    h, tl, a, ch = hp(it)
                    s = it & 3
                    qsl = (slice(64 * a, 64 * a + 64), ch,
                           slice(128 * tl, 128 * tl + 128))
                    w = P.W(f"a_p{'ACFG'[s]}")
                    if it == 0:
                        w = w + P.after("VE") + (("LC", 48),) + (("LD", 48),)
                    P.emit("tensor", "matmul", p_sc[s][:, 0:128],
                           s_qT[qsl], s_kT[qsl], start=True, stop=False, wait=w)
                    P.emit("tensor", "matmul", p_sc[s][:, 128:144],
                           s_qT[qsl], s_pmk[64 * a:64 * a + 64, 16 * ch:16 * ch + 16],
                           start=False, stop=True, inc=[("PE", 1)])
                    st[("S", it)] = P.val("PE")

                def stage_sm1(it):
                    s = it & 3
                    P.emit("vector", "tensor_tensor", s_sm[:, s, :],
                           p_sc[s][:, 0:144], s_mask[:, :], op=ALU.add,
                           wait=(("PE", st[("S", it)]),) + P.W(f"sm{s}"))
                    P.emit("vector", "tensor_reduce", s_nm[:, s:s + 1],
                           s_sm[:, s, :], axis=AX.X, op=ALU.max, negate=True,
                           inc=[("VE", 1)])
                    P.guards.setdefault(f"a_p{'ACFG'[s]}", []).append(("VE", P.val("VE")))
                    st[("A", it)] = P.val("VE")

                def stage_exp(it):
                    s = it & 3
                    P.emit("scalar", "activation", s_au[:, s, :], s_sm[:, s, :],
                           AF.Exp, bias=s_nm[:, s:s + 1], accum_out=s_rs[:, s:s + 1],
                           wait=(("VE", st[("A", it)]),) + P.W(f"au{s}"),
                           inc=[("AC", 1)])
                    P.guards.setdefault(f"sm{s}", []).append(("AC", P.val("AC")))
                    st[("B", it)] = P.val("AC")

                def stage_normA(it):
                    s = it & 3
                    P.emit("vector", "reciprocal", s_rn[:, s:s + 1], s_rs[:, s:s + 1],
                           wait=(("AC", st[("B", it)]),) + P.W(f"rn{s}"),
                           inc=[("VE", 1)])
                    st[("R", it)] = P.val("VE")
                    if env.dbg is not None and it % 4 == 1:
                        h = iters[it][0]
                        P.emit("sync", "dma_start", out=env.dbg[2][env.half, h, 0],
                               in_=s_rs[:, s:s + 1],
                               wait=(("AC", st[("B", it)]),), inc=[("TR", 16)])
                        P.guards.setdefault(f"au{s}", []).append(("TR", P.val("TR")))
                        P.emit("sync", "dma_start", out=env.dbg[2][env.half, h, 1],
                               in_=s_rn[:, s:s + 1],
                               wait=(("VE", st[("R", it)]),), inc=[("TR", 16)])
                        P.guards.setdefault(f"rn{s}", []).append(("TR", P.val("TR")))

                def stage_normB(it):
                    s = it & 3
                    P.emit("vector", "tensor_scalar", s_an[:, s, 0:144], s_au[:, s, :],
                           s_rn[:, s:s + 1], None, op0=ALU.mult,
                           wait=P.W(f"an{s}"), inc=[("VE", 1)])
                    P.guards.setdefault(f"au{s}", []).append(("VE", P.val("VE")))
                    st[("C", it)] = P.val("VE")
                    if env.dbg is not None and it % 4 == 1:
                        h = iters[it][0]
                        P.emit("sync", "dma_start", out=env.dbg[0][env.half, h],
                               in_=s_an[:, s, :],
                               wait=(("VE", st[("C", it)]),), inc=[("TR", 16)])
                        P.guards.setdefault(f"an{s}", []).append(("TR", P.val("TR")))

                def stage_norm(it):
                    stage_normA(it)
                    stage_normB(it)

                def stage_T(it):
                    s = it & 3
                    s2 = it & 1
                    w = (("VE", st[("C", it)]),) + P.W(f"a_pE{s2}")
                    P.emit("tensor", "transpose", p_at[s2][:, 0:128],
                           s_an[:, s, 0:128], s_id[:, :], wait=w)
                    P.emit("tensor", "transpose", p_at[s2][0:16, 128:256],
                           s_an[:, s, 128:144], s_id[:, :], inc=[("PE", 1)])
                    P.guards.setdefault(f"an{s}", []).append(("PE", P.val("PE")))
                    P.emit("vector", "tensor_copy",
                           s_at[:, s2, :, :].rearrange("p a b -> p (a b)"),
                           p_at[s2][:, 0:256],
                           wait=(("PE", P.val("PE")),) + P.W(f"at{s2}"),
                           inc=[("VE", 1)])
                    P.R(f"a_pE{s2}", "VE")
                    st[("E", it)] = ("VE", P.val("VE"))
                    if env.dbg is not None and it % 4 == 1:
                        h = iters[it][0]
                        P.emit("sync", "dma_start",
                               out=env.dbg[1][env.half, h],
                               in_=s_at[:, s2, :, :].rearrange("p a b -> p (a b)"),
                               wait=(("VE", P.val("VE")),), inc=[("TR", 16)])
                        P.guards.setdefault(f"at{s2}", []).append(("TR", P.val("TR")))

                def stage_av(it):
                    h, tl, a, ch = hp(it)
                    s2 = it & 1
                    w = (st[("E", it)],) + P.W(f"a_p{'BD'[s2]}")
                    if it == 0:
                        w = w + (("LC", 32),)
                    tp = (0, 64) if a else None
                    osl = slice(64 * a, 64 * a + 64)
                    P.emit("tensor", "matmul", p_av[s2][osl, 0:128],
                           s_vtok[:, ch, tl, 64 * a:64 * a + 64], s_at[:, s2, 0, :],
                           start=True, stop=False, tile_position=tp, wait=w)
                    P.emit("tensor", "matmul", p_av[s2][osl, 0:128],
                           s_pmv[0:16, 64 * h:64 * h + 64], s_at[0:16, s2, 1, :],
                           start=False, stop=True, tile_position=tp, inc=[("PE", 1)])
                    P.guards.setdefault(f"at{s2}", []).append(("PE", P.val("PE")))
                    P.guards.setdefault(f"vtok{ch}", []).append(("PE", P.val("PE")))
                    st[("F", it)] = P.val("PE")

                def stage_oT(it):
                    h, tl, a, ch = hp(it)
                    s2 = it & 1
                    osl = slice(64 * a, 64 * a + 64)
                    P.emit("vector", "tensor_copy",
                           s_oT[osl, ch, 128 * tl:128 * tl + 128],
                           p_av[s2][osl, 0:128],
                           wait=(("PE", st[("F", it)]),), inc=[("VE", 1)])
                    P.R(f"a_p{'BD'[s2]}", "VE")

                _o = (0, 0, 1, 2, 2)
                for step in range(NIT + max(_o)):
                    if 0 <= step - _o[0] < NIT:
                        stage_scores(step - _o[0])
                        stage_sm1(step - _o[0])
                    if 0 <= step - _o[1] < NIT:
                        stage_exp(step - _o[1])
                    if 0 <= step - _o[2] < NIT:
                        stage_normA(step - _o[2])
                    if 0 <= step - _o[3] < NIT:
                        stage_normB(step - _o[3])
                    if 0 <= step - _o[4] < NIT:
                        t = step - _o[4]
                        stage_T(t)
                        stage_av(t)
                        stage_oT(t)

                if env.dbg is not None:
                    P.emit("vector", "tensor_copy", s_nm[:, 0:1], s_nm[:, 0:1],
                           wait=(("TR", P.val("TR")),), inc=[("VE", 1)])

                s_wo = alloc(env, sub, "wobuf", [128, 2, NCH, 128], BF16)
                seed(P, ["wo0", "wo1"])
                wor = wout.rearrange("(c p) o -> p c o", p=128)
                pw = [p_A, p_C]
                ove = P.val("VE")
                for dt in range(NCH):
                    s = dt & 1
                    P.emit("gpsimd", "dma_start", out=s_wo[:, s, :, :],
                           in_=wor[:, :, 128 * dt:128 * (dt + 1)],
                           wait=P.W(f"wo{s}"), inc=[("LC", 16)])
                    lc = P.val("LC")
                    for c in range(NCH):
                        w = ()
                        if c == 0:
                            w = (("LC", lc), ("VE", ove)) + P.W(f"a_p{'ACFG'[s]}")
                        P.emit("tensor", "matmul", pw[s][:, 0:512],
                               s_wo[:, s, c, :], s_oT[:, c, :],
                               start=(c == 0), stop=(c == 15), wait=w,
                               inc=[("PE", 1)] if c == 15 else ())
                    P.guards.setdefault(f"wo{s}", []).append(("PE", P.val("PE")))
                    P.emit("vector", "tensor_tensor", env.s_xres[:, dt, hsl],
                           env.s_xres[:, dt, hsl], pw[s][:, 0:512], op=ALU.add,
                           wait=P.after("PE"), inc=[("VE", 1)])
                    P.R(f"a_p{'ACFG'[s]}", "VE")
                    if feeder is not None:
                        feeder.feed(4)

        if feeder is not None:
            feeder.flush()

    if post_attn is not None:
        post_attn()

    with ExitStack() as ffn:
        s_gatedT = alloc(env, ffn, "gatedT", [128, 32, TPC], BF16)
        p_A = psum(env, ffn, "f_pA")
        p_B = psum(env, ffn, "f_pB")
        p_C = psum(env, ffn, "f_pC")
        p_D = psum(env, ffn, "f_pD")
        for HH in range(2):
            hoff = 4096 * HH
            with ExitStack() as sub:
                s_graw = alloc(env, sub, "graw", [128, 2, DM], BF16)
                s_uraw = alloc(env, sub, "uraw", [128, 2, DM], BF16)
                s_gT = alloc(env, sub, "gT", [128, 2, NCH, 128], BF16)
                s_uT = alloc(env, sub, "uT", [128, 2, NCH, 128], BF16)
                s_sg = alloc(env, sub, "sg", [128, 2, 512], F32)
                seed(P, ["graw0", "graw1", "uraw0", "uraw1", "gT0", "gT1",
                         "uT0", "uT1", "sg0", "sg1"])
                pg = [p_A, p_C]
                pu = [p_B, p_D]
                for hb in range(32):
                    s = hb & 1
                    rows = slice(hoff + 128 * hb, hoff + 128 * (hb + 1))
                    P.emit("gpsimd", "dma_start", out=s_graw[:, s, :],
                           in_=wg[rows, :], wait=P.W(f"graw{s}"), inc=[("LC", 16)])
                    P.emit("gpsimd", "dma_start", out=s_uraw[:, s, :],
                           in_=wu[rows, :], wait=P.W(f"uraw{s}"), inc=[("LC", 16)])
                    lc = P.val("LC")
                    P.emit("sync", "dma_start_transpose", s_gT[:, s, :, :],
                           s_graw[:, s, :], wait=(("LC", lc),) + P.W(f"gT{s}"),
                           inc=[("TR", 16)])
                    P.guards.setdefault(f"graw{s}", []).append(("TR", P.val("TR")))
                    P.emit("sync", "dma_start_transpose", s_uT[:, s, :, :],
                           s_uraw[:, s, :], wait=P.W(f"uT{s}"), inc=[("TR", 16)])
                    P.guards.setdefault(f"uraw{s}", []).append(("TR", P.val("TR")))
                    tr = P.val("TR")
                    for mi, (sw, pp) in enumerate(((s_gT, pg), (s_uT, pu))):
                        for th in range(2):
                            tsl = slice(512 * th, 512 * th + 512)
                            for c in range(NCH):
                                w = ()
                                if c == 0:
                                    w = (("TR", tr),) + P.W(f"f_p{'ACBD'[2 * mi + th]}")
                                    if hb == 0 and mi == 0 and th == 0:
                                        w = w + P.after("VE") + P.W("gatedT")
                                P.emit("tensor", "matmul",
                                       pp[th][:, 0:512], sw[:, s, c, :],
                                       env.s_xnT[:, c, tsl],
                                       start=(c == 0), stop=(c == 15), wait=w,
                                       inc=[("PE", 1)] if c == 15 else ())
                            if mi == 0:
                                P.guards.setdefault(f"gT{s}", []).append(("PE", P.val("PE")))
                            else:
                                P.guards.setdefault(f"uT{s}", []).append(("PE", P.val("PE")))
                    for th in range(2):
                        tsl = slice(512 * th, 512 * th + 512)
                        P.emit("scalar", "activation", s_sg[:, th, :],
                               pg[th][:, 0:512], AF.Silu,
                               wait=P.after("PE") + P.W(f"sg{th}"), inc=[("AC", 1)])
                        P.R(f"f_p{'AC'[th]}", "AC")
                        P.emit("vector", "tensor_tensor",
                               s_gatedT[:, hb, tsl], s_sg[:, th, :],
                               pu[th][:, 0:512], op=ALU.mult,
                               wait=P.after("AC"), inc=[("VE", 1)])
                        P.R(f"f_p{'BD'[th]}", "VE")
                        P.guards.setdefault(f"sg{th}", []).append(("VE", P.val("VE")))
            with ExitStack() as sub:
                s_draw = alloc(env, sub, "draw", [128, 2, 4096], BF16)
                s_dT = alloc(env, sub, "dT", [128, 2, 32, 128], BF16)
                seed(P, ["draw0", "draw1", "dT0", "dT1"])
                pd = [p_A, p_C]
                gve = P.val("VE")
                for dt in range(NCH):
                    s = dt & 1
                    P.emit("gpsimd", "dma_start", out=s_draw[:, s, :],
                           in_=wd[128 * dt:128 * (dt + 1), hoff:hoff + 4096],
                           wait=P.W(f"draw{s}"), inc=[("LC", 16)])
                    lc = P.val("LC")
                    P.emit("sync", "dma_start_transpose", s_dT[:, s, :, :],
                           s_draw[:, s, :], wait=(("LC", lc),) + P.W(f"dT{s}"),
                           inc=[("TR", 16)])
                    P.guards.setdefault(f"draw{s}", []).append(("TR", P.val("TR")))
                    tr = P.val("TR")
                    for th in range(2):
                        tsl = slice(512 * th, 512 * th + 512)
                        for k in range(32):
                            w = ()
                            if k == 0:
                                w = (("TR", tr), ("VE", gve)) + P.W(f"f_p{'AC'[th]}")
                            P.emit("tensor", "matmul", pd[th][:, 0:512],
                                   s_dT[:, s, k, :], s_gatedT[:, k, tsl],
                                   start=(k == 0), stop=(k == 31), wait=w,
                                   inc=[("PE", 1)] if k == 31 else ())
                        P.emit("vector", "tensor_tensor", env.s_xres[:, dt, tsl],
                               env.s_xres[:, dt, tsl], pd[th][:, 0:512], op=ALU.add,
                               wait=P.after("PE"), inc=[("VE", 1)])
                        P.R(f"f_p{'AC'[th]}", "VE")
                    P.guards.setdefault(f"dT{s}", []).append(("PE", P.val("PE")))
                P.R("gatedT", "PE")



def plan_block1(nc, ctx):
    xAT = nc.dram_tensor("xAT", [DM, TPC], F32, kind="ExternalInput").ap()
    xBT = nc.dram_tensor("xBT", [DM, 640], F32, kind="ExternalInput").ap()
    Wq = nc.dram_tensor("Wq", [DM, DM], F32, kind="ExternalInput").ap()
    Wk = nc.dram_tensor("Wk", [DM, 256], F32, kind="ExternalInput").ap()
    Wv = nc.dram_tensor("Wv", [DM, 256], F32, kind="ExternalInput").ap()
    Wo = nc.dram_tensor("Wo", [DM, DM], F32, kind="ExternalInput").ap()
    bqp = nc.dram_tensor("bq_pp", [128, 16], F32, kind="ExternalInput").ap()
    bkp = nc.dram_tensor("bk_pp", [128, 2], F32, kind="ExternalInput").ap()
    bvp = nc.dram_tensor("bv_pp", [128, 2], F32, kind="ExternalInput").ap()
    bop = nc.dram_tensor("bo_pp", [128, 16], F32, kind="ExternalInput").ap()
    lut = nc.dram_tensor("lut_t", [128, 16], F32, kind="ExternalInput").ap()
    gidx = nc.dram_tensor("gate_idx", [5504, DM], I32, kind="ExternalInput").ap()
    uidx = nc.dram_tensor("up_idx", [5504, DM], I32, kind="ExternalInput").ap()
    didx = nc.dram_tensor("down_idx", [DM, 5504], I32, kind="ExternalInput").ap()
    iden = nc.dram_tensor("ident", [128, 128], F32, kind="ExternalInput").ap()
    oT = nc.dram_tensor("outT", [DM, TPC], F32, kind="ExternalOutput").ap()

    env = Env()
    env.nc = nc
    env.ctx = ctx
    env.P = P = Plan()

    env.s_xres = alloc(env, env.ctx, "xres", [128, NCH, TPC], F32)
    env.s_xnT = alloc(env, env.ctx, "xnTA", [128, NCH, TPC], BF16)
    env.ones_b = alloc(env, env.ctx, "ones_b", [128, 1], BF16)
    env.ones_f = alloc(env, env.ctx, "ones_f", [1, 128], F32)
    s_bq = alloc(env, env.ctx, "bq", [128, 16], F32)
    s_bk = alloc(env, env.ctx, "bk", [128, 2], F32)
    s_bv = alloc(env, env.ctx, "bv", [128, 2], F32)
    s_bo = alloc(env, env.ctx, "bo", [128, 16], F32)
    s_lut = alloc(env, env.ctx, "lut", [128, 16], F32)
    s_id = alloc(env, env.ctx, "id", [128, 128], BF16)

    xar = xAT.rearrange("(c p) t -> p c t", p=128)
    P.emit("scalar", "dma_start", out=env.s_xres[:, :, :], in_=xar, inc=[("LD", 16)])
    for dst, src in ((s_bq, bqp), (s_bk, bkp), (s_bv, bvp), (s_bo, bop), (s_lut, lut)):
        P.emit("scalar", "dma_start", out=dst[:, :], in_=src, inc=[("LD", 16)])
    P.emit("gpsimd", "dma_start", out=s_id[:, :], in_=iden, inc=[("LC", 16)])
    P.emit("vector", "memset", env.ones_b[:, :], 1.0, inc=[("VE", 1)])
    P.emit("vector", "memset", env.ones_f[:, :], 1.0, inc=[("VE", 1)])

    s_xnB = alloc(env, env.ctx, "xnTB", [128, NCH, 640], BF16)

    with ExitStack() as sub:
        emit_ln_setup(env, sub)
        emit_ln(env, "lA", env.s_xres, env.s_xnT, TPC,
                dep=(("LD", 16),) + P.after("VE"))
        s_xB = alloc(env, sub, "xBres", [128, NCH, 640], F32)
        xbr = xBT.rearrange("(c p) t -> p c t", p=128)
        P.emit("scalar", "dma_start", out=s_xB[:, :, :], in_=xbr, inc=[("LD", 16)])
        emit_ln(env, "lB", s_xB, s_xnB, 640, dep=P.after("LD", "VE"))

    emit_block1(env, Wq, Wk, Wv, Wo, gidx, uidx, didx, s_bq, s_bk, s_bv, s_bo,
                s_lut, s_id, s_xnB, P.after("VE"))

    orr = oT.rearrange("(c p) t -> p c t", p=128)
    P.emit("sync", "dma_start", out=orr, in_=env.s_xres[:, :, :],
           wait=P.after("VE", "AC", "PE"), inc=[("ST", 16)])
    P.emit("sync", "nop", wait=P.after("ST"))
    return env


def emit_block1(env, Wq, Wk, Wv, Wo, gidx, uidx, didx, s_bq, s_bk, s_bv, s_bo,
                s_lut, s_id, s_xnB, xnb_dep, ghost=("local",),
                wload=("gpsimd", "LC", ())):
    P = env.P
    weng, wsem, wdep = wload
    with ExitStack() as attn:
        s_qT = alloc(env, attn, "qT1", [128, NCH, 640], BF16)
        s_kT = alloc(env, attn, "kT1", [128, 2, 640], BF16)
        s_vtok = alloc(env, attn, "vtok1", [128, 2, 5, 128], BF16)
        s_oT = alloc(env, attn, "oT1", [128, NCH, TPC], BF16)
        seed(P, ["vtok0", "vtok1"])
        p_A = psum(env, attn, "a_pA")
        p_B = psum(env, attn, "a_pB")
        p_C = psum(env, attn, "a_pC")
        p_D = psum(env, attn, "a_pD")
        p_E0 = psum(env, attn, "a_pE0", BF16)
        p_E1 = psum(env, attn, "a_pE1", BF16)
        p_F = psum(env, attn, "a_pF")
        p_G = psum(env, attn, "a_pG")

        with ExitStack() as sub:
            s_wb = alloc(env, sub, "wbuf1", [128, 2, NCH, 128], BF16)
            s_vT = alloc(env, sub, "vT1", [128, 2, 640], BF16)
            seed(P, ["w10", "w11"])
            wqr = Wq.rearrange("(c p) o -> p c o", p=128)
            wkr = Wk.rearrange("(c p) o -> p c o", p=128)
            wvr = Wv.rearrange("(c p) o -> p c o", p=128)
            pq = [p_A, p_C]
            NT = ((0, 512), (512, 128))
            for ob in range(20):
                s = ob & 1
                if ob < 16:
                    src = wqr[:, :, 128 * ob:128 * (ob + 1)]
                elif ob < 18:
                    g = ob - 16
                    src = wkr[:, :, 128 * g:128 * (g + 1)]
                else:
                    g = ob - 18
                    src = wvr[:, :, 128 * g:128 * (g + 1)]
                P.emit(weng, "dma_start", out=s_wb[:, s, :, :], in_=src,
                       wait=P.W(f"w1{s}") + (wdep if ob < 2 else ()),
                       inc=[(wsem, 16)])
                lc = P.val(wsem)
                for nt, (t0, tw) in enumerate(NT):
                    for c in range(NCH):
                        w = ()
                        if c == 0:
                            w = ((wsem, lc),) + P.W(f"a_p{'ACFG'[s]}")
                            if ob == 0 and nt == 0:
                                w = w + P.after("VE") + xnb_dep
                        P.emit("tensor", "matmul", pq[s][:, 0:tw],
                               s_wb[:, s, c, :], s_xnB[:, c, t0:t0 + tw],
                               start=(c == 0), stop=(c == 15), wait=w,
                               inc=[("PE", 1)] if c == 15 else ())
                    if ob < 16:
                        dst, bias = s_qT[:, ob, t0:t0 + tw], s_bq[:, ob:ob + 1]
                    elif ob < 18:
                        dst, bias = s_kT[:, ob - 16, t0:t0 + tw], s_bk[:, ob - 16:ob - 15]
                    else:
                        dst, bias = s_vT[:, ob - 18, t0:t0 + tw], s_bv[:, ob - 18:ob - 17]
                    P.emit("vector", "tensor_scalar", dst, pq[s][:, 0:tw],
                           bias, None, op0=ALU.add,
                           wait=P.after("PE"), inc=[("VE", 1)])
                    P.R(f"a_p{'ACFG'[s]}", "VE")
                P.guards.setdefault(f"w1{s}", []).append(("PE", P.val("PE")))
            ve = P.val("VE")
            for g in range(2):
                P.emit("sync", "dma_start_transpose", s_vtok[:, g, :, :],
                       s_vT[:, g, :], wait=(("VE", ve),) + P.W(f"vtok{g}"),
                       inc=[("TR", 16)])

        with ExitStack() as sub:
            s_nm = alloc(env, sub, "nm", [128, 4], F32)
            seed(P, [f"nm{i}" for i in range(4)] + [f"au{i}" for i in range(4)]
                 + [f"an{i}" for i in range(4)] + ["at0", "at1"])
            s_bias = alloc(env, sub, "sb", [128, 4], F32)
            s_au = alloc(env, sub, "au", [128, 4, 256], BF16)
            s_rs = alloc(env, sub, "rs", [128, 4], F32)
            s_rn = alloc(env, sub, "rn", [128, 4], F32)
            s_an = alloc(env, sub, "an", [128, 4, 256], BF16)
            s_at = alloc(env, sub, "at", [128, 2, 2, 128], BF16)
            p_sc = [p_A, p_C, p_F, p_G]
            p_at = [p_E0, p_E1]
            p_av = [p_B, p_D]
            trv = P.val("TR")
            iters = [(wdw, h, qt) for wdw in range(4) for h in range(16)
                     for qt in range(2)]
            NIT = len(iters)
            st = {}

            def stage_scores(it):
                wdw, h, qt = iters[it]
                s = it & 3
                g = h & 1
                w = P.W(f"a_p{'ACFG'[s]}")
                if it == 0:
                    w = w + P.after("VE") + (("TR", trv),)
                P.emit("tensor", "matmul", p_sc[s][:, 0:256],
                       s_qT[:, h, 128 * (wdw + qt):128 * (wdw + qt) + 128],
                       s_kT[:, g, 128 * wdw:128 * wdw + 256],
                       start=True, stop=True, wait=w, inc=[("PE", 1)])
                st[("S", it)] = P.val("PE")

            def stage_red(it):
                s = it & 3
                P.emit("vector", "tensor_reduce", s_nm[:, s:s + 1],
                       p_sc[s][:, 0:256], axis=AX.X, op=ALU.max, negate=True,
                       wait=(("PE", st[("S", it)]),) + P.W(f"nm{s}"))
                P.emit("vector", "tensor_scalar", s_bias[:, s:s + 1],
                       s_nm[:, s:s + 1], SCALE1, None, op0=ALU.mult,
                       inc=[("VE", 1)])
                st[("A", it)] = P.val("VE")

            def stage_exp(it):
                s = it & 3
                P.emit("scalar", "activation", s_au[:, s, :],
                       p_sc[s][:, 0:256], AF.Exp, scale=SCALE1,
                       bias=s_bias[:, s:s + 1], accum_out=s_rs[:, s:s + 1],
                       wait=(("VE", st[("A", it)]),) + P.W(f"au{s}"),
                       inc=[("AC", 1)])
                P.R(f"a_p{'ACFG'[s]}", "AC")
                P.guards.setdefault(f"nm{s}", []).append(("AC", P.val("AC")))
                st[("B", it)] = P.val("AC")

            def stage_normA(it):
                s = it & 3
                P.emit("vector", "reciprocal", s_rn[:, s:s + 1], s_rs[:, s:s + 1],
                       wait=(("AC", st[("B", it)]),))

            def stage_normB(it):
                s = it & 3
                P.emit("vector", "tensor_scalar", s_an[:, s, :], s_au[:, s, :],
                       s_rn[:, s:s + 1], None, op0=ALU.mult,
                       wait=P.W(f"an{s}"), inc=[("VE", 1)])
                P.guards.setdefault(f"au{s}", []).append(("VE", P.val("VE")))
                st[("C", it)] = P.val("VE")

            def stage_T(it):
                s = it & 3
                s2 = it & 1
                w = (("VE", st[("C", it)]),) + P.W(f"a_pE{s2}")
                P.emit("tensor", "transpose", p_at[s2][:, 0:128],
                       s_an[:, s, 0:128], s_id[:, :], wait=w)
                P.emit("tensor", "transpose", p_at[s2][:, 128:256],
                       s_an[:, s, 128:256], s_id[:, :], inc=[("PE", 1)])
                P.guards.setdefault(f"an{s}", []).append(("PE", P.val("PE")))
                P.emit("vector", "tensor_copy",
                       s_at[:, s2, :, :].rearrange("p a b -> p (a b)"),
                       p_at[s2][:, 0:256],
                       wait=(("PE", P.val("PE")),) + P.W(f"at{s2}"),
                       inc=[("VE", 1)])
                P.R(f"a_pE{s2}", "VE")
                st[("E", it)] = ("VE", P.val("VE"))

            def stage_av(it):
                wdw, h, qt = iters[it]
                s2 = it & 1
                g = h & 1
                w = (st[("E", it)],) + P.W(f"a_p{'BD'[s2]}")
                for kc in range(2):
                    P.emit("tensor", "matmul", p_av[s2][:, 0:128],
                           s_vtok[:, g, wdw + kc, :],
                           s_at[:, s2, kc, :],
                           start=(kc == 0), stop=(kc == 1),
                           wait=w if kc == 0 else (),
                           inc=[("PE", 1)] if kc == 1 else ())
                P.guards.setdefault(f"at{s2}", []).append(("PE", P.val("PE")))
                P.guards.setdefault(f"vtok{g}", []).append(("PE", P.val("PE")))
                st[("F", it)] = P.val("PE")

            def stage_oT(it):
                wdw, h, qt = iters[it]
                s2 = it & 1
                P.emit("vector", "tensor_copy",
                       s_oT[:, h, 256 * wdw + 128 * qt:256 * wdw + 128 * qt + 128],
                       p_av[s2][:, 0:128], wait=(("PE", st[("F", it)]),),
                       inc=[("VE", 1)])
                P.R(f"a_p{'BD'[s2]}", "VE")

            _o = (0, 1, 2, 3, 3)
            for step in range(NIT + max(_o)):
                if 0 <= step - _o[0] < NIT:
                    stage_scores(step - _o[0])
                    stage_red(step - _o[0])
                if 0 <= step - _o[1] < NIT:
                    stage_exp(step - _o[1])
                if 0 <= step - _o[2] < NIT:
                    stage_normA(step - _o[2])
                if 0 <= step - _o[3] < NIT:
                    stage_normB(step - _o[3])
                if 0 <= step - _o[4] < NIT:
                    t = step - _o[4]
                    stage_T(t)
                    stage_av(t)
                    stage_oT(t)

            s_wo = alloc(env, sub, "wobuf1", [128, 2, NCH, 128], BF16)
            seed(P, ["wo0", "wo1"])
            wor = Wo.rearrange("(c p) o -> p c o", p=128)
            pw = [p_A, p_C]
            ove = P.val("VE")
            for dt in range(NCH):
                s = dt & 1
                P.emit(weng, "dma_start", out=s_wo[:, s, :, :],
                       in_=wor[:, :, 128 * dt:128 * (dt + 1)],
                       wait=P.W(f"wo{s}"), inc=[(wsem, 16)])
                lc = P.val(wsem)
                for th in range(2):
                    tsl = slice(512 * th, 512 * th + 512)
                    for c in range(NCH):
                        w = ()
                        if c == 0:
                            w = ((wsem, lc), ("VE", ove)) + P.W(f"a_p{'ACFG'[s]}")
                        P.emit("tensor", "matmul", pw[s][:, 0:512],
                               s_wo[:, s, c, :], s_oT[:, c, tsl],
                               start=(c == 0), stop=(c == 15), wait=w,
                               inc=[("PE", 1)] if c == 15 else ())
                    P.emit("vector", "scalar_tensor_tensor", env.s_xres[:, dt, tsl],
                           pw[s][:, 0:512], s_bo[:, dt:dt + 1], env.s_xres[:, dt, tsl],
                           op0=ALU.add, op1=ALU.add,
                           wait=P.after("PE"), inc=[("VE", 1)])
                    P.R(f"a_p{'ACFG'[s]}", "VE")
                P.guards.setdefault(f"wo{s}", []).append(("PE", P.val("PE")))

    with ExitStack() as ffn:
        s_gatedT = alloc(env, ffn, "gatedT1", [128, 22, TPC], BF16)
        p_A = psum(env, ffn, "f_pA")
        p_B = psum(env, ffn, "f_pB")
        p_C = psum(env, ffn, "f_pC")
        p_D = psum(env, ffn, "f_pD")
        OH_N = (22, 21)
        OH_O = (0, 2816)
        for OH in range(2):
            nblk = OH_N[OH]
            obase = OH_O[OH]
            with ExitStack() as sub:
                if ghost[0] != "cc":
                    s_raw = alloc(env, sub, "giraw", [128, DM], BF16)
                    s_iT = alloc(env, sub, "giT", [128, NCH, 128], BF16)
                    s_term = alloc(env, sub, "term", [128, DM], BF16)
                s_wbt = alloc(env, sub, "wbuilt", [128, 2, 2, NCH, 128], BF16)
                s_sg = alloc(env, sub, "sg1", [128, 2, 512], F32)
                seed(P, ["giraw", "giT", "wbuilt00", "wbuilt01", "wbuilt10",
                         "wbuilt11", "term", "sg0", "sg1"])
                pg = [p_A, p_C]
                pu = [p_B, p_D]
                for ob in range(nblk):
                    rows = slice(obase + 128 * ob, obase + 128 * (ob + 1))
                    for mi, idxs in enumerate((gidx, uidx)):
                        s = ob & 1
                        if ghost[0] == "cc":
                            pos = GB_POS[obase // 128 + ob]
                            r6, i6 = divmod(pos, 6)
                            row0 = r6 * 1536 + (6 * mi + i6) * 128
                            wflat = s_wbt[:, s, mi, :, :].rearrange("p c o -> p (c o)")
                            P.emit("scalar", "dma_start", out=wflat,
                                   in_=ghost[1][row0:row0 + 128, :],
                                   wait=P.W(f"wbuilt{s}{mi}") +
                                        (ghost[3] if (ob == 0 and mi == 0 and OH == 0) else ()),
                                   inc=[("LD", 16)])
                            vb = ("LD", P.val("LD"))
                        else:
                            P.emit("gpsimd", "dma_start", out=s_raw[:, :],
                                   in_=idxs[rows, :], wait=P.W("giraw"), inc=[("LC", 16)])
                            lc = P.val("LC")
                            P.emit("sync", "dma_start_transpose", s_iT[:, :, :],
                                   s_raw[:, :], wait=(("LC", lc),) + P.W("giT"),
                                   inc=[("TR", 16)])
                            P.R("giraw", "TR")
                            tr = P.val("TR")
                            wflat = s_wbt[:, s, mi, :, :].rearrange("p c o -> p (c o)")
                            P.emit("vector", "memset", wflat, 0.0,
                                   wait=(("TR", tr),) + P.W(f"wbuilt{s}{mi}") + P.W("term"))
                            iflat = s_iT[:, :, :].rearrange("p c o -> p (c o)")
                            for j in range(16):
                                P.emit("vector", "tensor_scalar", s_term[:, :], iflat,
                                       float(j), s_lut[:, j:j + 1],
                                       op0=ALU.is_equal, op1=ALU.mult,
                                       wait=(("LD", P.val("LD")),) if (ob == 0 and mi == 0 and j == 0 and OH == 0) else ())
                                P.emit("vector", "tensor_tensor", wflat, wflat,
                                       s_term[:, :], op=ALU.add,
                                       inc=[("VE", 1)] if j == 15 else ())
                            P.R("giT", "VE")
                            vb = ("VE", P.val("VE"))
                        pp = pg if mi == 0 else pu
                        for th in range(2):
                            tsl = slice(512 * th, 512 * th + 512)
                            for c in range(NCH):
                                w = ()
                                if c == 0:
                                    w = (vb,) + P.W(f"f_p{'ACBD'[2 * mi + th]}")
                                P.emit("tensor", "matmul", pp[th][:, 0:512],
                                       s_wbt[:, s, mi, c, :], env.s_xnT[:, c, tsl],
                                       start=(c == 0), stop=(c == 15), wait=w,
                                       inc=[("PE", 1)] if c == 15 else ())
                        P.guards.setdefault(f"wbuilt{s}{mi}", []).append(("PE", P.val("PE")))
                        if ghost[0] != "cc":
                            P.guards.setdefault("term", []).append(vb)
                    for th in range(2):
                        tsl = slice(512 * th, 512 * th + 512)
                        P.emit("scalar", "activation", s_sg[:, th, :],
                               pg[th][:, 0:512], AF.Silu,
                               wait=P.after("PE") + P.W(f"sg{th}"), inc=[("AC", 1)])
                        P.R(f"f_p{'AC'[th]}", "AC")
                        P.emit("vector", "tensor_tensor", s_gatedT[:, ob, tsl],
                               s_sg[:, th, :], pu[th][:, 0:512], op=ALU.mult,
                               wait=P.after("AC"), inc=[("VE", 1)])
                        P.R(f"f_p{'BD'[th]}", "VE")
                        P.guards.setdefault(f"sg{th}", []).append(("VE", P.val("VE")))
            with ExitStack() as sub:
                if ghost[0] != "cc":
                    s_draw = alloc(env, sub, "diraw", [128, nblk * 128], BF16)
                    s_diT = alloc(env, sub, "diT", [128, nblk, 128], BF16)
                    s_term = alloc(env, sub, "term2", [128, nblk * 128], BF16)
                s_wdb = alloc(env, sub, "wdb", [128, 2, 22, 128], BF16)
                seed(P, ["diraw", "diT", "wdb0", "wdb1", "term2"])
                pd = [p_A, p_C]
                gve = P.val("VE")
                for dt in range(NCH):
                    s = dt & 1
                    if ghost[0] == "cc":
                        wflat = s_wdb[:, s, 0:nblk, :].rearrange("p c o -> p (c o)")
                        P.emit("scalar", "dma_start", out=wflat,
                               in_=ghost[2][128 * dt:128 * (dt + 1),
                                            obase:obase + nblk * 128],
                               wait=P.W(f"wdb{s}"), inc=[("LD", 16)])
                        vbw = ("LD", P.val("LD"))
                    else:
                        P.emit("gpsimd", "dma_start", out=s_draw[:, :],
                               in_=didx[128 * dt:128 * (dt + 1), obase:obase + nblk * 128],
                               wait=P.W("diraw"), inc=[("LC", 16)])
                        lc = P.val("LC")
                        P.emit("sync", "dma_start_transpose", s_diT[:, :, :],
                               s_draw[:, :], wait=(("LC", lc),) + P.W("diT"),
                               inc=[("TR", 16)])
                        P.R("diraw", "TR")
                        tr = P.val("TR")
                        wflat = s_wdb[:, s, 0:nblk, :].rearrange("p c o -> p (c o)")
                        P.emit("vector", "memset", wflat, 0.0,
                               wait=(("TR", tr),) + P.W(f"wdb{s}") + P.W("term2"))
                        iflat = s_diT[:, :, :].rearrange("p c o -> p (c o)")
                        for j in range(16):
                            P.emit("vector", "tensor_scalar", s_term[:, :], iflat,
                                   float(j), s_lut[:, j:j + 1],
                                   op0=ALU.is_equal, op1=ALU.mult)
                            P.emit("vector", "tensor_tensor", wflat, wflat,
                                   s_term[:, :], op=ALU.add,
                                   inc=[("VE", 1)] if j == 15 else ())
                        P.R("diT", "VE")
                        vbw = ("VE", P.val("VE"))
                    for th in range(2):
                        tsl = slice(512 * th, 512 * th + 512)
                        for k in range(nblk):
                            w = ()
                            if k == 0:
                                w = (vbw, ("VE", gve)) + P.W(f"f_p{'AC'[th]}")
                            P.emit("tensor", "matmul", pd[th][:, 0:512],
                                   s_wdb[:, s, k, :], s_gatedT[:, k, tsl],
                                   start=(k == 0), stop=(k == nblk - 1), wait=w,
                                   inc=[("PE", 1)] if k == nblk - 1 else ())
                        P.emit("vector", "tensor_tensor", env.s_xres[:, dt, tsl],
                               env.s_xres[:, dt, tsl], pd[th][:, 0:512], op=ALU.add,
                               wait=P.after("PE"), inc=[("VE", 1)])
                        P.R(f"f_p{'AC'[th]}", "VE")
                    P.guards.setdefault(f"wdb{s}", []).append(("PE", P.val("PE")))
                    if ghost[0] != "cc":
                        P.guards.setdefault("term2", []).append(vbw)



def plan_fused(nc, ctx):
    xT = nc.dram_tensor("xT", [DM, TPC], F32, kind="ExternalInput").ap()
    wqkv = nc.dram_tensor("w_qkv", [DM, 3 * DM], F32, kind="ExternalInput").ap()
    wout = nc.dram_tensor("w_out", [DM, DM], F32, kind="ExternalInput").ap()
    wnpp = nc.dram_tensor("w_norm_pp", [128, 16], F32, kind="ExternalInput").ap()
    pmk = nc.dram_tensor("pmkT", [128, 256], F32, kind="ExternalInput").ap()
    pmv = nc.dram_tensor("pmvT", [16, DM], F32, kind="ExternalInput").ap()
    wg = nc.dram_tensor("w_gate", [8192, DM], F32, kind="ExternalInput").ap()
    wu = nc.dram_tensor("w_up", [8192, DM], F32, kind="ExternalInput").ap()
    wd = nc.dram_tensor("w_down", [DM, 8192], F32, kind="ExternalInput").ap()
    mask = nc.dram_tensor("mask0", [128, 144], F32, kind="ExternalInput").ap()
    iden = nc.dram_tensor("ident", [128, 128], F32, kind="ExternalInput").ap()
    Wq = nc.dram_tensor("Wq", [DM, DM], F32, kind="ExternalInput").ap()
    Wk = nc.dram_tensor("Wk", [DM, 256], F32, kind="ExternalInput").ap()
    Wv = nc.dram_tensor("Wv", [DM, 256], F32, kind="ExternalInput").ap()
    Wo = nc.dram_tensor("Wo", [DM, DM], F32, kind="ExternalInput").ap()
    bqp = nc.dram_tensor("bq_pp", [128, 16], F32, kind="ExternalInput").ap()
    bkp = nc.dram_tensor("bk_pp", [128, 2], F32, kind="ExternalInput").ap()
    bvp = nc.dram_tensor("bv_pp", [128, 2], F32, kind="ExternalInput").ap()
    bop = nc.dram_tensor("bo_pp", [128, 16], F32, kind="ExternalInput").ap()
    lut = nc.dram_tensor("lut_t", [128, 16], F32, kind="ExternalInput").ap()
    g_my = nc.dram_tensor("g_my", [768, DM], I32, kind="ExternalInput").ap()
    u_my = nc.dram_tensor("u_my", [768, DM], I32, kind="ExternalInput").ap()
    d_my = nc.dram_tensor("d_my", [256, 5504], I32, kind="ExternalInput").ap()
    boff = nc.dram_tensor("boff", [1, 10], I32, kind="ExternalInput").ap()
    oT = nc.dram_tensor("outT", [DM, TPC], F32, kind="ExternalOutput").ap()
    cc_in = nc.dram_tensor("cc_in", [DM, TPC], BF16)
    cc_out = nc.dram_tensor("cc_out", [4 * DM, TPC], BF16)
    cc_gu_in = nc.dram_tensor("cc_gu_in", [1536, DM], BF16)
    cc_gu_out = nc.dram_tensor("cc_gu_out", [8 * 1536, DM], BF16)
    cc_d_in = nc.dram_tensor("cc_d_in", [256, 5504], BF16)
    cc_d_out = nc.dram_tensor("cc_d_out", [2048, 5504], BF16)
    Wq_b = nc.dram_tensor("Wq_b", [DM, DM], BF16)
    Wk_b = nc.dram_tensor("Wk_b", [DM, 256], BF16)
    Wv_b = nc.dram_tensor("Wv_b", [DM, 256], BF16)
    Wo_b = nc.dram_tensor("Wo_b", [DM, DM], BF16)

    env = Env()
    env.nc = nc
    env.ctx = ctx
    env.P = P = Plan()

    env.s_xres = alloc(env, env.ctx, "xres", [128, NCH, TPC], F32)
    env.s_xnT = alloc(env, env.ctx, "xnT", [128, NCH, TPC], BF16)
    env.ones_b = alloc(env, env.ctx, "ones_b", [128, 1], BF16)
    env.ones_f = alloc(env, env.ctx, "ones_f", [1, 128], F32)
    s_wn = alloc(env, env.ctx, "wn", [128, 16], F32)
    s_mask = alloc(env, env.ctx, "mask", [128, 144], F32)
    s_pmk = alloc(env, env.ctx, "pmk", [128, 256], BF16)
    s_pmv = alloc(env, env.ctx, "pmv", [16, DM], BF16)
    s_id = alloc(env, env.ctx, "id", [128, 128], BF16)
    s_bq = alloc(env, env.ctx, "bq", [128, 16], F32)
    s_bk = alloc(env, env.ctx, "bk", [128, 2], F32)
    s_bv = alloc(env, env.ctx, "bv", [128, 2], F32)
    s_bo = alloc(env, env.ctx, "bo", [128, 16], F32)
    s_lut = alloc(env, env.ctx, "lut", [128, 16], F32)
    s_off = alloc(env, env.ctx, "boffs", [1, 10], I32)

    xr = xT.rearrange("(c p) t -> p c t", p=128)
    P.emit("scalar", "dma_start", out=env.s_xres[:, :, :], in_=xr, inc=[("LD", 16)])
    P.emit("scalar", "dma_start", out=s_wn[:, :], in_=wnpp, inc=[("LD", 16)])
    P.emit("scalar", "dma_start", out=s_mask[:, :], in_=mask, inc=[("LD", 16)])
    for dst, srcap in ((s_bq, bqp), (s_bk, bkp), (s_bv, bvp), (s_bo, bop),
                       (s_lut, lut), (s_off, boff)):
        P.emit("scalar", "dma_start", out=dst[:, :], in_=srcap, inc=[("LD", 16)])
    P.emit("gpsimd", "dma_start", out=s_pmk[:, :], in_=pmk, inc=[("LC", 16)])
    P.emit("gpsimd", "dma_start", out=s_pmv[:, :], in_=pmv, inc=[("LC", 16)])
    P.emit("gpsimd", "dma_start", out=s_id[:, :], in_=iden, inc=[("LC", 16)])
    P.emit("vector", "memset", env.ones_b[:, :], 1.0, inc=[("VE", 1)])
    P.emit("vector", "memset", env.ones_f[:, :], 1.0, inc=[("VE", 1)])

    for dst, srcap in ((Wq_b, Wq), (Wk_b, Wk), (Wv_b, Wv), (Wo_b, Wo)):
        P.emit("gpsimd", "dma_start", out=dst.ap(), in_=srcap, inc=[("PC", 16)])
    pcdep = P.after("PC")

    feeder = GhostFeeder(env, g_my, u_my, d_my, cc_gu_in.ap(), cc_d_in.ap(), s_lut)

    def _gather_ghost():
        cw = P.val("CW")
        P.emit("gpsimd", "collective_compute", "AllGather", ALU.bypass,
               replica_groups=[list(range(8))],
               ins=[cc_gu_in.ap().opt()], outs=[cc_gu_out.ap().opt()],
               wait=(("CW", cw),), inc=[("CD", 1)])
        P.emit("gpsimd", "collective_compute", "AllGather", ALU.bypass,
               replica_groups=[list(range(8))],
               ins=[cc_d_in.ap().opt()], outs=[cc_d_out.ap().opt()],
               inc=[("CD", 1)])

    emit_block0(env, wqkv, wout, wg, wu, wd, s_wn, s_mask, s_pmk, s_pmv, s_id,
                feeder=feeder, post_attn=_gather_ghost)
    assert feeder.done
    ccdep = P.after("CD")

    with ExitStack() as late:
        s_xnB = alloc(env, late, "xnTB", [128, NCH, 640], BF16)

        with ExitStack() as sub:
            emit_ln_setup(env, sub)
            emit_ln(env, "l1", env.s_xres, env.s_xnT, TPC, dep=P.after("VE"))

        P.emit("scalar", "dma_start",
               out=cc_in.rearrange("(c p) t -> p c t", p=128),
               in_=env.s_xnT[:, :, :], wait=P.after("VE"), inc=[("CC", 16)])
        P.emit("gpsimd", "collective_compute", "AllGather", ALU.bypass,
               replica_groups=[[0, 1, 2, 3], [4, 5, 6, 7]],
               ins=[cc_in.ap().opt()], outs=[cc_out.ap().opt()],
               wait=P.after("CC"), inc=[("CC", 1)])
        ccv = P.val("CC")

        def _bgather(eng, sems):
            for j in range(5):
                rv_row = eng.value_load(s_off[0:1, 2 * j:2 * j + 1])
                rv_col = eng.value_load(s_off[0:1, 2 * j + 1:2 * j + 2])
                src_ap = cc_out[bass.ds(rv_row, DM), bass.ds(rv_col, 128)]
                eng.dma_start(
                    out=s_xnB[:, :, 128 * j:128 * (j + 1)],
                    in_=src_ap.rearrange("(c p) t -> p c t", p=128)
                ).then_inc(sems["LC"], 16)

        P.emit("gpsimd", _bgather,
               wait=(("CC", ccv),) + P.after("LD", "PE"), inc=[("LC", 80)])

        emit_block1(env, Wq_b.ap(), Wk_b.ap(), Wv_b.ap(), Wo_b.ap(), None, None,
                    None, s_bq, s_bk, s_bv, s_bo, s_lut, s_id, s_xnB,
                    P.after("LC"),
                    ghost=("cc", cc_gu_out.ap(), cc_d_out.ap(), ccdep),
                    wload=("scalar", "LD", pcdep))

    orr = oT.rearrange("(c p) t -> p c t", p=128)
    P.emit("sync", "dma_start", out=orr, in_=env.s_xres[:, :, :],
           wait=P.after("VE", "AC", "PE"), inc=[("ST", 16)])
    P.emit("sync", "nop", wait=P.after("ST"))
    return env



def _build_mask0():
    m = np.full((128, 144), NEG, np.float32)
    for q in range(128):
        sseg, j = q // 32, q % 32
        for c in range(128):
            if c // 32 == sseg and (c % 32) <= j - 16:
                m[q, c] = 0.0
        for p in range(16):
            if p <= j:
                m[q, 128 + p] = 0.0
    return m


def _pp(v, n):
    return np.ascontiguousarray(np.asarray(v, np.float32).reshape(n, 128).T)


def kernel(**inp):
    x = np.asarray(inp["L0_attn_seq"], np.float32)
    pm = np.asarray(inp["L0_attn_pm"], np.float32)
    pmkT = np.ascontiguousarray(
        pm[0].reshape(16, 2, 16, 64).transpose(1, 3, 0, 2).reshape(128, 256))
    pmvT = np.ascontiguousarray(pm[1].transpose(1, 0, 2).reshape(16, 2048))
    wn_pp = _pp(inp["L0_attn_w_norm"], 16)
    mask0 = _build_mask0()
    ident = np.eye(128, dtype=np.float32)
    lut_t = np.ascontiguousarray(
        np.tile(np.asarray(inp["L1_ffn_lut"], np.float32)[None, :], (128, 1)))
    gidx_np = np.asarray(inp["L1_ffn_gate_idx"], np.int32)
    uidx_np = np.asarray(inp["L1_ffn_up_idx"], np.int32)
    didx_np = np.asarray(inp["L1_ffn_down_idx"], np.int32)

    with ExitStack() as ctx:
        nc = bass.Bass("TRN2", target_bir_lowering=False)
        plan = plan_fused(nc, ctx)
        playback(nc, ctx, plan.P)
        in_maps = []
        for core in range(NCORE):
            b, m = divmod(core, 4)
            offs = []
            for j in range(5):
                t = 512 * m + 128 * j
                offs += [2048 * (t // 1024), t % 1024]
            in_maps.append({
                "xT": np.ascontiguousarray(x[b, 1024 * m:1024 * (m + 1), :].T),
                "w_qkv": inp["L0_attn_w_qkv"],
                "w_out": inp["L0_attn_w_out"],
                "w_norm_pp": wn_pp,
                "pmkT": pmkT,
                "pmvT": pmvT,
                "w_gate": inp["L0_ffn_w_gate"],
                "w_up": inp["L0_ffn_w_up"],
                "w_down": inp["L0_ffn_w_down"],
                "mask0": mask0,
                "ident": ident,
                "Wq": inp["L1_attn_Wq"],
                "Wk": inp["L1_attn_Wk"],
                "Wv": inp["L1_attn_Wv"],
                "Wo": inp["L1_attn_Wo"],
                "bq_pp": _pp(inp["L1_attn_bq"], 16),
                "bk_pp": _pp(inp["L1_attn_bk"], 2),
                "bv_pp": _pp(inp["L1_attn_bv"], 2),
                "bo_pp": _pp(inp["L1_attn_bo"], 16),
                "lut_t": lut_t,
                "g_my": np.ascontiguousarray(
                    gidx_np[128 * GB_BASE[core]:128 * (GB_BASE[core] + 6), :]),
                "u_my": np.ascontiguousarray(
                    uidx_np[128 * GB_BASE[core]:128 * (GB_BASE[core] + 6), :]),
                "d_my": np.ascontiguousarray(didx_np[256 * core:256 * (core + 1), :]),
                "boff": np.asarray([offs], np.int32),
            })
        res = run_bass_kernel_spmd(nc, in_maps, core_ids=list(range(NCORE)))

    outf = [np.concatenate([res.results[4 * b + m]["outT"] for m in range(4)], axis=1)
            for b in range(2)]
    out = np.stack([outf[0].T, outf[1].T], axis=0)
    return np.ascontiguousarray(out).astype(np.float32)
